# version 29
# baseline (speedup 1.0000x reference)
import contextlib
import numpy as np
import concourse.bass as bass
import concourse.mybir as mybir
from concourse.bass_utils import run_bass_kernel_spmd

F32 = mybir.dt.float32
BF16 = mybir.dt.bfloat16
AF = mybir.ActivationFunctionType
ALU = mybir.AluOpType
AX = mybir.AxisListType.X
AXALL = mybir.AxisListType.XYZWC
MAGIC = 1.5 * 2.0 ** 23

ENG = ("sync", "tensor", "vector", "scalar", "gpsimd")

N_CORES = 8
SEQ_PER_CORE = 2
S = 2048
D = 1024
H = 16
DH = 64
FF = 4096
T_TILE = 128
N_TILES = (SEQ_PER_CORE * S) // T_TILE
TILES_PER_SEQ = S // T_TILE
KD = D // 128
KF = FF // 128
GROUP = 512
N_GROUPS = (SEQ_PER_CORE * S) // GROUP
INV_D = 1.0 / D
EPS = 1e-5


class Buf:
    __slots__ = ("w", "r")

    def __init__(self, w=None):
        self.w = list(w) if w else []
        self.r = []


class Sched:

    def __init__(self):
        self.ops = {e: [] for e in ENG}
        self.counts = {}
        self.last_wait = {e: {} for e in ENG}

    def op(self, eng, emit, reads=(), writes=(), inc=1, sem=None):
        key = sem if sem is not None else eng
        deps = []
        for b in reads:
            deps.extend(b.w)
        for b in writes:
            deps.extend(b.w)
            deps.extend(b.r)
        waits = {}
        lw = self.last_wait[eng]
        for (k2, v) in deps:
            if v > lw.get(k2, 0) and v > waits.get(k2, 0):
                waits[k2] = v
        for k2, v in waits.items():
            lw[k2] = v
        self.counts[key] = self.counts.get(key, 0) + inc
        tok = (key, self.counts[key])
        self.ops[eng].append((waits, emit, inc, key))
        for b in reads:
            b.r.append(tok)
        for b in writes:
            b.w = [tok]
            b.r = []
        return tok

    def snapshot(self):
        return [(k, v) for k, v in self.counts.items() if v > 0 and k in ENG]


def build_program():
    nc = bass.Bass("TRN2", target_bir_lowering=False)

    NT = N_TILES * T_TILE
    x_d = nc.dram_tensor("x", [NT, D], F32, kind="ExternalInput").ap()
    wq_d = nc.dram_tensor("wq", [D, D], F32, kind="ExternalInput").ap()
    wk_d = nc.dram_tensor("wk", [D, D], F32, kind="ExternalInput").ap()
    wv_d = nc.dram_tensor("wv", [D, D], F32, kind="ExternalInput").ap()
    wo_d = nc.dram_tensor("wo", [D, D], F32, kind="ExternalInput").ap()
    proj_d = nc.dram_tensor("proj", [DH, DH], F32, kind="ExternalInput").ap()
    w1_d = nc.dram_tensor("w1", [D, FF], F32, kind="ExternalInput").ap()
    b1_d = nc.dram_tensor("b1", [FF], F32, kind="ExternalInput").ap()
    w2_d = nc.dram_tensor("w2", [FF, D], F32, kind="ExternalInput").ap()
    b2_d = nc.dram_tensor("b2", [D], F32, kind="ExternalInput").ap()
    g1_d = nc.dram_tensor("g1", [D], F32, kind="ExternalInput").ap()
    be1_d = nc.dram_tensor("beta1", [D], F32, kind="ExternalInput").ap()
    g2_d = nc.dram_tensor("g2", [D], F32, kind="ExternalInput").ap()
    be2_d = nc.dram_tensor("beta2", [D], F32, kind="ExternalInput").ap()
    win_d = nc.dram_tensor("W_in", [D, FF], F32, kind="ExternalInput").ap()
    bin_d = nc.dram_tensor("b_in", [FF], F32, kind="ExternalInput").ap()
    wout_d = nc.dram_tensor("W_out", [FF, D], F32, kind="ExternalInput").ap()
    bout_d = nc.dram_tensor("b_out", [D], F32, kind="ExternalInput").ap()
    mask_d = nc.dram_tensor("mask", [128, 128], F32, kind="ExternalInput").ap()
    ident_d = nc.dram_tensor("ident", [128, 128], F32, kind="ExternalInput").ap()
    out_d = nc.dram_tensor("out", [NT, D], F32, kind="ExternalOutput").ap()

    s_r = nc.dram_tensor("s_r", [N_TILES, T_TILE, D], F32).ap()
    s_h2t_hi = nc.dram_tensor("s_h2t_hi", [N_GROUPS, 128, KD, GROUP], BF16).ap()
    s_h2t_lo = nc.dram_tensor("s_h2t_lo", [N_GROUPS, 128, KD, GROUP], BF16).ap()
    s_hidt_hi = nc.dram_tensor("s_hidt_hi", [KF, 128, NT], BF16).ap()
    s_hidt_lo = nc.dram_tensor("s_hidt_lo", [KF, 128, NT], BF16).ap()
    s_x1 = nc.dram_tensor("s_x1", [N_TILES, T_TILE, D], F32).ap()
    s_x1t = nc.dram_tensor("s_x1t", [N_GROUPS, 128, KD, GROUP], F32).ap()
    s_h2at = nc.dram_tensor("s_h2at", [KF, 128, NT], F32).ap()
    s_wint = nc.dram_tensor("s_wint", [KD, 128, FF], BF16).ap()
    cc1_in = nc.dram_tensor("cc1_in", [1, 16], F32).ap()
    cc1_out = nc.dram_tensor("cc1_out", [1, 16], F32).ap()
    cc2_in = nc.dram_tensor("cc2_in", [1, 16], F32).ap()
    cc2_out = nc.dram_tensor("cc2_out", [1, 16], F32).ap()

    st = contextlib.ExitStack()
    with st:
        def sbt(name, shape, dt=F32):
            return st.enter_context(nc.sbuf_tensor(name, shape, dt))

        AR_ELEMS = 17920
        warena = sbt("warena", [128, 32768])
        aarena = sbt("aarena", [128, AR_ELEMS])
        ident = sbt("ident_s", [128, 128])
        mask = sbt("mask_s", [128, 128])
        b1f = sbt("b1f", [128, KF])
        binf = sbt("binf", [128, KF])
        iscWi = sbt("iscWi", [128, KF])
        cmbE = sbt("cmbE", [128, KF])
        iscWo = sbt("iscWo", [128, KF])
        cmbF = sbt("cmbF", [128, KF])
        kvb = sbt("kvb", [128, H // 2, DH + 1])
        vsall = sbt("vsall", [128, H, DH + 1])
        amax1 = sbt("amax1", [128, 1])
        amax2 = sbt("amax2", [128, 1])
        ccrow = sbt("ccrow", [1, 16])
        sc1 = sbt("sc1", [128, 1])
        isc1 = sbt("isc1", [128, 1])
        sc2 = sbt("sc2", [128, 1])
        isc2 = sbt("isc2", [128, 1])
        gscal = sbt("gscal", [128, 1])
        stats = sbt("stats", [128, 32])
        wamax = sbt("wamax", [128, 1])
        wsc = sbt("wsc", [128, 1])
        wisc = sbt("wisc", [128, 1])
        wamax2 = sbt("wamax2", [128, 1])
        wsc2 = sbt("wsc2", [128, 1])
        wisc2 = sbt("wisc2", [128, 1])

        psumt = [st.enter_context(nc.psum_tensor(f"pb{i}", [128, 1024], F32))
                 for i in range(4)]
        psum = [psumt[i // 2].ap()[:, (i % 2) * 512:(i % 2 + 1) * 512]
                for i in range(8)]
        psum_pair = [psumt[i].ap().rearrange("p (b n) -> p b n", b=2)
                     for i in range(4)]

        sch = Sched()
        O = sch.op
        pb = [Buf() for _ in range(8)]

        wfp = warena.ap()
        wbf = warena.ap().bitcast(BF16)
        abf = aarena.ap().bitcast(BF16)

        b_ident = Buf(); b_mask = Buf(); b_b1f = Buf(); b_binf = Buf()
        b_iscWi = Buf(); b_cmbE = Buf(); b_iscWo = Buf(); b_cmbF = Buf()
        b_kvh = [Buf() for _ in range(H)]
        b_vsall = Buf()
        b_amax1 = Buf(); b_amax2 = Buf(); b_ccrow = Buf()
        b_sc1 = Buf(); b_isc1 = Buf(); b_sc2 = Buf(); b_isc2 = Buf(); b_gscal = Buf()
        b_stats = Buf(); b_stats2 = Buf()
        b_wamax = Buf(); b_wsc = Buf(); b_wisc = Buf()
        b_x_d = Buf(); b_out_d = Buf()
        b_sr = Buf(); b_sh2t = Buf(); b_shidt = Buf(); b_sx1 = Buf()
        b_sh2at = Buf(); b_swint = [Buf(), Buf(), Buf()]
        b_cc1i = Buf(); b_cc1o = Buf(); b_cc2i = Buf(); b_cc2o = Buf()

        def barrier():
            return sch.snapshot()

        class AV:
            def __init__(self):
                self.off = 0

            def take(self, name, n_elems):
                lo = self.off
                self.off += n_elems
                assert self.off <= AR_ELEMS, f"aarena overflow at {name}: {self.off}"
                return aarena.ap()[:, lo:lo + n_elems], Buf(barrier())

            def take_pair(self, name, n_elems, kc, npc):
                lo_off = self.off
                self.off += n_elems
                assert self.off <= AR_ELEMS, f"aarena overflow at {name}: {self.off}"
                hi = abf[:, 2 * lo_off: 2 * lo_off + n_elems]
                lo = abf[:, 2 * lo_off + n_elems: 2 * lo_off + 2 * n_elems]
                if kc is not None:
                    hi = hi.rearrange("p (k n) -> p k n", k=kc)
                    lo = lo.rearrange("p (k n) -> p k n", k=kc)
                return hi, lo, Buf(barrier())

        def wpair(base, size, kc):
            hi = wbf[:, 2 * base: 2 * base + size].rearrange("p (k n) -> p k n", k=kc)
            lo = wbf[:, 2 * base + size: 2 * base + 2 * size].rearrange(
                "p (k n) -> p k n", k=kc)
            return hi, lo

        O("sync", lambda e: e.dma_start(out=ident[:, :], in_=ident_d[:, :]),
          writes=[b_ident], inc=16, sem="const")
        O("sync", lambda e: e.dma_start(out=mask[:, :], in_=mask_d[:, :]),
          writes=[b_mask], inc=16, sem="const")

        avP = AV()
        g1b, b_g1b = avP.take("g1b", D)
        be1b, b_be1b = avP.take("be1b", D)
        g2b, b_g2b = avP.take("g2b", D)
        be2b, b_be2b = avP.take("be2b", D)
        for (dst, bf, src) in ((g1b, b_g1b, g1_d), (be1b, b_be1b, be1_d),
                               (g2b, b_g2b, g2_d), (be2b, b_be2b, be2_d)):
            O("sync", lambda e, d=dst, s=src: e.dma_start(
                out=d, in_=s[None, :].broadcast_to([128, D])),
              writes=[bf], inc=16, sem="const")

        xb_ap = [None, None]; xb_b = [None, None]
        xb_ap[0], xb_b[0] = avP.take("xb0", D)
        xb_ap[1], xb_b[1] = avP.take("xb1", D)
        scrA, b_scrA = avP.take("scrA", D)
        xn, b_xn = avP.take("xn", D)
        hT_hi, hT_lo, b_hT = avP.take_pair("hT", D, KD, 128)
        qphi, b_qphi = avP.take("qphi", D)
        kphi, b_kphi = avP.take("kphi", D)
        scrE, b_scrE0 = avP.take("scrE", D)
        b_scrE = [b_scrE0, Buf(barrier())]
        qT, b_qT = avP.take("qT", D)
        kT, b_kT = avP.take("kT", D)
        attn, b_attn = avP.take("attn", D)
        aT_hi, aT_lo, b_aT = avP.take_pair("attnT", D, KD, 128)
        _h2T_off = avP.off
        h2T_hi, h2T_lo, b_h2T = avP.take_pair("h2T", D, KD, 128)
        den, b_den = avP.take("den", 8)
        smb, b_smb = avP.take("smb", 256)
        qTv = qT.rearrange("p (k n) -> p k n", k=KD)
        kTv = kT.rearrange("p (k n) -> p k n", k=KD)


        O("vector", lambda e: e.memset(vsall[:, :, DH:DH + 1], 1.0),
          writes=[b_vsall])
        O("vector", lambda e: e.memset(amax1[:, :], 0.0), writes=[b_amax1])
        O("vector", lambda e: e.memset(amax2[:, :], 0.0), writes=[b_amax2])
        O("vector", lambda e: e.memset(ccrow[:, :], 0.0), writes=[b_ccrow])

        proj2 = xb_ap[0][:, 128:192]
        b_proj2 = Buf()
        O("sync", lambda e: e.dma_start(out=proj2[0:DH, :], in_=proj_d[:, :]),
          writes=[b_proj2], inc=16, sem="const")
        O("sync", lambda e: e.dma_start(out=proj2[DH:128, :], in_=proj_d[:, :]),
          writes=[b_proj2], inc=16, sem="const")
        _ctot = [("const", sch.counts["const"])]
        for _b in (b_ident, b_mask, b_g1b, b_be1b, b_g2b, b_be2b, b_proj2):
            _b.w = list(_ctot)

        btmp = xb_ap[0][:, 0:128]
        b_btmp = Buf()
        for (srcd, dst_t, dstb) in ((b1_d, b1f, b_b1f), (bin_d, binf, b_binf)):
            srcv = srcd.rearrange("(m p) -> m p", p=128)
            O("sync", lambda e, s=srcv, d=btmp: e.dma_start(out=d[0:KF, :], in_=s),
              writes=[b_btmp], inc=16, sem="stg0")
            O("tensor", lambda e, d=btmp: e.transpose(
                psum[0][:, 0:KF], d[0:KF, :], ident[0:KF, 0:KF]),
              reads=[b_btmp, b_ident], writes=[pb[0]])
            O("vector", lambda e, d=dst_t: e.tensor_copy(d[:, :], psum[0][:, 0:KF]),
              reads=[pb[0]], writes=[dstb])

        fst_ap = [xb_ap[0][:, 192:320], xb_ap[0][:, 320:448]]
        fst_b = [Buf(), Buf()]
        stgw_ap = [scrA, xn]
        stgw_b = [Buf(), Buf()]

        wq_hi, wq_lo = wpair(0, 8192, KD)
        wk_hi, wk_lo = wpair(8192, 8192, KD)
        wv_hi, wv_lo = wpair(16384, 8192, KD)
        wo_hi, wo_lo = wpair(24576, 8192, KD)
        b_wq = Buf(); b_wk = Buf(); b_wv = Buf(); b_wo = Buf()

        wtmp_s = wfp[:, 16384:24576].rearrange("p (k n) -> p k n", k=KD)
        b_wtmp = Buf()

        def fold_split(src_d, hi_flat, lo_flat, b_w):
            O("sync", lambda e, s=src_d: e.dma_start(
                out=wtmp_s, in_=s.rearrange("(k p) n -> p k n", p=128)),
              writes=[b_wtmp], inc=16, sem="warena")
            for k in range(KD):
                for mp in range(KD):
                    i = k * KD + mp
                    bank = 6 + (i % 2)
                    fs, b_fs = fst_ap[i % 2], fst_b[i % 2]
                    ob = (2 * (i % 2), 2 * (i % 2) + 1)
                    O("tensor", lambda e, k_=k, mp_=mp, t=psum[bank]:
                      e.transpose(t[:, 0:128],
                                  wtmp_s[:, k_, mp_ * 128:(mp_ + 1) * 128],
                                  ident[:, :]),
                      reads=[b_wtmp, b_ident], writes=[pb[bank]])
                    O("scalar", lambda e, t=psum[bank], d=fs: e.copy(
                        d, t[:, 0:128]), reads=[pb[bank]], writes=[b_fs])
                    O("tensor", lambda e, d=fs, p=proj2, o=ob[0]: e.matmul(
                        psum[o][:, 0:DH], d[0:DH, :], p[0:DH, :],
                        start=True, stop=True),
                      reads=[b_fs, b_proj2], writes=[pb[ob[0]]])
                    O("tensor", lambda e, d=fs, p=proj2, o=ob[1]: e.matmul(
                        psum[o][:, 0:DH], d[DH:128, :], p[DH:128, :],
                        start=True, stop=True),
                      reads=[b_fs, b_proj2], writes=[pb[ob[1]]])
                    O("vector", lambda e, k_=k, mp_=mp, o=ob[0]: e.tensor_copy(
                        wtmp_s[:, k_, mp_ * 128: mp_ * 128 + DH],
                        psum[o][:, 0:DH]),
                      reads=[pb[ob[0]]], writes=[b_wtmp])
                    O("vector", lambda e, k_=k, mp_=mp, o=ob[1]: e.tensor_copy(
                        wtmp_s[:, k_, mp_ * 128 + DH:(mp_ + 1) * 128],
                        psum[o][:, 0:DH]),
                      reads=[pb[ob[1]]], writes=[b_wtmp])
            src_flat = wfp[:, 16384:24576]
            O("vector", lambda e, h=hi_flat: e.tensor_copy(h, src_flat),
              reads=[b_wtmp], writes=[b_w])
            O("vector", lambda e, h=hi_flat, l=lo_flat: e.tensor_tensor(
                l, src_flat, h, ALU.subtract),
              reads=[b_wtmp, b_w], writes=[b_w])

        def stream_split(src_d, rows, cols, base, b_w, nbands, stg_bufs,
                         extra_writes=()):
            size = rows * cols // 128
            hi_flat = wbf[:, 2 * base: 2 * base + size]
            lo_flat = wbf[:, 2 * base + size: 2 * base + 2 * size]
            per_row = cols
            bands_per_k = per_row // 1024
            ew = list(extra_writes)
            band_bufs = [Buf() for _ in range(nbands)]
            for i in range(nbands):
                k, w = i // bands_per_k, i % bands_per_k
                j = i % 2
                stg, b_stg = stgw_ap[j], stg_bufs[j]
                O("sync", lambda e, k_=k, w_=w, d=stg, s=src_d: e.dma_start(
                    out=d, in_=s[k_ * 128:(k_ + 1) * 128,
                                 w_ * 1024:(w_ + 1) * 1024]),
                  writes=[b_stg], inc=16, sem=f"stg{j}")
                off = k * per_row + w * 1024
                bb = band_bufs[i]
                O("vector", lambda e, d=stg, o=off: e.tensor_copy(
                    hi_flat[:, o:o + 1024], d),
                  reads=[b_stg], writes=[bb] + ew)
                O("vector", lambda e, d=stg, o=off: e.tensor_tensor(
                    lo_flat[:, o:o + 1024], d, hi_flat[:, o:o + 1024],
                    ALU.subtract),
                  reads=[b_stg, bb], writes=[bb] + ew)
            b_w.w = [t for bb in band_bufs for t in bb.w]
            b_w.r = []

        fold_split(wq_d, wbf[:, 0:8192], wbf[:, 8192:16384], b_wq)
        fold_split(wk_d, wbf[:, 16384:24576], wbf[:, 24576:32768], b_wk)
        stream_split(wv_d, D, D, 16384, b_wv, 8, stgw_b, extra_writes=[b_wtmp])
        stream_split(wo_d, D, D, 24576, b_wo, 8, stgw_b)

        _bar = barrier()
        for _b in (xb_b[0], xb_b[1], b_scrA, b_xn):
            _b.w = list(_bar)
        xn2, b_xn2 = scrA, b_scrA

        def ln_stats(src_ap, src_b, so, sb):
            O("vector", lambda e, s=src_ap: e.bn_stats(
                stats[:, so:so + 6], s[:, 0:512]),
              reads=[src_b], writes=[sb])
            O("vector", lambda e, s=src_ap: e.bn_stats(
                stats[:, so + 6:so + 12], s[:, 512:1024]),
              reads=[src_b], writes=[sb])
            O("vector", lambda e: e.bn_aggr(
                stats[:, so + 12:so + 14], stats[:, so:so + 12]),
              reads=[sb], writes=[sb])
            O("vector", lambda e: e.tensor_scalar(
                stats[:, so + 13:so + 14], stats[:, so + 13:so + 14],
                EPS, None, ALU.add),
              reads=[sb], writes=[sb])
            O("scalar", lambda e: e.activation(
                stats[:, so + 14:so + 15], stats[:, so + 13:so + 14], AF.Sqrt),
              reads=[sb], writes=[sb])
            O("vector", lambda e: e.reciprocal(
                stats[:, so + 15:so + 16], stats[:, so + 14:so + 15]),
              reads=[sb], writes=[sb])

        def ln_norm(src_ap, src_b, dst_ap, dst_b, so, sb,
                    gb=None, bb=None, gb_b=None, bb_b=None):
            O("vector", lambda e, s=src_ap, d=dst_ap: e.tensor_scalar(
                d, s, stats[:, so + 12:so + 13], stats[:, so + 15:so + 16],
                ALU.subtract, ALU.mult),
              reads=[src_b, sb], writes=[dst_b])
            if gb is not None:
                O("vector", lambda e, d=dst_ap, g=gb: e.tensor_tensor(
                    d, d, g, ALU.mult), reads=[dst_b, gb_b], writes=[dst_b])
                O("vector", lambda e, d=dst_ap, b=bb: e.tensor_tensor(
                    d, d, b, ALU.add), reads=[dst_b, bb_b], writes=[dst_b])

        def layernorm(src_ap, src_b, dst_ap, dst_b, scr_ap, scr_b,
                      gb=None, bb=None, gb_b=None, bb_b=None):
            ln_stats(src_ap, src_b, 0, b_stats)
            ln_norm(src_ap, src_b, dst_ap, dst_b, 0, b_stats,
                    gb, bb, gb_b, bb_b)

        def transpose8(src_ap, src_b, dstv, dst_b, evac_eng="scalar",
                       pairs=(3, 0)):
            for p in range(4):
                pp = pairs[p % len(pairs)]
                for h in range(2):
                    k = 2 * p + h
                    bank = 2 * pp + h
                    O("tensor", lambda e, k_=k, s=src_ap, t=psum[bank]:
                      e.transpose(t[:, 0:128],
                                  s[:, k_ * 128:(k_ + 1) * 128], ident[:, :]),
                      reads=[src_b, b_ident], writes=[pb[bank]])
                pv = psum_pair[pp][:, :, 0:128]
                if evac_eng == "scalar":
                    O("scalar", lambda e, p_=p, pv=pv, d=dstv: e.copy(
                        d[:, 2 * p_:2 * p_ + 2, :], pv),
                      reads=[pb[2 * pp], pb[2 * pp + 1]], writes=[dst_b])
                else:
                    O("vector", lambda e, p_=p, pv=pv, d=dstv: e.tensor_copy(
                        d[:, 2 * p_:2 * p_ + 2, :], pv),
                      reads=[pb[2 * pp], pb[2 * pp + 1]], writes=[dst_b])

        def transpose8_pair(src_ap, src_b, hiv, lov, dst_b, pairs=(3, 0)):
            for p in range(4):
                pp = pairs[p % len(pairs)]
                for h in range(2):
                    k = 2 * p + h
                    bank = 2 * pp + h
                    O("tensor", lambda e, k_=k, s=src_ap, t=psum[bank]:
                      e.transpose(t[:, 0:128],
                                  s[:, k_ * 128:(k_ + 1) * 128], ident[:, :]),
                      reads=[src_b, b_ident], writes=[pb[bank]])
                pv = psum_pair[pp][:, :, 0:128]
                O("scalar", lambda e, p_=p, pv=pv, d=hiv: e.copy(
                    d[:, 2 * p_:2 * p_ + 2, :], pv),
                  reads=[pb[2 * pp], pb[2 * pp + 1]], writes=[dst_b])
                O("vector", lambda e, p_=p, pv=pv, h=hiv, l=lov:
                  e.tensor_tensor(l[:, 2 * p_:2 * p_ + 2, :], pv,
                                  h[:, 2 * p_:2 * p_ + 2, :], ALU.subtract),
                  reads=[pb[2 * pp], pb[2 * pp + 1], dst_b], writes=[dst_b])

        class GroupEmitter:

            def __init__(self, bank_i, out_view, total):
                self.bank = bank_i
                self.ov = out_view
                self.total = total
                self.i = 0

            def emit(self, mms):
                for (lt, rt, rbufs) in mms:
                    first = self.i == 0
                    last = self.i == self.total - 1
                    if first:
                        sch.op("tensor",
                               lambda e, lt=lt, rt=rt, ov=self.ov, l=last:
                               e.matmul(ov, lt, rt, start=True, stop=l),
                               reads=list(rbufs), writes=[pb[self.bank]])
                    else:
                        sch.op("tensor",
                               lambda e, lt=lt, rt=rt, ov=self.ov, l=last:
                               e.matmul(ov, lt, rt, start=False, stop=l),
                               reads=list(rbufs))
                    self.i += 1
                if self.i == self.total:
                    pb[self.bank].w = [("tensor", sch.counts["tensor"])]
                    pb[self.bank].r = []

        def mm_group(bank_i, out_view, mms):
            n = len(mms)
            for i, (lt, rt, rbufs) in enumerate(mms):
                if i == 0:
                    sch.op("tensor",
                           lambda e, lt=lt, rt=rt, ov=out_view, last=(n == 1):
                           e.matmul(ov, lt, rt, start=True, stop=last),
                           reads=list(rbufs), writes=[pb[bank_i]])
                else:
                    sch.op("tensor",
                           lambda e, lt=lt, rt=rt, ov=out_view, last=(i == n - 1):
                           e.matmul(ov, lt, rt, start=False, stop=last),
                           reads=list(rbufs))
            pb[bank_i].w = [("tensor", sch.counts["tensor"])]
            pb[bank_i].r = []

        def dma_fill(eng, key, buf, emits, extra_reads=()):
            for i, em in enumerate(emits):
                if i == 0:
                    sch.op(eng, em, reads=list(extra_reads), writes=[buf],
                           inc=16, sem=key)
                else:
                    sch.op(eng, em, reads=list(extra_reads), inc=16, sem=key)
            buf.w = [(key, sch.counts[key])]
            buf.r = []

        def mm3(xhi, xlo, b_x, whi, wlo, b_w, nwin):
            mms = []
            for k in range(KD):
                mms.append((xhi[:, k, :], whi[:, k, nwin], [b_x, b_w]))
                mms.append((xhi[:, k, :], wlo[:, k, nwin], [b_x, b_w]))
            for k in range(KD):
                mms.append((xlo[:, k, :], whi[:, k, nwin], [b_x, b_w]))
            return mms

        def elu_pair(b0, b1, dst, dstb):
            halves = [Buf(dstb.w + dstb.r), Buf(dstb.w + dstb.r)]
            for i, bk in ((0, b0), (1, b1)):
                O("vector", lambda e, i_=i, bk_=bk: e.tensor_scalar(
                    scrE[:, i_ * 512:(i_ + 1) * 512],
                    psum[bk_][:, 0:512], 0.0, None, ALU.min),
                  reads=[pb[bk]], writes=[b_scrE[i]])
            toks = []
            for i, bk in ((0, b0), (1, b1)):
                d = dst[:, i * 512:(i + 1) * 512]
                O("scalar", lambda e, i_=i, d=d: e.activation(
                    d, scrE[:, i_ * 512:(i_ + 1) * 512], AF.Exp),
                  reads=[b_scrE[i]], writes=[halves[i]])
                tk = O("vector", lambda e, d=d, bk_=bk: e.scalar_tensor_tensor(
                    d, psum[bk_][:, 0:512], 0.0, d, ALU.max, ALU.add),
                  reads=[pb[bk], halves[i]], writes=[halves[i]])
                toks.append(tk)
            dstb.w = list(toks)
            dstb.r = []

        def k_stage(t):
            for n in range(2):
                bank = 2 + n
                nwin = slice(n * 512, (n + 1) * 512)
                mm_group(bank, psum[bank][:, :],
                         mm3(hT_hi, hT_lo, b_hT, wk_hi, wk_lo, b_wk, nwin))

        def v_stage(t):
            for n in range(2):
                bank = 4 + n
                nwin = slice(n * 512, (n + 1) * 512)
                mm_group(bank, psum[bank][:, :],
                         mm3(hT_hi, hT_lo, b_hT, wv_hi, wv_lo, b_wv, nwin))

        def v_evac(t):
            for n in range(2):
                O("scalar", lambda e, n_=n: e.copy(
                    vsall[:, n_ * 8:(n_ + 1) * 8, 0:DH],
                    psum[4 + n_][:, :].rearrange("p (h n) -> p h n", h=8)),
                  reads=[pb[4 + n]], writes=[b_vsall])

        def h2t_stage(t):
            transpose8_pair(xn2, b_xn2, h2T_hi, h2T_lo, b_h2T,
                            pairs=(3, 0, 1))
            g, jj = t // 4, t % 4
            dma_fill("gpsimd", "sh2t", b_sh2t, [
                (lambda e, g_=g, j_=jj, d=h2T_hi: e.dma_start(
                    out=s_h2t_hi[g_, :, :, j_ * 128:(j_ + 1) * 128],
                    in_=d[:, :, :])),
                (lambda e, g_=g, j_=jj, d=h2T_lo: e.dma_start(
                    out=s_h2t_lo[g_, :, :, j_ * 128:(j_ + 1) * 128],
                    in_=d[:, :, :]))],
                extra_reads=[b_h2T])

        def q_mms(t):
            out = []
            for n in range(2):
                nwin = slice(n * 512, (n + 1) * 512)
                out.append((n, mm3(hT_hi, hT_lo, b_hT, wq_hi, wq_lo, b_wq, nwin)))
            return out

        def attn_stage(t, interleave_q):
            c = t % TILES_PER_SEQ
            if c == 0:
                O("vector", lambda e: e.memset(kvb[:, :, :], 0.0), writes=b_kvh)
            mask3 = mask.ap().rearrange("p (o n) -> p o n", o=1).broadcast_to(
                [128, 2, 128])
            smbv = smb.rearrange("p (b n) -> p b n", b=2)
            den01 = den[:, 0:2].rearrange("p (b o) -> p b o", b=2)
            qgroups = [GroupEmitter(n, psum[n][:, :], 24) for n in range(2)]
            qchunks = None
            if interleave_q:
                qm = q_mms(t + 1)
                flat = [(n, mm) for n, mms in qm for mm in mms]
                qchunks = [flat[i * 6:(i + 1) * 6] for i in range(8)]
            for hpair in range(H // 2):
                h0 = 2 * hpair
                h1 = h0 + 1
                blkh = hpair
                for hp, h in ((0, h0), (1, h1)):
                    p0 = hp * DH
                    O("tensor", lambda e, p0_=p0, blk=blkh, sb_=2 + hp: e.matmul(
                        psum[sb_][:, 0:128],
                        kTv[p0_:p0_ + DH, blk, :], qTv[p0_:p0_ + DH, blk, :],
                        start=True, stop=True),
                      reads=[b_kT, b_qT], writes=[pb[2 + hp]])
                O("vector", lambda e, sm=smbv: e.tensor_tensor(
                    sm, psum_pair[1][:, :, 0:128], mask3, ALU.mult),
                  reads=[pb[2], pb[3], b_mask], writes=[b_smb])
                if qchunks is not None:
                    for n, mm in qchunks[hpair][:3]:
                        qgroups[n].emit([mm])
                for hp, h in ((0, h0), (1, h1)):
                    p0 = hp * DH
                    nbank = 4 + hp
                    mms = []
                    if c > 0:
                        mms.append((qTv[p0:p0 + DH, blkh, :],
                                    kvb[p0:p0 + DH, blkh, :], [b_qT, b_kvh[h]]))
                    mms.append((smb[:, hp * 128:(hp + 1) * 128],
                                vsall[:, h, :], [b_smb, b_vsall]))
                    mm_group(nbank, psum[nbank][:, 0:DH + 1], mms)
                O("vector", lambda e, d=den01: e.tensor_scalar(
                    d, psum_pair[2][:, :, DH:DH + 1], 1e-6, None, ALU.add),
                  reads=[pb[4], pb[5]], writes=[b_den])
                O("vector", lambda e: e.reciprocal(den[:, 2:4], den[:, 0:2]),
                  reads=[b_den], writes=[b_den])
                O("vector", lambda e, h0_=h0: e.tensor_tensor(
                    attn[:, h0_ * DH:(h0_ + 2) * DH].rearrange(
                        "p (b n) -> p b n", b=2),
                    psum_pair[2][:, :, 0:DH],
                    den[:, 2:4].rearrange("p (b o) -> p b o", b=2)
                    .broadcast_to([128, 2, DH]), ALU.mult),
                  reads=[pb[4], pb[5], b_den], writes=[b_attn])
                for hp, h in ((0, h0), (1, h1)):
                    p0 = hp * DH
                    kbank = 2 + hp
                    O("tensor", lambda e, h_=h, kb=kbank: e.matmul(
                        psum[kb][0:DH, 0:DH + 1], kphi[:, h_ * DH:(h_ + 1) * DH],
                        vsall[:, h_, :], start=True, stop=True),
                      reads=[b_kphi, b_vsall], writes=[pb[kbank]])
                    O("vector", lambda e, p0_=p0, blk=blkh, kb=kbank:
                      e.tensor_tensor(
                        kvb[p0_:p0_ + DH, blk, :], kvb[p0_:p0_ + DH, blk, :],
                        psum[kb][0:DH, 0:DH + 1], ALU.add),
                      reads=[b_kvh[h], pb[kbank]], writes=[b_kvh[h]])
                if qchunks is not None:
                    for n, mm in qchunks[hpair][3:]:
                        qgroups[n].emit([mm])
            if interleave_q:
                assert qgroups[0].i == 24 and qgroups[1].i == 24

        def wo_stage(t, xb, b_xb):
            transpose8_pair(attn, b_attn, aT_hi, aT_lo, b_aT, pairs=(3, 2))
            for n in range(2):
                bank = 4 + n
                nwin = slice(n * 512, (n + 1) * 512)
                mm_group(bank, psum[bank][:, :],
                         mm3(aT_hi, aT_lo, b_aT, wo_hi, wo_lo, b_wo, nwin))
                O("vector", lambda e, b_=bank, n_=n, x_=xb: e.tensor_tensor(
                    attn[:, n_ * 512:(n_ + 1) * 512], psum[b_][:, :],
                    x_[:, n_ * 512:(n_ + 1) * 512], ALU.add),
                  reads=[pb[bank], b_xb], writes=[b_attn])
            O("gpsimd", lambda e, t_=t: e.dma_start(
                out=s_r[t_, :, :], in_=attn[:, :]),
              reads=[b_attn], writes=[b_sr], inc=16, sem="sr")

        def xload(tt):
            jn = tt % 2
            O("sync", lambda e, t_=tt, d=xb_ap[jn]: e.dma_start(
                out=d, in_=x_d[t_ * T_TILE:(t_ + 1) * T_TILE, :]),
              reads=[b_x_d], writes=[xb_b[jn]], inc=16, sem=f"xb{jn}")

        xload(0)
        xload(1)
        layernorm(xb_ap[0], xb_b[0], xn, b_xn, scrA, b_scrA,
                  g1b, be1b, b_g1b, b_be1b)
        transpose8_pair(xn, b_xn, hT_hi, hT_lo, b_hT)
        for n, mms in q_mms(-1):
            mm_group(n, psum[n][:, :], mms)
        k_stage(0)
        v_stage(0)
        for t in range(N_TILES):
            j = t % 2
            elu_pair(0, 1, qphi, b_qphi)
            elu_pair(2, 3, kphi, b_kphi)
            if t + 1 < N_TILES:
                ln_stats(xb_ap[(t + 1) % 2], xb_b[(t + 1) % 2], 0, b_stats)
            if t > 0:
                ln_stats(attn, b_attn, 16, b_stats2)
            if t + 1 < N_TILES:
                ln_norm(xb_ap[(t + 1) % 2], xb_b[(t + 1) % 2], xn, b_xn,
                        0, b_stats, g1b, be1b, b_g1b, b_be1b)
            if t > 0:
                ln_norm(attn, b_attn, xn2, b_xn2, 16, b_stats2,
                        g2b, be2b, b_g2b, b_be2b)
            transpose8(qphi, b_qphi, qTv, b_qT, pairs=(3, 0, 1))
            transpose8(kphi, b_kphi, kTv, b_kT, pairs=(3, 0, 1))
            if t + 1 < N_TILES:
                transpose8_pair(xn, b_xn, hT_hi, hT_lo, b_hT,
                                pairs=(3, 0, 1))
            if t > 0:
                h2t_stage(t - 1)
            v_evac(t)
            attn_stage(t, interleave_q=(t + 1 < N_TILES))
            if t + 1 < N_TILES:
                k_stage(t + 1)
            wo_stage(t, xb_ap[j], xb_b[j])
            if t + 1 < N_TILES:
                v_stage(t + 1)
            if t + 2 < N_TILES:
                xload(t + 2)
        layernorm(attn, b_attn, xn2, b_xn2, scrA, b_scrA,
                  g2b, be2b, b_g2b, b_be2b)
        h2t_stage(N_TILES - 1)


        w1_hi, w1_lo = wpair(0, 32768, KD)
        b_w1 = Buf(barrier())
        stgC_b = [Buf(barrier()), Buf(barrier())]
        stream_split(w1_d, D, FF, 0, b_w1, 32, stgC_b)

        avC = AV()
        h2g_hi = [None, None]; h2g_lo = [None, None]; h2g_b = [None, None]
        h2g_hi[0], h2g_lo[0], h2g_b[0] = avC.take_pair("h2g0", KD * GROUP, KD, GROUP)
        h2g_hi[1], h2g_lo[1], h2g_b[1] = avC.take_pair("h2g1", KD * GROUP, KD, GROUP)
        stage_ap = []; stage_b = []
        for i in range(4):
            a_, b_ = avC.take(f"stC{i}", 512)
            stage_ap.append(a_); stage_b.append(b_)
        stbf_hi = []; stbf_lo = []; stbf_b = []
        for i in range(4):
            h_, l_, b_ = avC.take_pair(f"stCb{i}", 512, None, None)
            stbf_hi.append(h_); stbf_lo.append(l_); stbf_b.append(b_)

        for g in range(N_GROUPS):
            j = g % 2
            dma_fill("sync", f"h2g{j}", h2g_b[j], [
                (lambda e, g_=g, d=h2g_hi[j]: e.dma_start(
                    out=d, in_=s_h2t_hi[g_, :, :, :])),
                (lambda e, g_=g, d=h2g_lo[j]: e.dma_start(
                    out=d, in_=s_h2t_lo[g_, :, :, :]))],
                extra_reads=[b_sh2t])
            for m in range(KF):
                bank = m % 6
                nwin = slice(m * 128, (m + 1) * 128)
                mms = []
                for k in range(KD):
                    mms.append((w1_hi[:, k, nwin], h2g_hi[j][:, k, :],
                                [b_w1, h2g_b[j]]))
                    mms.append((w1_lo[:, k, nwin], h2g_hi[j][:, k, :],
                                [b_w1, h2g_b[j]]))
                for k in range(KD):
                    mms.append((w1_hi[:, k, nwin], h2g_lo[j][:, k, :],
                                [b_w1, h2g_b[j]]))
                mm_group(bank, psum[bank][:, :], mms)
                si = m % 4
                O("scalar", lambda e, b_=bank, m_=m, s_=si: e.activation(
                    stage_ap[s_], psum[b_][:, :], AF.Gelu,
                    bias=b1f[:, m_:m_ + 1], scale=1.0),
                  reads=[pb[bank], b_b1f], writes=[stage_b[si]])
                O("vector", lambda e, s_=si: e.tensor_copy(
                    stbf_hi[s_], stage_ap[s_]),
                  reads=[stage_b[si]], writes=[stbf_b[si]])
                O("vector", lambda e, s_=si: e.tensor_tensor(
                    stbf_lo[s_], stage_ap[s_], stbf_hi[s_], ALU.subtract),
                  reads=[stage_b[si], stbf_b[si]], writes=[stbf_b[si]])
                O("gpsimd", lambda e, g_=g, m_=m, s_=si: e.dma_start(
                    out=s_hidt_hi[m_, :, g_ * GROUP:(g_ + 1) * GROUP],
                    in_=stbf_hi[s_]),
                  reads=[stbf_b[si]], writes=[b_shidt], inc=16, sem=f"st{si}")
                O("gpsimd", lambda e, g_=g, m_=m, s_=si: e.dma_start(
                    out=s_hidt_lo[m_, :, g_ * GROUP:(g_ + 1) * GROUP],
                    in_=stbf_lo[s_]),
                  reads=[stbf_b[si]], writes=[b_shidt], inc=16, sem=f"st{si}")

        w2_hi, w2_lo = wpair(0, 32768, KF)
        b_w2 = Buf(barrier())
        stgD_b = [Buf(barrier()), Buf(barrier())]
        stream_split(w2_d, FF, D, 0, b_w2, 32, stgD_b)

        avD = AV()
        hidt_hi = [None, None]; hidt_lo = [None, None]; hidt_b = [None, None]
        hidt_hi[0], hidt_lo[0], hidt_b[0] = avD.take_pair("hidt0", KF * 128, KF, 128)
        hidt_hi[1], hidt_lo[1], hidt_b[1] = avD.take_pair("hidt1", KF * 128, KF, 128)
        xbD, b_xbD = avD.take("xbD", D)
        rbD, b_rbD = avD.take("rbD", D)
        s2D, b_s2D = avD.take("s2D", D)
        x1D, b_x1D = avD.take("x1D", D)
        scrD, b_scrD = avD.take("scrD", D)
        b2b, b_b2b = avD.take("b2b", D)
        x1TD, b_x1TD = avD.take("x1TD", D)
        x1TDv = x1TD.rearrange("p (k n) -> p k n", k=KD)
        O("sync", lambda e: e.dma_start(
            out=b2b, in_=b2_d[None, :].broadcast_to([128, D])),
          writes=[b_b2b], inc=16, sem="b2b")

        def hidt_load(t):
            j = t % 2
            emits = []
            for (plane_d, dstv) in ((s_hidt_hi, hidt_hi[j]),
                                    (s_hidt_lo, hidt_lo[j])):
                for q4 in range(4):
                    emits.append(lambda e, t_=t, q_=q4, d=dstv, s=plane_d:
                                 e.dma_start(
                        out=d[:, q_ * 8:(q_ + 1) * 8, :],
                        in_=s[q_ * 8:(q_ + 1) * 8, :, t_ * 128:(t_ + 1) * 128]
                        .rearrange("m p n -> p m n")))
            dma_fill("sync", f"hidt{j}", hidt_b[j], emits,
                     extra_reads=[b_shidt])

        def scr_load(t):
            O("sync", lambda e, t_=t: e.dma_start(
                out=xbD, in_=x_d[t_ * T_TILE:(t_ + 1) * T_TILE, :]),
              reads=[b_x_d], writes=[b_xbD], inc=16, sem="xbD")
            O("sync", lambda e, t_=t: e.dma_start(out=rbD, in_=s_r[t_, :, :]),
              reads=[b_sr], writes=[b_rbD], inc=16, sem="rbD")
            O("vector", lambda e: e.tensor_tensor(scrD, rbD, xbD, ALU.add),
              reads=[b_rbD, b_xbD], writes=[b_scrD])
            O("vector", lambda e: e.tensor_tensor(scrD, scrD, b2b, ALU.add),
              reads=[b_scrD, b_b2b], writes=[b_scrD])

        def phd_groups(t):
            j = t % 2
            bank0 = (2 * t) % 6
            for n in range(2):
                bank = (bank0 + n) % 6
                nwin = slice(n * 512, (n + 1) * 512)
                mms = []
                for k in range(KF):
                    mms.append((hidt_hi[j][:, k, :], w2_hi[:, k, nwin],
                                [hidt_b[j], b_w2]))
                    mms.append((hidt_hi[j][:, k, :], w2_lo[:, k, nwin],
                                [hidt_b[j], b_w2]))
                for k in range(KF):
                    mms.append((hidt_lo[j][:, k, :], w2_hi[:, k, nwin],
                                [hidt_b[j], b_w2]))
                mm_group(bank, psum[bank][:, :], mms)

        def phd_evac(t):
            bank0 = (2 * t) % 6
            for n in range(2):
                bank = (bank0 + n) % 6
                sl = slice(n * 512, (n + 1) * 512)
                O("vector", lambda e, b_=bank, s_=sl: e.tensor_tensor(
                    s2D[:, s_], psum[b_][:, :], scrD[:, s_], ALU.add),
                  reads=[pb[bank], b_scrD], writes=[b_s2D])
            if t + 1 < N_TILES:
                scr_load(t + 1)
            layernorm(s2D, b_s2D, x1D, b_x1D, scrD, b_scrD)
            O("vector", lambda e: e.tensor_reduce(
                stats[:, 7:8], x1D, axis=AX, op=ALU.max,
                apply_absolute_value=True),
              reads=[b_x1D], writes=[b_stats])
            O("vector", lambda e: e.tensor_tensor(
                amax1[:, :], amax1[:, :], stats[:, 7:8], ALU.max),
              reads=[b_amax1, b_stats], writes=[b_amax1])
            O("gpsimd", lambda e, t_=t: e.dma_start(out=s_x1[t_, :, :], in_=x1D),
              reads=[b_x1D], writes=[b_sx1], inc=16, sem="sx1")
            transpose8(x1D, b_x1D, x1TDv, b_x1TD,
                       pairs=(3, (t + 2) % 3, t % 3))
            gD, jjD = t // 4, t % 4
            O("gpsimd", lambda e, g_=gD, j_=jjD: e.dma_start(
                out=s_x1t[g_, :, :, j_ * 128:(j_ + 1) * 128], in_=x1TDv[:, :, :]),
              reads=[b_x1TD], writes=[b_sx1], inc=16, sem="sx1t")

        hidt_load(0)
        scr_load(0)
        phd_groups(0)
        for t in range(N_TILES):
            if t + 1 < N_TILES:
                hidt_load(t + 1)
                phd_groups(t + 1)
            phd_evac(t)

        def all_reduce_amax(amax_t, b_amax, cci, cco, b_cci, b_cco,
                            sc, b_sc, isc, b_isc):
            O("gpsimd", lambda e: e.tensor_reduce(
                ccrow[0:1, 0:1], amax_t[:, :], axis=AXALL, op=ALU.max),
              reads=[b_amax], writes=[b_ccrow])
            O("gpsimd", lambda e: e.dma_start(out=cci[:, :], in_=ccrow[:, :]),
              reads=[b_ccrow], writes=[b_cci], inc=16, sem="ccio")
            O("gpsimd", lambda e: e.collective_compute(
                "AllReduce", ALU.max, replica_groups=[list(range(N_CORES))],
                ins=[cci], outs=[cco]),
              reads=[b_cci], writes=[b_cco], sem="cc")
            O("gpsimd", lambda e: e.dma_start(
                out=gscal[:, :], in_=cco[0:1, 0:1].broadcast_to([128, 1])),
              reads=[b_cco], writes=[b_gscal], inc=16, sem="gsc")
            O("vector", lambda e: e.tensor_scalar(
                gscal[:, :], gscal[:, :], 1e-8, None, ALU.max),
              reads=[b_gscal], writes=[b_gscal])
            O("vector", lambda e, s=sc: e.reciprocal(s[:, :], gscal[:, :]),
              reads=[b_gscal], writes=[b_sc])
            O("vector", lambda e, s=sc: e.tensor_scalar(
                s[:, :], s[:, :], 32767.0, None, ALU.mult),
              reads=[b_sc], writes=[b_sc])
            O("vector", lambda e, s=sc, i_=isc: e.reciprocal(i_[:, :], s[:, :]),
              reads=[b_sc], writes=[b_isc])

        all_reduce_amax(amax1, b_amax1, cc1_in, cc1_out, b_cc1i, b_cc1o,
                        sc1, b_sc1, isc1, b_isc1)

        w_in_s = wfp.rearrange("p (k n) -> p k n", k=KD)
        b_winf32 = Buf(barrier())
        dma_fill("sync", "warena", b_winf32, [
            (lambda e: e.dma_start(
                out=w_in_s[:, 0:4, :],
                in_=win_d[0:512, :].rearrange("(k p) n -> p k n", p=128))),
            (lambda e: e.dma_start(
                out=w_in_s[:, 4:8, :],
                in_=win_d[512:1024, :].rearrange("(k p) n -> p k n", p=128)))])

        avE = AV()
        NW = 3
        wtm_ap = []; wtm_b = []
        wib_ap = []; wib_b = []
        for i_ in range(NW):
            a_, b_ = avE.take(f"wtm{i_}", D)
            wtm_ap.append(a_); wtm_b.append(b_)
        for i_ in range(NW):
            a_, b_ = avE.take(f"wib{i_}", 512)
            wib_ap.append(a_); wib_b.append(b_)

        def wq_tps_in(m):
            wtm = wtm_ap[m % NW]
            for k in range(KD):
                bank = 6 + (k % 2)
                O("tensor", lambda e, k_=k, m_=m, t=psum[bank]: e.transpose(
                    t[:, 0:128], w_in_s[:, k_, m_ * 128:(m_ + 1) * 128],
                    ident[:, :]),
                  reads=[b_winf32, b_ident], writes=[pb[bank]])
                O("scalar", lambda e, k_=k, t=psum[bank], d=wtm: e.copy(
                    d[:, k_ * 128:(k_ + 1) * 128], t[:, 0:128]),
                  reads=[pb[bank]], writes=[wtm_b[m % NW]])

        def wq_chain_store(m):
            wtm, b_wtm = wtm_ap[m % NW], wtm_b[m % NW]
            wib = wib_ap[m % NW].bitcast(BF16)
            O("vector", lambda e, d=wtm: e.tensor_reduce(
                wamax[:, :], d, axis=AX, op=ALU.max, apply_absolute_value=True),
              reads=[b_wtm], writes=[b_wamax])
            O("vector", lambda e: e.tensor_scalar(
                wamax[:, :], wamax[:, :], 1e-8, None, ALU.max),
              reads=[b_wamax], writes=[b_wamax])
            O("vector", lambda e: e.reciprocal(wsc[:, :], wamax[:, :]),
              reads=[b_wamax], writes=[b_wsc])
            O("vector", lambda e: e.tensor_scalar(
                wsc[:, :], wsc[:, :], 127.0, None, ALU.mult),
              reads=[b_wsc], writes=[b_wsc])
            O("vector", lambda e: e.reciprocal(wisc[:, :], wsc[:, :]),
              reads=[b_wsc], writes=[b_wisc])
            O("vector", lambda e, m_=m: e.tensor_copy(
                iscWi[:, m_:m_ + 1], wisc[:, :]),
              reads=[b_wisc], writes=[b_iscWi])
            O("vector", lambda e, d=wtm: e.tensor_scalar(
                d, d, wsc[:, :], MAGIC, ALU.mult, ALU.add),
              reads=[b_wtm, b_wsc], writes=[b_wtm])
            O("vector", lambda e, d=wtm: e.tensor_scalar(
                d, d, MAGIC, 127.0, ALU.subtract, ALU.min),
              reads=[b_wtm], writes=[b_wtm])
            O("vector", lambda e, d=wtm: e.tensor_scalar(
                d, d, -128.0, None, ALU.max),
              reads=[b_wtm], writes=[b_wtm])
            for k in range(KD):
                bank = 4 + (k % 2)
                O("tensor", lambda e, k_=k, t=psum[bank], d=wtm: e.transpose(
                    t[:, 0:128], d[:, k_ * 128:(k_ + 1) * 128], ident[:, :]),
                  reads=[b_wtm, b_ident], writes=[pb[bank]])
                O("vector", lambda e, k_=k, t=psum[bank], d=wib: e.tensor_copy(
                    d[:, k_ * 128:(k_ + 1) * 128], t[:, 0:128]),
                  reads=[pb[bank]], writes=[wib_b[m % NW]])
            O("gpsimd", lambda e, m_=m, d=wib: e.dma_start(
                out=s_wint.rearrange("k p n -> p k n")[
                    :, :, m_ * 128:(m_ + 1) * 128],
                in_=d.rearrange("p (k n) -> p k n", k=KD)),
              reads=[wib_b[m % NW]], writes=[b_swint[m % NW]],
              inc=16, sem=f"swint{m % NW}")

        for m in range(KF + 1):
            if m < KF:
                wq_tps_in(m)
            if m > 0:
                wq_chain_store(m - 1)

        winq = wbf[:, 0:32768].rearrange("p (k n) -> p k n", k=KD)
        b_winq = Buf(barrier())
        O("sync", lambda e: e.dma_start(
            out=winq, in_=s_wint.rearrange("k p n -> p k n")),
          reads=[b_swint[0], b_swint[1], b_swint[2]], writes=[b_winq], inc=16, sem="warena")
        O("vector", lambda e: e.tensor_scalar(
            cmbE[:, :], iscWi[:, :], isc1[:, 0:1], None, ALU.mult),
          reads=[b_iscWi, b_isc1], writes=[b_cmbE])

        avE2 = AV()
        _x1t_ap, _x1t_b = avE2.take("x1T", KD * GROUP)
        x1T_ap = [_x1t_ap, _x1t_ap]; x1T_b = [_x1t_b, _x1t_b]
        xsT_hi = [None, None]; xsT_lo = [None, None]; xsT_b = [None, None]
        xsT_hi[0], xsT_lo[0], xsT_b[0] = avE2.take_pair(
            "xsT0", KD * GROUP, KD, GROUP)
        xsT_hi[1], xsT_lo[1], xsT_b[1] = avE2.take_pair(
            "xsT1", KD * GROUP, KD, GROUP)
        stE_ap = []; stE_b = []
        for i in range(4):
            a_, b_ = avE2.take(f"stE{i}", 512)
            stE_ap.append(a_); stE_b.append(b_)
        stgE_ap = [None, None]; stgE_b = [None, None]
        stgE_ap[0], stgE_b[0] = avE2.take("stgE0", D)
        stgE_ap[1], stgE_b[1] = avE2.take("stgE1", D)

        def prep_group(g):
            jg = g % 2
            x1t, b_x1t = x1T_ap[jg], x1T_b[jg]
            O("sync", lambda e, g_=g, d=x1t.rearrange("p (k n) -> p k n", k=KD):
              e.dma_start(out=d, in_=s_x1t[g_, :, :, :]),
              reads=[b_sx1], writes=[b_x1t], inc=16, sem=f"x1T{jg}")
            O("vector", lambda e, s=x1t: e.tensor_scalar(
                s, s, sc1[:, :], MAGIC, ALU.mult, ALU.add),
              reads=[b_x1t, b_sc1], writes=[b_x1t])
            O("vector", lambda e, s=x1t: e.tensor_scalar(
                s, s, MAGIC, 32767.0, ALU.subtract, ALU.min),
              reads=[b_x1t], writes=[b_x1t])
            O("vector", lambda e, s=x1t: e.tensor_scalar(
                s, s, -32768.0, None, ALU.max),
              reads=[b_x1t], writes=[b_x1t])
            O("vector", lambda e, s=x1t, h=xsT_hi[jg]: e.tensor_copy(
                h.rearrange("p k n -> p (k n)"), s),
              reads=[b_x1t], writes=[xsT_b[jg]])
            O("vector", lambda e, s=x1t, h=xsT_hi[jg], l=xsT_lo[jg]:
              e.tensor_tensor(
                l.rearrange("p k n -> p (k n)"), s,
                h.rearrange("p k n -> p (k n)"), ALU.subtract),
              reads=[b_x1t, xsT_b[jg]], writes=[xsT_b[jg]])

        woq = wbf[:, 32768:65536].rearrange("p (k n) -> p k n", k=KF)
        b_woq = Buf(barrier())
        stgE2_b = [Buf(barrier()), Buf(barrier())]
        wo_band_bufs = [Buf() for _ in range(KF)]
        b_wamE = [Buf(barrier()), Buf(barrier())]
        b_wscE = [Buf(barrier()), Buf(barrier())]
        b_wiscE = [Buf(barrier()), Buf(barrier())]
        wamE = [wamax, wamax2]
        wscE = [wsc[:, :], wsc2[:, :]]
        wiscE = [wisc[:, :], wisc2[:, :]]

        def wout_band(k):
            j = k % 2
            stg, b_stg = stgE_ap[j], stgE_b[j]
            wam, bam = wamE[j], b_wamE[j]
            ws_, bws = wscE[j], b_wscE[j]
            wi_, bwi = wiscE[j], b_wiscE[j]
            bb = wo_band_bufs[k]
            O("sync", lambda e, k_=k, d=stg: e.dma_start(
                out=d, in_=wout_d[k_ * 128:(k_ + 1) * 128, :]),
              writes=[b_stg], inc=16, sem=f"stgE{j}")
            O("vector", lambda e, d=stg, w=wam: e.tensor_reduce(
                w[:, :], d, axis=AX, op=ALU.max, apply_absolute_value=True),
              reads=[b_stg], writes=[bam])
            O("vector", lambda e, w=wam: e.tensor_scalar(
                w[:, :], w[:, :], 1e-8, None, ALU.max),
              reads=[bam], writes=[bam])
            O("vector", lambda e, w=wam, s=ws_: e.reciprocal(s, w[:, :]),
              reads=[bam], writes=[bws])
            O("vector", lambda e, s=ws_: e.tensor_scalar(
                s, s, 127.0, None, ALU.mult),
              reads=[bws], writes=[bws])
            O("vector", lambda e, s=ws_, i_=wi_: e.reciprocal(i_, s),
              reads=[bws], writes=[bwi])
            O("vector", lambda e, k_=k, i_=wi_: e.tensor_copy(
                iscWo[:, k_:k_ + 1], i_),
              reads=[bwi], writes=[b_iscWo])
            O("vector", lambda e, d=stg, s=ws_: e.tensor_scalar(
                d, d, s, MAGIC, ALU.mult, ALU.add),
              reads=[b_stg, bws], writes=[b_stg])
            O("vector", lambda e, d=stg: e.tensor_scalar(
                d, d, MAGIC, 127.0, ALU.subtract, ALU.min),
              reads=[b_stg], writes=[b_stg])
            O("vector", lambda e, d=stg: e.tensor_scalar(
                d, d, -128.0, None, ALU.max),
              reads=[b_stg], writes=[b_stg])
            O("vector", lambda e, k_=k, d=stg: e.tensor_copy(
                woq[:, k_, :], d),
              reads=[b_stg], writes=[bb])

        prep_group(0)
        for g in range(N_GROUPS):
            jg = g % 2
            for kb in range(4):
                wout_band(g * 4 + kb)
            for m in range(KF):
                bank = m % 4
                nwin = slice(m * 128, (m + 1) * 128)
                mms = []
                for k in range(KD):
                    mms.append((winq[:, k, nwin], xsT_hi[jg][:, k, :],
                                [b_winq, xsT_b[jg]]))
                for k in range(KD):
                    mms.append((winq[:, k, nwin], xsT_lo[jg][:, k, :],
                                [b_winq, xsT_b[jg]]))
                mm_group(bank, psum[bank][:, :], mms)
                si = m % 4
                O("scalar", lambda e, b_=bank, m_=m, s_=si: e.activation(
                    stE_ap[s_], psum[b_][:, :], AF.Gelu,
                    bias=binf[:, m_:m_ + 1], scale=cmbE[:, m_:m_ + 1]),
                  reads=[pb[bank], b_binf, b_cmbE], writes=[stE_b[si]])
                O("vector", lambda e, s_=si: e.tensor_reduce(
                    stats[:, 7:8], stE_ap[s_], axis=AX, op=ALU.max,
                    apply_absolute_value=True),
                  reads=[stE_b[si]], writes=[b_stats])
                O("vector", lambda e: e.tensor_tensor(
                    amax2[:, :], amax2[:, :], stats[:, 7:8], ALU.max),
                  reads=[b_amax2, b_stats], writes=[b_amax2])
                O("gpsimd", lambda e, g_=g, m_=m, s_=si: e.dma_start(
                    out=s_h2at[m_, :, g_ * GROUP:(g_ + 1) * GROUP],
                    in_=stE_ap[s_]),
                  reads=[stE_b[si]], writes=[b_sh2at], inc=16, sem=f"st{si}")
                if m == 0 and g + 1 < N_GROUPS:
                    prep_group(g + 1)

        b_woq.w = b_woq.w + [t for bb in wo_band_bufs for t in bb.w]
        b_woq.r = []
        all_reduce_amax(amax2, b_amax2, cc2_in, cc2_out, b_cc2i, b_cc2o,
                        sc2, b_sc2, isc2, b_isc2)
        O("vector", lambda e: e.tensor_scalar(
            cmbF[:, :], iscWo[:, :], isc2[:, 0:1], None, ALU.mult),
          reads=[b_iscWo, b_isc2], writes=[b_cmbF])

        avF = AV()
        RF = []
        for i in range(3):
            offF = avF.off
            a_, b_ = avF.take(f"hF{i}", KF * 128)
            hiF = abf[:, 2 * offF: 2 * offF + KF * 128].rearrange(
                "p (k n) -> p k n", k=KF)
            loF = abf[:, 2 * offF + KF * 128: 2 * offF + 2 * KF * 128].rearrange(
                "p (k n) -> p k n", k=KF)
            RF.append([a_, hiF, loF, b_])
        x1F, b_x1F = avF.take("x1F", D)
        s2F, b_s2F = avF.take("s2F", D)
        outF, b_outF = avF.take("outF", D)
        scrF, b_scrF = avF.take("scrF", D)
        boutb, b_boutb = avF.take("boutb", D)
        O("sync", lambda e: e.dma_start(
            out=boutb, in_=bout_d[None, :].broadcast_to([128, D])),
          writes=[b_boutb], inc=16, sem="boutb")

        cmbFb = cmbF.ap() if hasattr(cmbF, "ap") else cmbF
        cmbF3 = cmbF[:, :, None].broadcast_to([128, KF, 128])

        def loadquantF(t):
            j = (t + 1) % 3
            src_ap, src_b = RF[j][0], RF[j][3]
            h2a = src_ap.rearrange("p (k n) -> p k n", k=KF)
            emitsF = []
            for q4 in range(4):
                emitsF.append(lambda e, t_=t, q_=q4, d=h2a: e.dma_start(
                    out=d[:, q_ * 8:(q_ + 1) * 8, :],
                    in_=s_h2at[q_ * 8:(q_ + 1) * 8, :, t_ * 128:(t_ + 1) * 128]
                    .rearrange("m p n -> p m n")))
            dma_fill("sync", f"h2a{j}", src_b, emitsF,
                     extra_reads=[b_sh2at])
            O("vector", lambda e, d=src_ap: e.tensor_scalar(
                d, d, sc2[:, :], MAGIC, ALU.mult, ALU.add),
              reads=[src_b, b_sc2], writes=[src_b])
            O("vector", lambda e, d=src_ap: e.tensor_scalar(
                d, d, MAGIC, 32767.0, ALU.subtract, ALU.min),
              reads=[src_b], writes=[src_b])
            O("vector", lambda e, d=src_ap: e.tensor_scalar(
                d, d, -32768.0, None, ALU.max),
              reads=[src_b], writes=[src_b])
            O("vector", lambda e, d=h2a: e.tensor_tensor(
                d, d, cmbF3, ALU.mult),
              reads=[src_b, b_cmbF], writes=[src_b])

        def splitF(t):
            src_ap, src_b = RF[(t + 1) % 3][0], RF[(t + 1) % 3][3]
            _, hiF, loF, dst_b = RF[t % 3]
            O("vector", lambda e, d=src_ap, h=hiF: e.tensor_copy(
                h.rearrange("p k n -> p (k n)"), d),
              reads=[src_b], writes=[dst_b])
            O("vector", lambda e, d=src_ap, h=hiF, l=loF: e.tensor_tensor(
                l.rearrange("p k n -> p (k n)"), d,
                h.rearrange("p k n -> p (k n)"), ALU.subtract),
              reads=[src_b, dst_b], writes=[dst_b])

        loadquantF(0)
        splitF(0)
        for t in range(N_TILES):
            j = t % 2
            bank0 = (2 * t) % 6
            O("sync", lambda e, t_=t: e.dma_start(out=x1F, in_=s_x1[t_, :, :]),
              reads=[b_sx1], writes=[b_x1F], inc=16, sem="x1F")
            _, mhi, mlo, b_m = RF[t % 3]
            for n in range(2):
                bank = (bank0 + n) % 6
                nwin = slice(n * 512, (n + 1) * 512)
                mms = []
                for k in range(KF):
                    mms.append((mhi[:, k, :], woq[:, k, nwin],
                                [b_m, b_woq]))
                for k in range(KF):
                    mms.append((mlo[:, k, :], woq[:, k, nwin],
                                [b_m, b_woq]))
                mm_group(bank, psum[bank][:, :], mms)
            O("vector", lambda e: e.tensor_tensor(scrF, boutb, x1F, ALU.add),
              reads=[b_boutb, b_x1F], writes=[b_scrF])
            if t + 1 < N_TILES:
                loadquantF(t + 1)
                splitF(t + 1)
            for n in range(2):
                bank = (bank0 + n) % 6
                sl = slice(n * 512, (n + 1) * 512)
                O("vector", lambda e, b_=bank, s_=sl: e.tensor_tensor(
                    s2F[:, s_], psum[b_][:, :], scrF[:, s_], ALU.add),
                  reads=[pb[bank], b_scrF], writes=[b_s2F])
            layernorm(s2F, b_s2F, outF, b_outF, scrF, b_scrF)
            O("gpsimd", lambda e, t_=t: e.dma_start(
                out=out_d[t_ * T_TILE:(t_ + 1) * T_TILE, :], in_=outF),
              reads=[b_outF], writes=[b_out_d], inc=16, sem="out")

        store_keys = [k for k in sch.counts
                      if k not in ENG and k.startswith(("sr", "sh2t", "st", "sx1",
                                                        "out", "swint"))]
        final_totals = [(k, sch.counts[k]) for k in store_keys]

        sems = {k: st.enter_context(nc.semaphore(f"sem_{k}"))
                for k in sch.counts}
        blk = st.enter_context(nc.Block())

        def make_body(ename):
            def body(eng):
                for waits, emit, inc, key in sch.ops[ename]:
                    for k2, v in waits.items():
                        eng.wait_ge(sems[k2], v)
                    inst = emit(eng)
                    inst.then_inc(sems[key], inc)
                if ename == "gpsimd":
                    for k, v in final_totals:
                        eng.wait_ge(sems[k], v)
            return body

        blk.sync(make_body("sync"))
        blk.tensor(make_body("tensor"))
        blk.vector(make_body("vector"))
        blk.scalar(make_body("scalar"))
        blk.gpsimd(make_body("gpsimd"))
    return nc


_MASK = np.triu(np.ones((128, 128), dtype=np.float32))
_IDENT = np.eye(128, dtype=np.float32)


def kernel(**inputs):
    f32 = np.float32

    def c(name):
        return np.ascontiguousarray(np.asarray(inputs[name], dtype=f32))

    x = c("L0_attn_x")
    xs = x.reshape(N_CORES, SEQ_PER_CORE * S, D)
    common = {
        "wq": c("L0_attn_wq"), "wk": c("L0_attn_wk"), "wv": c("L0_attn_wv"),
        "wo": c("L0_attn_wo"), "proj": c("L0_attn_proj"),
        "w1": c("L0_attn_w1"), "b1": c("L0_attn_b1"),
        "w2": c("L0_attn_w2"), "b2": c("L0_attn_b2"),
        "g1": c("L0_attn_g1"), "beta1": c("L0_attn_beta1"),
        "g2": c("L0_attn_g2"), "beta2": c("L0_attn_beta2"),
        "W_in": c("L0_ffn_W_in"), "b_in": c("L0_ffn_b_in"),
        "W_out": c("L0_ffn_W_out"), "b_out": c("L0_ffn_b_out"),
        "mask": _MASK, "ident": _IDENT,
    }
    in_maps = [dict(common, x=np.ascontiguousarray(xs[i])) for i in range(N_CORES)]
    nc = build_program()
    res = run_bass_kernel_spmd(nc, in_maps, core_ids=list(range(N_CORES)))
    out = np.stack([np.asarray(r["out"]) for r in res.results], axis=0)
    return out.reshape(16, S, D).astype(np.float32)


# version 30
# speedup vs baseline: 1.0086x; 1.0086x over previous
import contextlib
import numpy as np
import concourse.bass as bass
import concourse.mybir as mybir
from concourse.bass_utils import run_bass_kernel_spmd

F32 = mybir.dt.float32
BF16 = mybir.dt.bfloat16
AF = mybir.ActivationFunctionType
ALU = mybir.AluOpType
AX = mybir.AxisListType.X
AXALL = mybir.AxisListType.XYZWC
MAGIC = 1.5 * 2.0 ** 23

ENG = ("sync", "tensor", "vector", "scalar", "gpsimd")

N_CORES = 8
SEQ_PER_CORE = 2
S = 2048
D = 1024
H = 16
DH = 64
FF = 4096
T_TILE = 128
N_TILES = (SEQ_PER_CORE * S) // T_TILE
TILES_PER_SEQ = S // T_TILE
KD = D // 128
KF = FF // 128
GROUP = 512
N_GROUPS = (SEQ_PER_CORE * S) // GROUP
INV_D = 1.0 / D
EPS = 1e-5


class Buf:
    __slots__ = ("w", "r")

    def __init__(self, w=None):
        self.w = list(w) if w else []
        self.r = []


class Sched:

    def __init__(self):
        self.ops = {e: [] for e in ENG}
        self.counts = {}
        self.last_wait = {e: {} for e in ENG}

    def op(self, eng, emit, reads=(), writes=(), inc=1, sem=None):
        key = sem if sem is not None else eng
        deps = []
        for b in reads:
            deps.extend(b.w)
        for b in writes:
            deps.extend(b.w)
            deps.extend(b.r)
        waits = {}
        lw = self.last_wait[eng]
        for (k2, v) in deps:
            if v > lw.get(k2, 0) and v > waits.get(k2, 0):
                waits[k2] = v
        for k2, v in waits.items():
            lw[k2] = v
        self.counts[key] = self.counts.get(key, 0) + inc
        tok = (key, self.counts[key])
        self.ops[eng].append((waits, emit, inc, key))
        for b in reads:
            b.r.append(tok)
        for b in writes:
            b.w = [tok]
            b.r = []
        return tok

    def snapshot(self):
        return [(k, v) for k, v in self.counts.items() if v > 0 and k in ENG]


def build_program():
    nc = bass.Bass("TRN2", target_bir_lowering=False)

    NT = N_TILES * T_TILE
    x_d = nc.dram_tensor("x", [NT, D], F32, kind="ExternalInput").ap()
    wq_d = nc.dram_tensor("wq", [D, D], F32, kind="ExternalInput").ap()
    wk_d = nc.dram_tensor("wk", [D, D], F32, kind="ExternalInput").ap()
    wv_d = nc.dram_tensor("wv", [D, D], F32, kind="ExternalInput").ap()
    wo_d = nc.dram_tensor("wo", [D, D], F32, kind="ExternalInput").ap()
    proj_d = nc.dram_tensor("proj", [DH, DH], F32, kind="ExternalInput").ap()
    w1_d = nc.dram_tensor("w1", [D, FF], F32, kind="ExternalInput").ap()
    b1_d = nc.dram_tensor("b1", [FF], F32, kind="ExternalInput").ap()
    w2_d = nc.dram_tensor("w2", [FF, D], F32, kind="ExternalInput").ap()
    b2_d = nc.dram_tensor("b2", [D], F32, kind="ExternalInput").ap()
    g1_d = nc.dram_tensor("g1", [D], F32, kind="ExternalInput").ap()
    be1_d = nc.dram_tensor("beta1", [D], F32, kind="ExternalInput").ap()
    g2_d = nc.dram_tensor("g2", [D], F32, kind="ExternalInput").ap()
    be2_d = nc.dram_tensor("beta2", [D], F32, kind="ExternalInput").ap()
    win_d = nc.dram_tensor("W_in", [D, FF], F32, kind="ExternalInput").ap()
    bin_d = nc.dram_tensor("b_in", [FF], F32, kind="ExternalInput").ap()
    wout_d = nc.dram_tensor("W_out", [FF, D], F32, kind="ExternalInput").ap()
    bout_d = nc.dram_tensor("b_out", [D], F32, kind="ExternalInput").ap()
    mask_d = nc.dram_tensor("mask", [128, 128], F32, kind="ExternalInput").ap()
    ident_d = nc.dram_tensor("ident", [128, 128], F32, kind="ExternalInput").ap()
    out_d = nc.dram_tensor("out", [NT, D], F32, kind="ExternalOutput").ap()

    s_r = nc.dram_tensor("s_r", [N_TILES, T_TILE, D], F32).ap()
    s_h2t_hi = nc.dram_tensor("s_h2t_hi", [N_GROUPS, 128, KD, GROUP], BF16).ap()
    s_h2t_lo = nc.dram_tensor("s_h2t_lo", [N_GROUPS, 128, KD, GROUP], BF16).ap()
    s_hidt_hi = nc.dram_tensor("s_hidt_hi", [KF, 128, NT], BF16).ap()
    s_hidt_lo = nc.dram_tensor("s_hidt_lo", [KF, 128, NT], BF16).ap()
    s_x1 = nc.dram_tensor("s_x1", [N_TILES, T_TILE, D], F32).ap()
    s_x1t = nc.dram_tensor("s_x1t", [N_GROUPS, 128, KD, GROUP], F32).ap()
    s_h2at = nc.dram_tensor("s_h2at", [KF, 128, NT], F32).ap()
    s_wint = nc.dram_tensor("s_wint", [KD, 128, FF], BF16).ap()
    cc1_in = nc.dram_tensor("cc1_in", [1, 16], F32).ap()
    cc1_out = nc.dram_tensor("cc1_out", [1, 16], F32).ap()
    cc2_in = nc.dram_tensor("cc2_in", [1, 16], F32).ap()
    cc2_out = nc.dram_tensor("cc2_out", [1, 16], F32).ap()

    st = contextlib.ExitStack()
    with st:
        def sbt(name, shape, dt=F32):
            return st.enter_context(nc.sbuf_tensor(name, shape, dt))

        AR_ELEMS = 17920
        warena = sbt("warena", [128, 32768])
        aarena = sbt("aarena", [128, AR_ELEMS])
        ident = sbt("ident_s", [128, 128])
        mask = sbt("mask_s", [128, 128])
        b1f = sbt("b1f", [128, KF])
        binf = sbt("binf", [128, KF])
        iscWi = sbt("iscWi", [128, KF])
        cmbE = sbt("cmbE", [128, KF])
        iscWo = sbt("iscWo", [128, KF])
        cmbF = sbt("cmbF", [128, KF])
        kvb = sbt("kvb", [128, H // 2, DH + 1])
        vsall = sbt("vsall", [128, H, DH + 1])
        amax1 = sbt("amax1", [128, 1])
        amax2 = sbt("amax2", [128, 1])
        ccrow = sbt("ccrow", [1, 16])
        sc1 = sbt("sc1", [128, 1])
        isc1 = sbt("isc1", [128, 1])
        sc2 = sbt("sc2", [128, 1])
        isc2 = sbt("isc2", [128, 1])
        gscal = sbt("gscal", [128, 1])
        stats = sbt("stats", [128, 32])
        wamax = sbt("wamax", [128, 1])
        wsc = sbt("wsc", [128, 1])
        wisc = sbt("wisc", [128, 1])
        wamax2 = sbt("wamax2", [128, 1])
        wsc2 = sbt("wsc2", [128, 1])
        wisc2 = sbt("wisc2", [128, 1])

        psumt = [st.enter_context(nc.psum_tensor(f"pb{i}", [128, 1024], F32))
                 for i in range(4)]
        psum = [psumt[i // 2].ap()[:, (i % 2) * 512:(i % 2 + 1) * 512]
                for i in range(8)]
        psum_pair = [psumt[i].ap().rearrange("p (b n) -> p b n", b=2)
                     for i in range(4)]

        sch = Sched()
        O = sch.op
        pb = [Buf() for _ in range(8)]

        wfp = warena.ap()
        wbf = warena.ap().bitcast(BF16)
        abf = aarena.ap().bitcast(BF16)

        b_ident = Buf(); b_mask = Buf(); b_b1f = Buf(); b_binf = Buf()
        b_iscWi = Buf(); b_cmbE = Buf(); b_iscWo = Buf(); b_cmbF = Buf()
        b_kvh = [Buf() for _ in range(H)]
        b_vsall = Buf()
        b_amax1 = Buf(); b_amax2 = Buf(); b_ccrow = Buf()
        b_sc1 = Buf(); b_isc1 = Buf(); b_sc2 = Buf(); b_isc2 = Buf(); b_gscal = Buf()
        b_stats = Buf(); b_stats2 = Buf()
        b_wamax = Buf(); b_wsc = Buf(); b_wisc = Buf()
        b_x_d = Buf(); b_out_d = Buf()
        b_sr = Buf(); b_sh2t = Buf(); b_shidt = Buf(); b_sx1 = Buf()
        b_sh2at = Buf(); b_swint = [Buf(), Buf(), Buf()]
        b_cc1i = Buf(); b_cc1o = Buf(); b_cc2i = Buf(); b_cc2o = Buf()

        def barrier():
            return sch.snapshot()

        class AV:
            def __init__(self):
                self.off = 0

            def take(self, name, n_elems):
                lo = self.off
                self.off += n_elems
                assert self.off <= AR_ELEMS, f"aarena overflow at {name}: {self.off}"
                return aarena.ap()[:, lo:lo + n_elems], Buf(barrier())

            def take_pair(self, name, n_elems, kc, npc):
                lo_off = self.off
                self.off += n_elems
                assert self.off <= AR_ELEMS, f"aarena overflow at {name}: {self.off}"
                hi = abf[:, 2 * lo_off: 2 * lo_off + n_elems]
                lo = abf[:, 2 * lo_off + n_elems: 2 * lo_off + 2 * n_elems]
                if kc is not None:
                    hi = hi.rearrange("p (k n) -> p k n", k=kc)
                    lo = lo.rearrange("p (k n) -> p k n", k=kc)
                return hi, lo, Buf(barrier())

        def wpair(base, size, kc):
            hi = wbf[:, 2 * base: 2 * base + size].rearrange("p (k n) -> p k n", k=kc)
            lo = wbf[:, 2 * base + size: 2 * base + 2 * size].rearrange(
                "p (k n) -> p k n", k=kc)
            return hi, lo

        O("sync", lambda e: e.dma_start(out=ident[:, :], in_=ident_d[:, :]),
          writes=[b_ident], inc=16, sem="const")
        O("sync", lambda e: e.dma_start(out=mask[:, :], in_=mask_d[:, :]),
          writes=[b_mask], inc=16, sem="const")

        avP = AV()
        g1b, b_g1b = avP.take("g1b", D)
        be1b, b_be1b = avP.take("be1b", D)
        g2b, b_g2b = avP.take("g2b", D)
        be2b, b_be2b = avP.take("be2b", D)
        for (dst, bf, src) in ((g1b, b_g1b, g1_d), (be1b, b_be1b, be1_d),
                               (g2b, b_g2b, g2_d), (be2b, b_be2b, be2_d)):
            O("sync", lambda e, d=dst, s=src: e.dma_start(
                out=d, in_=s[None, :].broadcast_to([128, D])),
              writes=[bf], inc=16, sem="const")

        xb_ap = [None, None]; xb_b = [None, None]
        xb_ap[0], xb_b[0] = avP.take("xb0", D)
        xb_ap[1], xb_b[1] = avP.take("xb1", D)
        scrA, b_scrA = avP.take("scrA", D)
        xn, b_xn = avP.take("xn", D)
        hT_hi, hT_lo, b_hT = avP.take_pair("hT", D, KD, 128)
        qphi, b_qphi = avP.take("qphi", D)
        kphi, b_kphi = avP.take("kphi", D)
        scrE, b_scrE0 = avP.take("scrE", D)
        b_scrE = [b_scrE0, Buf(barrier())]
        qT, b_qT = avP.take("qT", D)
        kT, b_kT = avP.take("kT", D)
        attn, b_attn = avP.take("attn", D)
        aT_hi, aT_lo, b_aT = avP.take_pair("attnT", D, KD, 128)
        _h2T_off = avP.off
        h2T_hi, h2T_lo, b_h2T = avP.take_pair("h2T", D, KD, 128)
        den, b_den = avP.take("den", 8)
        smb, b_smb = avP.take("smb", 256)
        qTv = qT.rearrange("p (k n) -> p k n", k=KD)
        kTv = kT.rearrange("p (k n) -> p k n", k=KD)


        O("vector", lambda e: e.memset(vsall[:, :, DH:DH + 1], 1.0),
          writes=[b_vsall])
        O("vector", lambda e: e.memset(amax1[:, :], 0.0), writes=[b_amax1])
        O("vector", lambda e: e.memset(amax2[:, :], 0.0), writes=[b_amax2])
        O("vector", lambda e: e.memset(ccrow[:, :], 0.0), writes=[b_ccrow])

        proj2 = qT[:, 128:192]
        b_proj2 = Buf()
        O("sync", lambda e: e.dma_start(out=proj2[0:DH, :], in_=proj_d[:, :]),
          writes=[b_proj2], inc=16, sem="const")
        O("sync", lambda e: e.dma_start(out=proj2[DH:128, :], in_=proj_d[:, :]),
          writes=[b_proj2], inc=16, sem="const")
        _ctot = [("const", sch.counts["const"])]
        for _b in (b_ident, b_mask, b_g1b, b_be1b, b_g2b, b_be2b, b_proj2):
            _b.w = list(_ctot)

        for _tt in (0, 1):
            O("sync", lambda e, t_=_tt, d=xb_ap[_tt]: e.dma_start(
                out=d, in_=x_d[t_ * T_TILE:(t_ + 1) * T_TILE, :]),
              reads=[b_x_d], writes=[xb_b[_tt]], inc=16, sem=f"xb{_tt}")

        btmp = qT[:, 0:128]
        b_btmp = Buf()
        for (srcd, dst_t, dstb) in ((b1_d, b1f, b_b1f), (bin_d, binf, b_binf)):
            srcv = srcd.rearrange("(m p) -> m p", p=128)
            O("sync", lambda e, s=srcv, d=btmp: e.dma_start(out=d[0:KF, :], in_=s),
              writes=[b_btmp], inc=16, sem="stg0")
            O("tensor", lambda e, d=btmp: e.transpose(
                psum[0][:, 0:KF], d[0:KF, :], ident[0:KF, 0:KF]),
              reads=[b_btmp, b_ident], writes=[pb[0]])
            O("vector", lambda e, d=dst_t: e.tensor_copy(d[:, :], psum[0][:, 0:KF]),
              reads=[pb[0]], writes=[dstb])

        fst_ap = [qT[:, 192:320], qT[:, 320:448]]
        fst_b = [Buf(), Buf()]
        stgw_ap = [scrA, xn]
        stgw_b = [Buf(), Buf()]

        wq_hi, wq_lo = wpair(0, 8192, KD)
        wk_hi, wk_lo = wpair(8192, 8192, KD)
        wv_hi, wv_lo = wpair(16384, 8192, KD)
        wo_hi, wo_lo = wpair(24576, 8192, KD)
        b_wq = Buf(); b_wk = Buf(); b_wv = Buf(); b_wo = Buf()

        wtmp_s = wfp[:, 16384:24576].rearrange("p (k n) -> p k n", k=KD)
        b_wtmp = Buf()

        def fold_split(src_d, hi_flat, lo_flat, b_w):
            O("sync", lambda e, s=src_d: e.dma_start(
                out=wtmp_s, in_=s.rearrange("(k p) n -> p k n", p=128)),
              writes=[b_wtmp], inc=16, sem="warena")
            for k in range(KD):
                for mp in range(KD):
                    i = k * KD + mp
                    bank = 6 + (i % 2)
                    fs, b_fs = fst_ap[i % 2], fst_b[i % 2]
                    ob = (2 * (i % 2), 2 * (i % 2) + 1)
                    O("tensor", lambda e, k_=k, mp_=mp, t=psum[bank]:
                      e.transpose(t[:, 0:128],
                                  wtmp_s[:, k_, mp_ * 128:(mp_ + 1) * 128],
                                  ident[:, :]),
                      reads=[b_wtmp, b_ident], writes=[pb[bank]])
                    O("scalar", lambda e, t=psum[bank], d=fs: e.copy(
                        d, t[:, 0:128]), reads=[pb[bank]], writes=[b_fs])
                    O("tensor", lambda e, d=fs, p=proj2, o=ob[0]: e.matmul(
                        psum[o][:, 0:DH], d[0:DH, :], p[0:DH, :],
                        start=True, stop=True),
                      reads=[b_fs, b_proj2], writes=[pb[ob[0]]])
                    O("tensor", lambda e, d=fs, p=proj2, o=ob[1]: e.matmul(
                        psum[o][:, 0:DH], d[DH:128, :], p[DH:128, :],
                        start=True, stop=True),
                      reads=[b_fs, b_proj2], writes=[pb[ob[1]]])
                    O("vector", lambda e, k_=k, mp_=mp, o=ob[0]: e.tensor_copy(
                        wtmp_s[:, k_, mp_ * 128: mp_ * 128 + DH],
                        psum[o][:, 0:DH]),
                      reads=[pb[ob[0]]], writes=[b_wtmp])
                    O("vector", lambda e, k_=k, mp_=mp, o=ob[1]: e.tensor_copy(
                        wtmp_s[:, k_, mp_ * 128 + DH:(mp_ + 1) * 128],
                        psum[o][:, 0:DH]),
                      reads=[pb[ob[1]]], writes=[b_wtmp])
            src_flat = wfp[:, 16384:24576]
            O("vector", lambda e, h=hi_flat: e.tensor_copy(h, src_flat),
              reads=[b_wtmp], writes=[b_w])
            O("vector", lambda e, h=hi_flat, l=lo_flat: e.tensor_tensor(
                l, src_flat, h, ALU.subtract),
              reads=[b_wtmp, b_w], writes=[b_w])

        def stream_split(src_d, rows, cols, base, b_w, nbands, stg_bufs,
                         extra_writes=()):
            size = rows * cols // 128
            hi_flat = wbf[:, 2 * base: 2 * base + size]
            lo_flat = wbf[:, 2 * base + size: 2 * base + 2 * size]
            per_row = cols
            bands_per_k = per_row // 1024
            ew = list(extra_writes)
            band_bufs = [Buf() for _ in range(nbands)]
            for i in range(nbands):
                k, w = i // bands_per_k, i % bands_per_k
                j = i % 2
                stg, b_stg = stgw_ap[j], stg_bufs[j]
                O("sync", lambda e, k_=k, w_=w, d=stg, s=src_d: e.dma_start(
                    out=d, in_=s[k_ * 128:(k_ + 1) * 128,
                                 w_ * 1024:(w_ + 1) * 1024]),
                  writes=[b_stg], inc=16, sem=f"stg{j}")
                off = k * per_row + w * 1024
                bb = band_bufs[i]
                O("vector", lambda e, d=stg, o=off: e.tensor_copy(
                    hi_flat[:, o:o + 1024], d),
                  reads=[b_stg], writes=[bb] + ew)
                O("vector", lambda e, d=stg, o=off: e.tensor_tensor(
                    lo_flat[:, o:o + 1024], d, hi_flat[:, o:o + 1024],
                    ALU.subtract),
                  reads=[b_stg, bb], writes=[bb] + ew)
            b_w.w = [t for bb in band_bufs for t in bb.w]
            b_w.r = []

        fold_split(wq_d, wbf[:, 0:8192], wbf[:, 8192:16384], b_wq)
        fold_split(wk_d, wbf[:, 16384:24576], wbf[:, 24576:32768], b_wk)
        stream_split(wv_d, D, D, 16384, b_wv, 8, stgw_b, extra_writes=[b_wtmp])
        stream_split(wo_d, D, D, 24576, b_wo, 8, stgw_b)

        _bar = barrier()
        for _b in (b_scrA, b_xn, b_qT):
            _b.w = list(_bar)
        xn2, b_xn2 = scrA, b_scrA

        def ln_stats(src_ap, src_b, so, sb):
            O("vector", lambda e, s=src_ap: e.bn_stats(
                stats[:, so:so + 6], s[:, 0:512]),
              reads=[src_b], writes=[sb])
            O("vector", lambda e, s=src_ap: e.bn_stats(
                stats[:, so + 6:so + 12], s[:, 512:1024]),
              reads=[src_b], writes=[sb])
            O("vector", lambda e: e.bn_aggr(
                stats[:, so + 12:so + 14], stats[:, so:so + 12]),
              reads=[sb], writes=[sb])
            O("vector", lambda e: e.tensor_scalar(
                stats[:, so + 13:so + 14], stats[:, so + 13:so + 14],
                EPS, None, ALU.add),
              reads=[sb], writes=[sb])
            O("scalar", lambda e: e.activation(
                stats[:, so + 14:so + 15], stats[:, so + 13:so + 14], AF.Sqrt),
              reads=[sb], writes=[sb])
            O("vector", lambda e: e.reciprocal(
                stats[:, so + 15:so + 16], stats[:, so + 14:so + 15]),
              reads=[sb], writes=[sb])

        def ln_norm(src_ap, src_b, dst_ap, dst_b, so, sb,
                    gb=None, bb=None, gb_b=None, bb_b=None):
            O("vector", lambda e, s=src_ap, d=dst_ap: e.tensor_scalar(
                d, s, stats[:, so + 12:so + 13], stats[:, so + 15:so + 16],
                ALU.subtract, ALU.mult),
              reads=[src_b, sb], writes=[dst_b])
            if gb is not None:
                O("vector", lambda e, d=dst_ap, g=gb: e.tensor_tensor(
                    d, d, g, ALU.mult), reads=[dst_b, gb_b], writes=[dst_b])
                O("vector", lambda e, d=dst_ap, b=bb: e.tensor_tensor(
                    d, d, b, ALU.add), reads=[dst_b, bb_b], writes=[dst_b])

        def layernorm(src_ap, src_b, dst_ap, dst_b, scr_ap, scr_b,
                      gb=None, bb=None, gb_b=None, bb_b=None):
            ln_stats(src_ap, src_b, 0, b_stats)
            ln_norm(src_ap, src_b, dst_ap, dst_b, 0, b_stats,
                    gb, bb, gb_b, bb_b)

        def transpose8(src_ap, src_b, dstv, dst_b, evac_eng="scalar",
                       pairs=(3, 0)):
            for p in range(4):
                pp = pairs[p % len(pairs)]
                for h in range(2):
                    k = 2 * p + h
                    bank = 2 * pp + h
                    O("tensor", lambda e, k_=k, s=src_ap, t=psum[bank]:
                      e.transpose(t[:, 0:128],
                                  s[:, k_ * 128:(k_ + 1) * 128], ident[:, :]),
                      reads=[src_b, b_ident], writes=[pb[bank]])
                pv = psum_pair[pp][:, :, 0:128]
                if evac_eng == "scalar":
                    O("scalar", lambda e, p_=p, pv=pv, d=dstv: e.copy(
                        d[:, 2 * p_:2 * p_ + 2, :], pv),
                      reads=[pb[2 * pp], pb[2 * pp + 1]], writes=[dst_b])
                else:
                    O("vector", lambda e, p_=p, pv=pv, d=dstv: e.tensor_copy(
                        d[:, 2 * p_:2 * p_ + 2, :], pv),
                      reads=[pb[2 * pp], pb[2 * pp + 1]], writes=[dst_b])

        def transpose8_pair(src_ap, src_b, hiv, lov, dst_b, pairs=(3, 0)):
            for p in range(4):
                pp = pairs[p % len(pairs)]
                for h in range(2):
                    k = 2 * p + h
                    bank = 2 * pp + h
                    O("tensor", lambda e, k_=k, s=src_ap, t=psum[bank]:
                      e.transpose(t[:, 0:128],
                                  s[:, k_ * 128:(k_ + 1) * 128], ident[:, :]),
                      reads=[src_b, b_ident], writes=[pb[bank]])
                pv = psum_pair[pp][:, :, 0:128]
                O("scalar", lambda e, p_=p, pv=pv, d=hiv: e.copy(
                    d[:, 2 * p_:2 * p_ + 2, :], pv),
                  reads=[pb[2 * pp], pb[2 * pp + 1]], writes=[dst_b])
                O("vector", lambda e, p_=p, pv=pv, h=hiv, l=lov:
                  e.tensor_tensor(l[:, 2 * p_:2 * p_ + 2, :], pv,
                                  h[:, 2 * p_:2 * p_ + 2, :], ALU.subtract),
                  reads=[pb[2 * pp], pb[2 * pp + 1], dst_b], writes=[dst_b])

        class GroupEmitter:

            def __init__(self, bank_i, out_view, total):
                self.bank = bank_i
                self.ov = out_view
                self.total = total
                self.i = 0

            def emit(self, mms):
                for (lt, rt, rbufs) in mms:
                    first = self.i == 0
                    last = self.i == self.total - 1
                    if first:
                        sch.op("tensor",
                               lambda e, lt=lt, rt=rt, ov=self.ov, l=last:
                               e.matmul(ov, lt, rt, start=True, stop=l),
                               reads=list(rbufs), writes=[pb[self.bank]])
                    else:
                        sch.op("tensor",
                               lambda e, lt=lt, rt=rt, ov=self.ov, l=last:
                               e.matmul(ov, lt, rt, start=False, stop=l),
                               reads=list(rbufs))
                    self.i += 1
                if self.i == self.total:
                    pb[self.bank].w = [("tensor", sch.counts["tensor"])]
                    pb[self.bank].r = []

        def mm_group(bank_i, out_view, mms):
            n = len(mms)
            for i, (lt, rt, rbufs) in enumerate(mms):
                if i == 0:
                    sch.op("tensor",
                           lambda e, lt=lt, rt=rt, ov=out_view, last=(n == 1):
                           e.matmul(ov, lt, rt, start=True, stop=last),
                           reads=list(rbufs), writes=[pb[bank_i]])
                else:
                    sch.op("tensor",
                           lambda e, lt=lt, rt=rt, ov=out_view, last=(i == n - 1):
                           e.matmul(ov, lt, rt, start=False, stop=last),
                           reads=list(rbufs))
            pb[bank_i].w = [("tensor", sch.counts["tensor"])]
            pb[bank_i].r = []

        def dma_fill(eng, key, buf, emits, extra_reads=()):
            for i, em in enumerate(emits):
                if i == 0:
                    sch.op(eng, em, reads=list(extra_reads), writes=[buf],
                           inc=16, sem=key)
                else:
                    sch.op(eng, em, reads=list(extra_reads), inc=16, sem=key)
            buf.w = [(key, sch.counts[key])]
            buf.r = []

        def mm3(xhi, xlo, b_x, whi, wlo, b_w, nwin):
            mms = []
            for k in range(KD):
                mms.append((xhi[:, k, :], whi[:, k, nwin], [b_x, b_w]))
                mms.append((xhi[:, k, :], wlo[:, k, nwin], [b_x, b_w]))
            for k in range(KD):
                mms.append((xlo[:, k, :], whi[:, k, nwin], [b_x, b_w]))
            return mms

        def elu_pair(b0, b1, dst, dstb):
            halves = [Buf(dstb.w + dstb.r), Buf(dstb.w + dstb.r)]
            for i, bk in ((0, b0), (1, b1)):
                O("vector", lambda e, i_=i, bk_=bk: e.tensor_scalar(
                    scrE[:, i_ * 512:(i_ + 1) * 512],
                    psum[bk_][:, 0:512], 0.0, None, ALU.min),
                  reads=[pb[bk]], writes=[b_scrE[i]])
            toks = []
            for i, bk in ((0, b0), (1, b1)):
                d = dst[:, i * 512:(i + 1) * 512]
                O("scalar", lambda e, i_=i, d=d: e.activation(
                    d, scrE[:, i_ * 512:(i_ + 1) * 512], AF.Exp),
                  reads=[b_scrE[i]], writes=[halves[i]])
                tk = O("vector", lambda e, d=d, bk_=bk: e.scalar_tensor_tensor(
                    d, psum[bk_][:, 0:512], 0.0, d, ALU.max, ALU.add),
                  reads=[pb[bk], halves[i]], writes=[halves[i]])
                toks.append(tk)
            dstb.w = list(toks)
            dstb.r = []

        def k_stage(t):
            for n in range(2):
                bank = 2 + n
                nwin = slice(n * 512, (n + 1) * 512)
                mm_group(bank, psum[bank][:, :],
                         mm3(hT_hi, hT_lo, b_hT, wk_hi, wk_lo, b_wk, nwin))

        def v_stage(t):
            for n in range(2):
                bank = 4 + n
                nwin = slice(n * 512, (n + 1) * 512)
                mm_group(bank, psum[bank][:, :],
                         mm3(hT_hi, hT_lo, b_hT, wv_hi, wv_lo, b_wv, nwin))

        def v_evac(t):
            for n in range(2):
                O("scalar", lambda e, n_=n: e.copy(
                    vsall[:, n_ * 8:(n_ + 1) * 8, 0:DH],
                    psum[4 + n_][:, :].rearrange("p (h n) -> p h n", h=8)),
                  reads=[pb[4 + n]], writes=[b_vsall])

        def h2t_stage(t):
            transpose8_pair(xn2, b_xn2, h2T_hi, h2T_lo, b_h2T,
                            pairs=(3, 0, 1))
            g, jj = t // 4, t % 4
            dma_fill("gpsimd", "sh2t", b_sh2t, [
                (lambda e, g_=g, j_=jj, d=h2T_hi: e.dma_start(
                    out=s_h2t_hi[g_, :, :, j_ * 128:(j_ + 1) * 128],
                    in_=d[:, :, :])),
                (lambda e, g_=g, j_=jj, d=h2T_lo: e.dma_start(
                    out=s_h2t_lo[g_, :, :, j_ * 128:(j_ + 1) * 128],
                    in_=d[:, :, :]))],
                extra_reads=[b_h2T])

        def q_mms(t):
            out = []
            for n in range(2):
                nwin = slice(n * 512, (n + 1) * 512)
                out.append((n, mm3(hT_hi, hT_lo, b_hT, wq_hi, wq_lo, b_wq, nwin)))
            return out

        def attn_stage(t, interleave_q):
            c = t % TILES_PER_SEQ
            if c == 0:
                O("vector", lambda e: e.memset(kvb[:, :, :], 0.0), writes=b_kvh)
            mask3 = mask.ap().rearrange("p (o n) -> p o n", o=1).broadcast_to(
                [128, 2, 128])
            smbv = smb.rearrange("p (b n) -> p b n", b=2)
            den01 = den[:, 0:2].rearrange("p (b o) -> p b o", b=2)
            qgroups = [GroupEmitter(n, psum[n][:, :], 24) for n in range(2)]
            qchunks = None
            if interleave_q:
                qm = q_mms(t + 1)
                flat = [(n, mm) for n, mms in qm for mm in mms]
                qchunks = [flat[i * 6:(i + 1) * 6] for i in range(8)]
            for hpair in range(H // 2):
                h0 = 2 * hpair
                h1 = h0 + 1
                blkh = hpair
                for hp, h in ((0, h0), (1, h1)):
                    p0 = hp * DH
                    O("tensor", lambda e, p0_=p0, blk=blkh, sb_=2 + hp: e.matmul(
                        psum[sb_][:, 0:128],
                        kTv[p0_:p0_ + DH, blk, :], qTv[p0_:p0_ + DH, blk, :],
                        start=True, stop=True),
                      reads=[b_kT, b_qT], writes=[pb[2 + hp]])
                O("vector", lambda e, sm=smbv: e.tensor_tensor(
                    sm, psum_pair[1][:, :, 0:128], mask3, ALU.mult),
                  reads=[pb[2], pb[3], b_mask], writes=[b_smb])
                if qchunks is not None:
                    for n, mm in qchunks[hpair][:3]:
                        qgroups[n].emit([mm])
                for hp, h in ((0, h0), (1, h1)):
                    p0 = hp * DH
                    nbank = 4 + hp
                    mms = []
                    if c > 0:
                        mms.append((qTv[p0:p0 + DH, blkh, :],
                                    kvb[p0:p0 + DH, blkh, :], [b_qT, b_kvh[h]]))
                    mms.append((smb[:, hp * 128:(hp + 1) * 128],
                                vsall[:, h, :], [b_smb, b_vsall]))
                    mm_group(nbank, psum[nbank][:, 0:DH + 1], mms)
                O("vector", lambda e, d=den01: e.tensor_scalar(
                    d, psum_pair[2][:, :, DH:DH + 1], 1e-6, None, ALU.add),
                  reads=[pb[4], pb[5]], writes=[b_den])
                O("vector", lambda e: e.reciprocal(den[:, 2:4], den[:, 0:2]),
                  reads=[b_den], writes=[b_den])
                O("vector", lambda e, h0_=h0: e.tensor_tensor(
                    attn[:, h0_ * DH:(h0_ + 2) * DH].rearrange(
                        "p (b n) -> p b n", b=2),
                    psum_pair[2][:, :, 0:DH],
                    den[:, 2:4].rearrange("p (b o) -> p b o", b=2)
                    .broadcast_to([128, 2, DH]), ALU.mult),
                  reads=[pb[4], pb[5], b_den], writes=[b_attn])
                for hp, h in ((0, h0), (1, h1)):
                    p0 = hp * DH
                    kbank = 2 + hp
                    O("tensor", lambda e, h_=h, kb=kbank: e.matmul(
                        psum[kb][0:DH, 0:DH + 1], kphi[:, h_ * DH:(h_ + 1) * DH],
                        vsall[:, h_, :], start=True, stop=True),
                      reads=[b_kphi, b_vsall], writes=[pb[kbank]])
                    O("vector", lambda e, p0_=p0, blk=blkh, kb=kbank:
                      e.tensor_tensor(
                        kvb[p0_:p0_ + DH, blk, :], kvb[p0_:p0_ + DH, blk, :],
                        psum[kb][0:DH, 0:DH + 1], ALU.add),
                      reads=[b_kvh[h], pb[kbank]], writes=[b_kvh[h]])
                if qchunks is not None:
                    for n, mm in qchunks[hpair][3:]:
                        qgroups[n].emit([mm])
            if interleave_q:
                assert qgroups[0].i == 24 and qgroups[1].i == 24

        def wo_stage(t, xb, b_xb):
            transpose8_pair(attn, b_attn, aT_hi, aT_lo, b_aT, pairs=(3, 2))
            for n in range(2):
                bank = 4 + n
                nwin = slice(n * 512, (n + 1) * 512)
                mm_group(bank, psum[bank][:, :],
                         mm3(aT_hi, aT_lo, b_aT, wo_hi, wo_lo, b_wo, nwin))
                O("vector", lambda e, b_=bank, n_=n, x_=xb: e.tensor_tensor(
                    attn[:, n_ * 512:(n_ + 1) * 512], psum[b_][:, :],
                    x_[:, n_ * 512:(n_ + 1) * 512], ALU.add),
                  reads=[pb[bank], b_xb], writes=[b_attn])
            O("gpsimd", lambda e, t_=t: e.dma_start(
                out=s_r[t_, :, :], in_=attn[:, :]),
              reads=[b_attn], writes=[b_sr], inc=16, sem="sr")

        def xload(tt):
            jn = tt % 2
            O("sync", lambda e, t_=tt, d=xb_ap[jn]: e.dma_start(
                out=d, in_=x_d[t_ * T_TILE:(t_ + 1) * T_TILE, :]),
              reads=[b_x_d], writes=[xb_b[jn]], inc=16, sem=f"xb{jn}")

        layernorm(xb_ap[0], xb_b[0], xn, b_xn, scrA, b_scrA,
                  g1b, be1b, b_g1b, b_be1b)
        transpose8_pair(xn, b_xn, hT_hi, hT_lo, b_hT)
        for n, mms in q_mms(-1):
            mm_group(n, psum[n][:, :], mms)
        k_stage(0)
        v_stage(0)
        for t in range(N_TILES):
            j = t % 2
            elu_pair(0, 1, qphi, b_qphi)
            elu_pair(2, 3, kphi, b_kphi)
            if t + 1 < N_TILES:
                ln_stats(xb_ap[(t + 1) % 2], xb_b[(t + 1) % 2], 0, b_stats)
            if t > 0:
                ln_stats(attn, b_attn, 16, b_stats2)
            if t + 1 < N_TILES:
                ln_norm(xb_ap[(t + 1) % 2], xb_b[(t + 1) % 2], xn, b_xn,
                        0, b_stats, g1b, be1b, b_g1b, b_be1b)
            if t > 0:
                ln_norm(attn, b_attn, xn2, b_xn2, 16, b_stats2,
                        g2b, be2b, b_g2b, b_be2b)
            transpose8(qphi, b_qphi, qTv, b_qT, pairs=(3, 0, 1))
            transpose8(kphi, b_kphi, kTv, b_kT, pairs=(3, 0, 1))
            if t + 1 < N_TILES:
                transpose8_pair(xn, b_xn, hT_hi, hT_lo, b_hT,
                                pairs=(3, 0, 1))
            if t > 0:
                h2t_stage(t - 1)
            v_evac(t)
            attn_stage(t, interleave_q=(t + 1 < N_TILES))
            if t + 1 < N_TILES:
                k_stage(t + 1)
            wo_stage(t, xb_ap[j], xb_b[j])
            if t + 1 < N_TILES:
                v_stage(t + 1)
            if t + 2 < N_TILES:
                xload(t + 2)
        layernorm(attn, b_attn, xn2, b_xn2, scrA, b_scrA,
                  g2b, be2b, b_g2b, b_be2b)
        h2t_stage(N_TILES - 1)


        w1_hi, w1_lo = wpair(0, 32768, KD)
        b_w1 = Buf(barrier())
        stgC_b = [Buf(barrier()), Buf(barrier())]
        stream_split(w1_d, D, FF, 0, b_w1, 32, stgC_b)

        avC = AV()
        h2g_hi = [None, None]; h2g_lo = [None, None]; h2g_b = [None, None]
        h2g_hi[0], h2g_lo[0], h2g_b[0] = avC.take_pair("h2g0", KD * GROUP, KD, GROUP)
        h2g_hi[1], h2g_lo[1], h2g_b[1] = avC.take_pair("h2g1", KD * GROUP, KD, GROUP)
        stage_ap = []; stage_b = []
        for i in range(4):
            a_, b_ = avC.take(f"stC{i}", 512)
            stage_ap.append(a_); stage_b.append(b_)
        stbf_hi = []; stbf_lo = []; stbf_b = []
        for i in range(4):
            h_, l_, b_ = avC.take_pair(f"stCb{i}", 512, None, None)
            stbf_hi.append(h_); stbf_lo.append(l_); stbf_b.append(b_)

        for g in range(N_GROUPS):
            j = g % 2
            dma_fill("sync", f"h2g{j}", h2g_b[j], [
                (lambda e, g_=g, d=h2g_hi[j]: e.dma_start(
                    out=d, in_=s_h2t_hi[g_, :, :, :])),
                (lambda e, g_=g, d=h2g_lo[j]: e.dma_start(
                    out=d, in_=s_h2t_lo[g_, :, :, :]))],
                extra_reads=[b_sh2t])
            for m in range(KF):
                bank = m % 6
                nwin = slice(m * 128, (m + 1) * 128)
                mms = []
                for k in range(KD):
                    mms.append((w1_hi[:, k, nwin], h2g_hi[j][:, k, :],
                                [b_w1, h2g_b[j]]))
                    mms.append((w1_lo[:, k, nwin], h2g_hi[j][:, k, :],
                                [b_w1, h2g_b[j]]))
                for k in range(KD):
                    mms.append((w1_hi[:, k, nwin], h2g_lo[j][:, k, :],
                                [b_w1, h2g_b[j]]))
                mm_group(bank, psum[bank][:, :], mms)
                si = m % 4
                O("scalar", lambda e, b_=bank, m_=m, s_=si: e.activation(
                    stage_ap[s_], psum[b_][:, :], AF.Gelu,
                    bias=b1f[:, m_:m_ + 1], scale=1.0),
                  reads=[pb[bank], b_b1f], writes=[stage_b[si]])
                O("vector", lambda e, s_=si: e.tensor_copy(
                    stbf_hi[s_], stage_ap[s_]),
                  reads=[stage_b[si]], writes=[stbf_b[si]])
                O("vector", lambda e, s_=si: e.tensor_tensor(
                    stbf_lo[s_], stage_ap[s_], stbf_hi[s_], ALU.subtract),
                  reads=[stage_b[si], stbf_b[si]], writes=[stbf_b[si]])
                O("gpsimd", lambda e, g_=g, m_=m, s_=si: e.dma_start(
                    out=s_hidt_hi[m_, :, g_ * GROUP:(g_ + 1) * GROUP],
                    in_=stbf_hi[s_]),
                  reads=[stbf_b[si]], writes=[b_shidt], inc=16, sem=f"st{si}")
                O("gpsimd", lambda e, g_=g, m_=m, s_=si: e.dma_start(
                    out=s_hidt_lo[m_, :, g_ * GROUP:(g_ + 1) * GROUP],
                    in_=stbf_lo[s_]),
                  reads=[stbf_b[si]], writes=[b_shidt], inc=16, sem=f"st{si}")

        w2_hi, w2_lo = wpair(0, 32768, KF)
        b_w2 = Buf(barrier())
        stgD_b = [Buf(barrier()), Buf(barrier())]
        stream_split(w2_d, FF, D, 0, b_w2, 32, stgD_b)

        avD = AV()
        hidt_hi = [None, None]; hidt_lo = [None, None]; hidt_b = [None, None]
        hidt_hi[0], hidt_lo[0], hidt_b[0] = avD.take_pair("hidt0", KF * 128, KF, 128)
        hidt_hi[1], hidt_lo[1], hidt_b[1] = avD.take_pair("hidt1", KF * 128, KF, 128)
        xbD, b_xbD = avD.take("xbD", D)
        rbD, b_rbD = avD.take("rbD", D)
        s2D, b_s2D = avD.take("s2D", D)
        x1D, b_x1D = avD.take("x1D", D)
        scrD, b_scrD = avD.take("scrD", D)
        b2b, b_b2b = avD.take("b2b", D)
        x1TD, b_x1TD = avD.take("x1TD", D)
        x1TDv = x1TD.rearrange("p (k n) -> p k n", k=KD)
        O("sync", lambda e: e.dma_start(
            out=b2b, in_=b2_d[None, :].broadcast_to([128, D])),
          writes=[b_b2b], inc=16, sem="b2b")

        def hidt_load(t):
            j = t % 2
            emits = []
            for (plane_d, dstv) in ((s_hidt_hi, hidt_hi[j]),
                                    (s_hidt_lo, hidt_lo[j])):
                for q4 in range(4):
                    emits.append(lambda e, t_=t, q_=q4, d=dstv, s=plane_d:
                                 e.dma_start(
                        out=d[:, q_ * 8:(q_ + 1) * 8, :],
                        in_=s[q_ * 8:(q_ + 1) * 8, :, t_ * 128:(t_ + 1) * 128]
                        .rearrange("m p n -> p m n")))
            dma_fill("sync", f"hidt{j}", hidt_b[j], emits,
                     extra_reads=[b_shidt])

        def scr_load(t):
            O("sync", lambda e, t_=t: e.dma_start(
                out=xbD, in_=x_d[t_ * T_TILE:(t_ + 1) * T_TILE, :]),
              reads=[b_x_d], writes=[b_xbD], inc=16, sem="xbD")
            O("sync", lambda e, t_=t: e.dma_start(out=rbD, in_=s_r[t_, :, :]),
              reads=[b_sr], writes=[b_rbD], inc=16, sem="rbD")
            O("vector", lambda e: e.tensor_tensor(scrD, rbD, xbD, ALU.add),
              reads=[b_rbD, b_xbD], writes=[b_scrD])
            O("vector", lambda e: e.tensor_tensor(scrD, scrD, b2b, ALU.add),
              reads=[b_scrD, b_b2b], writes=[b_scrD])

        def phd_groups(t):
            j = t % 2
            bank0 = (2 * t) % 6
            for n in range(2):
                bank = (bank0 + n) % 6
                nwin = slice(n * 512, (n + 1) * 512)
                mms = []
                for k in range(KF):
                    mms.append((hidt_hi[j][:, k, :], w2_hi[:, k, nwin],
                                [hidt_b[j], b_w2]))
                    mms.append((hidt_hi[j][:, k, :], w2_lo[:, k, nwin],
                                [hidt_b[j], b_w2]))
                for k in range(KF):
                    mms.append((hidt_lo[j][:, k, :], w2_hi[:, k, nwin],
                                [hidt_b[j], b_w2]))
                mm_group(bank, psum[bank][:, :], mms)

        def phd_evac(t):
            bank0 = (2 * t) % 6
            for n in range(2):
                bank = (bank0 + n) % 6
                sl = slice(n * 512, (n + 1) * 512)
                O("vector", lambda e, b_=bank, s_=sl: e.tensor_tensor(
                    s2D[:, s_], psum[b_][:, :], scrD[:, s_], ALU.add),
                  reads=[pb[bank], b_scrD], writes=[b_s2D])
            if t + 1 < N_TILES:
                scr_load(t + 1)
            layernorm(s2D, b_s2D, x1D, b_x1D, scrD, b_scrD)
            O("vector", lambda e: e.tensor_reduce(
                stats[:, 7:8], x1D, axis=AX, op=ALU.max,
                apply_absolute_value=True),
              reads=[b_x1D], writes=[b_stats])
            O("vector", lambda e: e.tensor_tensor(
                amax1[:, :], amax1[:, :], stats[:, 7:8], ALU.max),
              reads=[b_amax1, b_stats], writes=[b_amax1])
            O("gpsimd", lambda e, t_=t: e.dma_start(out=s_x1[t_, :, :], in_=x1D),
              reads=[b_x1D], writes=[b_sx1], inc=16, sem="sx1")
            transpose8(x1D, b_x1D, x1TDv, b_x1TD,
                       pairs=(3, (t + 2) % 3, t % 3))
            gD, jjD = t // 4, t % 4
            O("gpsimd", lambda e, g_=gD, j_=jjD: e.dma_start(
                out=s_x1t[g_, :, :, j_ * 128:(j_ + 1) * 128], in_=x1TDv[:, :, :]),
              reads=[b_x1TD], writes=[b_sx1], inc=16, sem="sx1t")

        hidt_load(0)
        scr_load(0)
        phd_groups(0)
        for t in range(N_TILES):
            if t + 1 < N_TILES:
                hidt_load(t + 1)
                phd_groups(t + 1)
            phd_evac(t)

        def all_reduce_amax(amax_t, b_amax, cci, cco, b_cci, b_cco,
                            sc, b_sc, isc, b_isc):
            O("gpsimd", lambda e: e.tensor_reduce(
                ccrow[0:1, 0:1], amax_t[:, :], axis=AXALL, op=ALU.max),
              reads=[b_amax], writes=[b_ccrow])
            O("gpsimd", lambda e: e.dma_start(out=cci[:, :], in_=ccrow[:, :]),
              reads=[b_ccrow], writes=[b_cci], inc=16, sem="ccio")
            O("gpsimd", lambda e: e.collective_compute(
                "AllReduce", ALU.max, replica_groups=[list(range(N_CORES))],
                ins=[cci], outs=[cco]),
              reads=[b_cci], writes=[b_cco], sem="cc")
            O("gpsimd", lambda e: e.dma_start(
                out=gscal[:, :], in_=cco[0:1, 0:1].broadcast_to([128, 1])),
              reads=[b_cco], writes=[b_gscal], inc=16, sem="gsc")
            O("vector", lambda e: e.tensor_scalar(
                gscal[:, :], gscal[:, :], 1e-8, None, ALU.max),
              reads=[b_gscal], writes=[b_gscal])
            O("vector", lambda e, s=sc: e.reciprocal(s[:, :], gscal[:, :]),
              reads=[b_gscal], writes=[b_sc])
            O("vector", lambda e, s=sc: e.tensor_scalar(
                s[:, :], s[:, :], 32767.0, None, ALU.mult),
              reads=[b_sc], writes=[b_sc])
            O("vector", lambda e, s=sc, i_=isc: e.reciprocal(i_[:, :], s[:, :]),
              reads=[b_sc], writes=[b_isc])

        all_reduce_amax(amax1, b_amax1, cc1_in, cc1_out, b_cc1i, b_cc1o,
                        sc1, b_sc1, isc1, b_isc1)

        w_in_s = wfp.rearrange("p (k n) -> p k n", k=KD)
        b_winf32 = Buf(barrier())
        dma_fill("sync", "warena", b_winf32, [
            (lambda e: e.dma_start(
                out=w_in_s[:, 0:4, :],
                in_=win_d[0:512, :].rearrange("(k p) n -> p k n", p=128))),
            (lambda e: e.dma_start(
                out=w_in_s[:, 4:8, :],
                in_=win_d[512:1024, :].rearrange("(k p) n -> p k n", p=128)))])

        avE = AV()
        NW = 3
        wtm_ap = []; wtm_b = []
        wib_ap = []; wib_b = []
        for i_ in range(NW):
            a_, b_ = avE.take(f"wtm{i_}", D)
            wtm_ap.append(a_); wtm_b.append(b_)
        for i_ in range(NW):
            a_, b_ = avE.take(f"wib{i_}", 512)
            wib_ap.append(a_); wib_b.append(b_)

        def wq_tps_in(m):
            wtm = wtm_ap[m % NW]
            for k in range(KD):
                bank = 6 + (k % 2)
                O("tensor", lambda e, k_=k, m_=m, t=psum[bank]: e.transpose(
                    t[:, 0:128], w_in_s[:, k_, m_ * 128:(m_ + 1) * 128],
                    ident[:, :]),
                  reads=[b_winf32, b_ident], writes=[pb[bank]])
                O("scalar", lambda e, k_=k, t=psum[bank], d=wtm: e.copy(
                    d[:, k_ * 128:(k_ + 1) * 128], t[:, 0:128]),
                  reads=[pb[bank]], writes=[wtm_b[m % NW]])

        def wq_chain_store(m):
            wtm, b_wtm = wtm_ap[m % NW], wtm_b[m % NW]
            wib = wib_ap[m % NW].bitcast(BF16)
            O("vector", lambda e, d=wtm: e.tensor_reduce(
                wamax[:, :], d, axis=AX, op=ALU.max, apply_absolute_value=True),
              reads=[b_wtm], writes=[b_wamax])
            O("vector", lambda e: e.tensor_scalar(
                wamax[:, :], wamax[:, :], 1e-8, None, ALU.max),
              reads=[b_wamax], writes=[b_wamax])
            O("vector", lambda e: e.reciprocal(wsc[:, :], wamax[:, :]),
              reads=[b_wamax], writes=[b_wsc])
            O("vector", lambda e: e.tensor_scalar(
                wsc[:, :], wsc[:, :], 127.0, None, ALU.mult),
              reads=[b_wsc], writes=[b_wsc])
            O("vector", lambda e: e.reciprocal(wisc[:, :], wsc[:, :]),
              reads=[b_wsc], writes=[b_wisc])
            O("vector", lambda e, m_=m: e.tensor_copy(
                iscWi[:, m_:m_ + 1], wisc[:, :]),
              reads=[b_wisc], writes=[b_iscWi])
            O("vector", lambda e, d=wtm: e.tensor_scalar(
                d, d, wsc[:, :], MAGIC, ALU.mult, ALU.add),
              reads=[b_wtm, b_wsc], writes=[b_wtm])
            O("vector", lambda e, d=wtm: e.tensor_scalar(
                d, d, MAGIC, 127.0, ALU.subtract, ALU.min),
              reads=[b_wtm], writes=[b_wtm])
            O("vector", lambda e, d=wtm: e.tensor_scalar(
                d, d, -128.0, None, ALU.max),
              reads=[b_wtm], writes=[b_wtm])
            for k in range(KD):
                bank = 4 + (k % 2)
                O("tensor", lambda e, k_=k, t=psum[bank], d=wtm: e.transpose(
                    t[:, 0:128], d[:, k_ * 128:(k_ + 1) * 128], ident[:, :]),
                  reads=[b_wtm, b_ident], writes=[pb[bank]])
                O("vector", lambda e, k_=k, t=psum[bank], d=wib: e.tensor_copy(
                    d[:, k_ * 128:(k_ + 1) * 128], t[:, 0:128]),
                  reads=[pb[bank]], writes=[wib_b[m % NW]])
            O("gpsimd", lambda e, m_=m, d=wib: e.dma_start(
                out=s_wint.rearrange("k p n -> p k n")[
                    :, :, m_ * 128:(m_ + 1) * 128],
                in_=d.rearrange("p (k n) -> p k n", k=KD)),
              reads=[wib_b[m % NW]], writes=[b_swint[m % NW]],
              inc=16, sem=f"swint{m % NW}")

        for m in range(KF + 1):
            if m < KF:
                wq_tps_in(m)
            if m > 0:
                wq_chain_store(m - 1)

        winq = wbf[:, 0:32768].rearrange("p (k n) -> p k n", k=KD)
        b_winq = Buf(barrier())
        O("sync", lambda e: e.dma_start(
            out=winq, in_=s_wint.rearrange("k p n -> p k n")),
          reads=[b_swint[0], b_swint[1], b_swint[2]], writes=[b_winq], inc=16, sem="warena")
        O("vector", lambda e: e.tensor_scalar(
            cmbE[:, :], iscWi[:, :], isc1[:, 0:1], None, ALU.mult),
          reads=[b_iscWi, b_isc1], writes=[b_cmbE])

        avE2 = AV()
        _x1t_ap, _x1t_b = avE2.take("x1T", KD * GROUP)
        x1T_ap = [_x1t_ap, _x1t_ap]; x1T_b = [_x1t_b, _x1t_b]
        xsT_hi = [None, None]; xsT_lo = [None, None]; xsT_b = [None, None]
        xsT_hi[0], xsT_lo[0], xsT_b[0] = avE2.take_pair(
            "xsT0", KD * GROUP, KD, GROUP)
        xsT_hi[1], xsT_lo[1], xsT_b[1] = avE2.take_pair(
            "xsT1", KD * GROUP, KD, GROUP)
        stE_ap = []; stE_b = []
        for i in range(4):
            a_, b_ = avE2.take(f"stE{i}", 512)
            stE_ap.append(a_); stE_b.append(b_)
        stgE_ap = [None, None]; stgE_b = [None, None]
        stgE_ap[0], stgE_b[0] = avE2.take("stgE0", D)
        stgE_ap[1], stgE_b[1] = avE2.take("stgE1", D)

        def prep_group(g):
            jg = g % 2
            x1t, b_x1t = x1T_ap[jg], x1T_b[jg]
            O("sync", lambda e, g_=g, d=x1t.rearrange("p (k n) -> p k n", k=KD):
              e.dma_start(out=d, in_=s_x1t[g_, :, :, :]),
              reads=[b_sx1], writes=[b_x1t], inc=16, sem=f"x1T{jg}")
            O("vector", lambda e, s=x1t: e.tensor_scalar(
                s, s, sc1[:, :], MAGIC, ALU.mult, ALU.add),
              reads=[b_x1t, b_sc1], writes=[b_x1t])
            O("vector", lambda e, s=x1t: e.tensor_scalar(
                s, s, MAGIC, 32767.0, ALU.subtract, ALU.min),
              reads=[b_x1t], writes=[b_x1t])
            O("vector", lambda e, s=x1t: e.tensor_scalar(
                s, s, -32768.0, None, ALU.max),
              reads=[b_x1t], writes=[b_x1t])
            O("vector", lambda e, s=x1t, h=xsT_hi[jg]: e.tensor_copy(
                h.rearrange("p k n -> p (k n)"), s),
              reads=[b_x1t], writes=[xsT_b[jg]])
            O("vector", lambda e, s=x1t, h=xsT_hi[jg], l=xsT_lo[jg]:
              e.tensor_tensor(
                l.rearrange("p k n -> p (k n)"), s,
                h.rearrange("p k n -> p (k n)"), ALU.subtract),
              reads=[b_x1t, xsT_b[jg]], writes=[xsT_b[jg]])

        woq = wbf[:, 32768:65536].rearrange("p (k n) -> p k n", k=KF)
        b_woq = Buf(barrier())
        stgE2_b = [Buf(barrier()), Buf(barrier())]
        wo_band_bufs = [Buf() for _ in range(KF)]
        b_wamE = [Buf(barrier()), Buf(barrier())]
        b_wscE = [Buf(barrier()), Buf(barrier())]
        b_wiscE = [Buf(barrier()), Buf(barrier())]
        wamE = [wamax, wamax2]
        wscE = [wsc[:, :], wsc2[:, :]]
        wiscE = [wisc[:, :], wisc2[:, :]]

        def wout_band(k):
            j = k % 2
            stg, b_stg = stgE_ap[j], stgE_b[j]
            wam, bam = wamE[j], b_wamE[j]
            ws_, bws = wscE[j], b_wscE[j]
            wi_, bwi = wiscE[j], b_wiscE[j]
            bb = wo_band_bufs[k]
            O("sync", lambda e, k_=k, d=stg: e.dma_start(
                out=d, in_=wout_d[k_ * 128:(k_ + 1) * 128, :]),
              writes=[b_stg], inc=16, sem=f"stgE{j}")
            O("vector", lambda e, d=stg, w=wam: e.tensor_reduce(
                w[:, :], d, axis=AX, op=ALU.max, apply_absolute_value=True),
              reads=[b_stg], writes=[bam])
            O("vector", lambda e, w=wam: e.tensor_scalar(
                w[:, :], w[:, :], 1e-8, None, ALU.max),
              reads=[bam], writes=[bam])
            O("vector", lambda e, w=wam, s=ws_: e.reciprocal(s, w[:, :]),
              reads=[bam], writes=[bws])
            O("vector", lambda e, s=ws_: e.tensor_scalar(
                s, s, 127.0, None, ALU.mult),
              reads=[bws], writes=[bws])
            O("vector", lambda e, s=ws_, i_=wi_: e.reciprocal(i_, s),
              reads=[bws], writes=[bwi])
            O("vector", lambda e, k_=k, i_=wi_: e.tensor_copy(
                iscWo[:, k_:k_ + 1], i_),
              reads=[bwi], writes=[b_iscWo])
            O("vector", lambda e, d=stg, s=ws_: e.tensor_scalar(
                d, d, s, MAGIC, ALU.mult, ALU.add),
              reads=[b_stg, bws], writes=[b_stg])
            O("vector", lambda e, d=stg: e.tensor_scalar(
                d, d, MAGIC, 127.0, ALU.subtract, ALU.min),
              reads=[b_stg], writes=[b_stg])
            O("vector", lambda e, d=stg: e.tensor_scalar(
                d, d, -128.0, None, ALU.max),
              reads=[b_stg], writes=[b_stg])
            O("vector", lambda e, k_=k, d=stg: e.tensor_copy(
                woq[:, k_, :], d),
              reads=[b_stg], writes=[bb])

        prep_group(0)
        for g in range(N_GROUPS):
            jg = g % 2
            for kb in range(4):
                wout_band(g * 4 + kb)
            for m in range(KF):
                bank = m % 4
                nwin = slice(m * 128, (m + 1) * 128)
                mms = []
                for k in range(KD):
                    mms.append((winq[:, k, nwin], xsT_hi[jg][:, k, :],
                                [b_winq, xsT_b[jg]]))
                for k in range(KD):
                    mms.append((winq[:, k, nwin], xsT_lo[jg][:, k, :],
                                [b_winq, xsT_b[jg]]))
                mm_group(bank, psum[bank][:, :], mms)
                si = m % 4
                O("scalar", lambda e, b_=bank, m_=m, s_=si: e.activation(
                    stE_ap[s_], psum[b_][:, :], AF.Gelu,
                    bias=binf[:, m_:m_ + 1], scale=cmbE[:, m_:m_ + 1]),
                  reads=[pb[bank], b_binf, b_cmbE], writes=[stE_b[si]])
                O("vector", lambda e, s_=si: e.tensor_reduce(
                    stats[:, 7:8], stE_ap[s_], axis=AX, op=ALU.max,
                    apply_absolute_value=True),
                  reads=[stE_b[si]], writes=[b_stats])
                O("vector", lambda e: e.tensor_tensor(
                    amax2[:, :], amax2[:, :], stats[:, 7:8], ALU.max),
                  reads=[b_amax2, b_stats], writes=[b_amax2])
                O("gpsimd", lambda e, g_=g, m_=m, s_=si: e.dma_start(
                    out=s_h2at[m_, :, g_ * GROUP:(g_ + 1) * GROUP],
                    in_=stE_ap[s_]),
                  reads=[stE_b[si]], writes=[b_sh2at], inc=16, sem=f"st{si}")
                if m == 0 and g + 1 < N_GROUPS:
                    prep_group(g + 1)

        b_woq.w = b_woq.w + [t for bb in wo_band_bufs for t in bb.w]
        b_woq.r = []
        avF = AV()
        RF = []
        for i in range(3):
            offF = avF.off
            a_, b_ = avF.take(f"hF{i}", KF * 128)
            hiF = abf[:, 2 * offF: 2 * offF + KF * 128].rearrange(
                "p (k n) -> p k n", k=KF)
            loF = abf[:, 2 * offF + KF * 128: 2 * offF + 2 * KF * 128].rearrange(
                "p (k n) -> p k n", k=KF)
            RF.append([a_, hiF, loF, b_])

        def loadF(t):
            j = (t + 1) % 3
            src_ap, src_b = RF[j][0], RF[j][3]
            h2a = src_ap.rearrange("p (k n) -> p k n", k=KF)
            emitsF = []
            for q4 in range(4):
                emitsF.append(lambda e, t_=t, q_=q4, d=h2a: e.dma_start(
                    out=d[:, q_ * 8:(q_ + 1) * 8, :],
                    in_=s_h2at[q_ * 8:(q_ + 1) * 8, :, t_ * 128:(t_ + 1) * 128]
                    .rearrange("m p n -> p m n")))
            dma_fill("sync", f"h2a{j}", src_b, emitsF,
                     extra_reads=[b_sh2at])

        loadF(0)
        all_reduce_amax(amax2, b_amax2, cc2_in, cc2_out, b_cc2i, b_cc2o,
                        sc2, b_sc2, isc2, b_isc2)
        O("vector", lambda e: e.tensor_scalar(
            cmbF[:, :], iscWo[:, :], isc2[:, 0:1], None, ALU.mult),
          reads=[b_iscWo, b_isc2], writes=[b_cmbF])

        x1F, b_x1F = avF.take("x1F", D)
        s2F, b_s2F = avF.take("s2F", D)
        outF, b_outF = avF.take("outF", D)
        scrF, b_scrF = avF.take("scrF", D)
        boutb, b_boutb = avF.take("boutb", D)
        O("sync", lambda e: e.dma_start(
            out=boutb, in_=bout_d[None, :].broadcast_to([128, D])),
          writes=[b_boutb], inc=16, sem="boutb")

        cmbFb = cmbF.ap() if hasattr(cmbF, "ap") else cmbF
        cmbF3 = cmbF[:, :, None].broadcast_to([128, KF, 128])

        def quantF(t):
            j = (t + 1) % 3
            src_ap, src_b = RF[j][0], RF[j][3]
            h2a = src_ap.rearrange("p (k n) -> p k n", k=KF)
            O("vector", lambda e, d=src_ap: e.tensor_scalar(
                d, d, sc2[:, :], MAGIC, ALU.mult, ALU.add),
              reads=[src_b, b_sc2], writes=[src_b])
            O("vector", lambda e, d=src_ap: e.tensor_scalar(
                d, d, MAGIC, 32767.0, ALU.subtract, ALU.min),
              reads=[src_b], writes=[src_b])
            O("vector", lambda e, d=src_ap: e.tensor_scalar(
                d, d, -32768.0, None, ALU.max),
              reads=[src_b], writes=[src_b])
            O("vector", lambda e, d=h2a: e.tensor_tensor(
                d, d, cmbF3, ALU.mult),
              reads=[src_b, b_cmbF], writes=[src_b])

        def splitF(t):
            src_ap, src_b = RF[(t + 1) % 3][0], RF[(t + 1) % 3][3]
            _, hiF, loF, dst_b = RF[t % 3]
            O("vector", lambda e, d=src_ap, h=hiF: e.tensor_copy(
                h.rearrange("p k n -> p (k n)"), d),
              reads=[src_b], writes=[dst_b])
            O("vector", lambda e, d=src_ap, h=hiF, l=loF: e.tensor_tensor(
                l.rearrange("p k n -> p (k n)"), d,
                h.rearrange("p k n -> p (k n)"), ALU.subtract),
              reads=[src_b, dst_b], writes=[dst_b])

        quantF(0)
        splitF(0)
        for t in range(N_TILES):
            j = t % 2
            bank0 = (2 * t) % 6
            O("sync", lambda e, t_=t: e.dma_start(out=x1F, in_=s_x1[t_, :, :]),
              reads=[b_sx1], writes=[b_x1F], inc=16, sem="x1F")
            _, mhi, mlo, b_m = RF[t % 3]
            for n in range(2):
                bank = (bank0 + n) % 6
                nwin = slice(n * 512, (n + 1) * 512)
                mms = []
                for k in range(KF):
                    mms.append((mhi[:, k, :], woq[:, k, nwin],
                                [b_m, b_woq]))
                for k in range(KF):
                    mms.append((mlo[:, k, :], woq[:, k, nwin],
                                [b_m, b_woq]))
                mm_group(bank, psum[bank][:, :], mms)
            O("vector", lambda e: e.tensor_tensor(scrF, boutb, x1F, ALU.add),
              reads=[b_boutb, b_x1F], writes=[b_scrF])
            if t + 1 < N_TILES:
                loadF(t + 1)
                quantF(t + 1)
                splitF(t + 1)
            for n in range(2):
                bank = (bank0 + n) % 6
                sl = slice(n * 512, (n + 1) * 512)
                O("vector", lambda e, b_=bank, s_=sl: e.tensor_tensor(
                    s2F[:, s_], psum[b_][:, :], scrF[:, s_], ALU.add),
                  reads=[pb[bank], b_scrF], writes=[b_s2F])
            layernorm(s2F, b_s2F, outF, b_outF, scrF, b_scrF)
            O("gpsimd", lambda e, t_=t: e.dma_start(
                out=out_d[t_ * T_TILE:(t_ + 1) * T_TILE, :], in_=outF),
              reads=[b_outF], writes=[b_out_d], inc=16, sem="out")

        store_keys = [k for k in sch.counts
                      if k not in ENG and k.startswith(("sr", "sh2t", "st", "sx1",
                                                        "out", "swint"))]
        final_totals = [(k, sch.counts[k]) for k in store_keys]

        sems = {k: st.enter_context(nc.semaphore(f"sem_{k}"))
                for k in sch.counts}
        blk = st.enter_context(nc.Block())

        def make_body(ename):
            def body(eng):
                for waits, emit, inc, key in sch.ops[ename]:
                    for k2, v in waits.items():
                        eng.wait_ge(sems[k2], v)
                    inst = emit(eng)
                    inst.then_inc(sems[key], inc)
                if ename == "gpsimd":
                    for k, v in final_totals:
                        eng.wait_ge(sems[k], v)
            return body

        blk.sync(make_body("sync"))
        blk.tensor(make_body("tensor"))
        blk.vector(make_body("vector"))
        blk.scalar(make_body("scalar"))
        blk.gpsimd(make_body("gpsimd"))
    return nc


_MASK = np.triu(np.ones((128, 128), dtype=np.float32))
_IDENT = np.eye(128, dtype=np.float32)


def kernel(**inputs):
    f32 = np.float32

    def c(name):
        return np.ascontiguousarray(np.asarray(inputs[name], dtype=f32))

    x = c("L0_attn_x")
    xs = x.reshape(N_CORES, SEQ_PER_CORE * S, D)
    common = {
        "wq": c("L0_attn_wq"), "wk": c("L0_attn_wk"), "wv": c("L0_attn_wv"),
        "wo": c("L0_attn_wo"), "proj": c("L0_attn_proj"),
        "w1": c("L0_attn_w1"), "b1": c("L0_attn_b1"),
        "w2": c("L0_attn_w2"), "b2": c("L0_attn_b2"),
        "g1": c("L0_attn_g1"), "beta1": c("L0_attn_beta1"),
        "g2": c("L0_attn_g2"), "beta2": c("L0_attn_beta2"),
        "W_in": c("L0_ffn_W_in"), "b_in": c("L0_ffn_b_in"),
        "W_out": c("L0_ffn_W_out"), "b_out": c("L0_ffn_b_out"),
        "mask": _MASK, "ident": _IDENT,
    }
    in_maps = [dict(common, x=np.ascontiguousarray(xs[i])) for i in range(N_CORES)]
    nc = build_program()
    res = run_bass_kernel_spmd(nc, in_maps, core_ids=list(range(N_CORES)))
    out = np.stack([np.asarray(r["out"]) for r in res.results], axis=0)
    return out.reshape(16, S, D).astype(np.float32)


# version 31 (final)
# speedup vs baseline: 1.0123x; 1.0037x over previous
import contextlib
import numpy as np
import concourse.bass as bass
import concourse.mybir as mybir
from concourse.bass_utils import run_bass_kernel_spmd

F32 = mybir.dt.float32
BF16 = mybir.dt.bfloat16
AF = mybir.ActivationFunctionType
ALU = mybir.AluOpType
AX = mybir.AxisListType.X
AXALL = mybir.AxisListType.XYZWC
MAGIC = 1.5 * 2.0 ** 23

ENG = ("sync", "tensor", "vector", "scalar", "gpsimd")

N_CORES = 8
SEQ_PER_CORE = 2
S = 2048
D = 1024
H = 16
DH = 64
FF = 4096
T_TILE = 128
N_TILES = (SEQ_PER_CORE * S) // T_TILE
TILES_PER_SEQ = S // T_TILE
KD = D // 128
KF = FF // 128
GROUP = 512
N_GROUPS = (SEQ_PER_CORE * S) // GROUP
INV_D = 1.0 / D
EPS = 1e-5


class Buf:
    __slots__ = ("w", "r")

    def __init__(self, w=None):
        self.w = list(w) if w else []
        self.r = []


class Sched:

    def __init__(self):
        self.ops = {e: [] for e in ENG}
        self.counts = {}
        self.last_wait = {e: {} for e in ENG}

    def op(self, eng, emit, reads=(), writes=(), inc=1, sem=None):
        key = sem if sem is not None else eng
        deps = []
        for b in reads:
            deps.extend(b.w)
        for b in writes:
            deps.extend(b.w)
            deps.extend(b.r)
        waits = {}
        lw = self.last_wait[eng]
        for (k2, v) in deps:
            if v > lw.get(k2, 0) and v > waits.get(k2, 0):
                waits[k2] = v
        for k2, v in waits.items():
            lw[k2] = v
        self.counts[key] = self.counts.get(key, 0) + inc
        tok = (key, self.counts[key])
        self.ops[eng].append((waits, emit, inc, key))
        for b in reads:
            b.r.append(tok)
        for b in writes:
            b.w = [tok]
            b.r = []
        return tok

    def snapshot(self):
        return [(k, v) for k, v in self.counts.items() if v > 0 and k in ENG]


def build_program():
    nc = bass.Bass("TRN2", target_bir_lowering=False)

    NT = N_TILES * T_TILE
    x_d = nc.dram_tensor("x", [NT, D], F32, kind="ExternalInput").ap()
    wq_d = nc.dram_tensor("wq", [D, D], F32, kind="ExternalInput").ap()
    wk_d = nc.dram_tensor("wk", [D, D], F32, kind="ExternalInput").ap()
    wv_d = nc.dram_tensor("wv", [D, D], F32, kind="ExternalInput").ap()
    wo_d = nc.dram_tensor("wo", [D, D], F32, kind="ExternalInput").ap()
    proj_d = nc.dram_tensor("proj", [DH, DH], F32, kind="ExternalInput").ap()
    w1_d = nc.dram_tensor("w1", [D, FF], F32, kind="ExternalInput").ap()
    b1_d = nc.dram_tensor("b1", [FF], F32, kind="ExternalInput").ap()
    w2_d = nc.dram_tensor("w2", [FF, D], F32, kind="ExternalInput").ap()
    b2_d = nc.dram_tensor("b2", [D], F32, kind="ExternalInput").ap()
    g1_d = nc.dram_tensor("g1", [D], F32, kind="ExternalInput").ap()
    be1_d = nc.dram_tensor("beta1", [D], F32, kind="ExternalInput").ap()
    g2_d = nc.dram_tensor("g2", [D], F32, kind="ExternalInput").ap()
    be2_d = nc.dram_tensor("beta2", [D], F32, kind="ExternalInput").ap()
    win_d = nc.dram_tensor("W_in", [D, FF], F32, kind="ExternalInput").ap()
    bin_d = nc.dram_tensor("b_in", [FF], F32, kind="ExternalInput").ap()
    wout_d = nc.dram_tensor("W_out", [FF, D], F32, kind="ExternalInput").ap()
    bout_d = nc.dram_tensor("b_out", [D], F32, kind="ExternalInput").ap()
    mask_d = nc.dram_tensor("mask", [128, 128], F32, kind="ExternalInput").ap()
    ident_d = nc.dram_tensor("ident", [128, 128], F32, kind="ExternalInput").ap()
    out_d = nc.dram_tensor("out", [NT, D], F32, kind="ExternalOutput").ap()

    s_r = nc.dram_tensor("s_r", [N_TILES, T_TILE, D], F32).ap()
    s_h2t_hi = nc.dram_tensor("s_h2t_hi", [N_GROUPS, 128, KD, GROUP], BF16).ap()
    s_h2t_lo = nc.dram_tensor("s_h2t_lo", [N_GROUPS, 128, KD, GROUP], BF16).ap()
    s_hidt_hi = nc.dram_tensor("s_hidt_hi", [KF, 128, NT], BF16).ap()
    s_hidt_lo = nc.dram_tensor("s_hidt_lo", [KF, 128, NT], BF16).ap()
    s_x1 = nc.dram_tensor("s_x1", [N_TILES, T_TILE, D], F32).ap()
    s_x1t = nc.dram_tensor("s_x1t", [N_GROUPS, 128, KD, GROUP], F32).ap()
    s_h2at = nc.dram_tensor("s_h2at", [KF, 128, NT], F32).ap()
    s_wint = nc.dram_tensor("s_wint", [KD, 128, FF], BF16).ap()
    cc1_in = nc.dram_tensor("cc1_in", [1, 16], F32).ap()
    cc1_out = nc.dram_tensor("cc1_out", [1, 16], F32).ap()
    cc2_in = nc.dram_tensor("cc2_in", [1, 16], F32).ap()
    cc2_out = nc.dram_tensor("cc2_out", [1, 16], F32).ap()

    st = contextlib.ExitStack()
    with st:
        def sbt(name, shape, dt=F32):
            return st.enter_context(nc.sbuf_tensor(name, shape, dt))

        AR_ELEMS = 17920
        warena = sbt("warena", [128, 32768])
        aarena = sbt("aarena", [128, AR_ELEMS])
        ident = sbt("ident_s", [128, 128])
        mask = sbt("mask_s", [128, 128])
        b1f = sbt("b1f", [128, KF])
        binf = sbt("binf", [128, KF])
        iscWi = sbt("iscWi", [128, KF])
        cmbE = sbt("cmbE", [128, KF])
        iscWo = sbt("iscWo", [128, KF])
        cmbF = sbt("cmbF", [128, KF])
        kvb = sbt("kvb", [128, H // 2, DH + 1])
        vsall = sbt("vsall", [128, H, DH + 1])
        amax1 = sbt("amax1", [128, 1])
        amax2 = sbt("amax2", [128, 1])
        ccrow = sbt("ccrow", [1, 16])
        sc1 = sbt("sc1", [128, 1])
        isc1 = sbt("isc1", [128, 1])
        sc2 = sbt("sc2", [128, 1])
        isc2 = sbt("isc2", [128, 1])
        gscal = sbt("gscal", [128, 1])
        stats = sbt("stats", [128, 32])
        magicb = sbt("magicb", [128, 1])
        wamax = sbt("wamax", [128, 1])
        wsc = sbt("wsc", [128, 1])
        wisc = sbt("wisc", [128, 1])
        wamax2 = sbt("wamax2", [128, 1])
        wsc2 = sbt("wsc2", [128, 1])
        wisc2 = sbt("wisc2", [128, 1])

        psumt = [st.enter_context(nc.psum_tensor(f"pb{i}", [128, 1024], F32))
                 for i in range(4)]
        psum = [psumt[i // 2].ap()[:, (i % 2) * 512:(i % 2 + 1) * 512]
                for i in range(8)]
        psum_pair = [psumt[i].ap().rearrange("p (b n) -> p b n", b=2)
                     for i in range(4)]

        sch = Sched()
        O = sch.op
        pb = [Buf() for _ in range(8)]

        wfp = warena.ap()
        wbf = warena.ap().bitcast(BF16)
        abf = aarena.ap().bitcast(BF16)

        b_ident = Buf(); b_mask = Buf(); b_b1f = Buf(); b_binf = Buf()
        b_iscWi = Buf(); b_cmbE = Buf(); b_iscWo = Buf(); b_cmbF = Buf()
        b_kvh = [Buf() for _ in range(H)]
        b_vsall = Buf()
        b_amax1 = Buf(); b_amax2 = Buf(); b_ccrow = Buf()
        b_sc1 = Buf(); b_isc1 = Buf(); b_sc2 = Buf(); b_isc2 = Buf(); b_gscal = Buf()
        b_stats = Buf(); b_stats2 = Buf(); b_magicb = Buf()
        b_wamax = Buf(); b_wsc = Buf(); b_wisc = Buf()
        b_x_d = Buf(); b_out_d = Buf()
        b_sr = Buf(); b_sh2t = Buf(); b_shidt = Buf(); b_sx1 = Buf()
        b_sh2at = Buf(); b_swint = [Buf(), Buf(), Buf()]
        b_cc1i = Buf(); b_cc1o = Buf(); b_cc2i = Buf(); b_cc2o = Buf()

        def barrier():
            return sch.snapshot()

        class AV:
            def __init__(self):
                self.off = 0

            def take(self, name, n_elems):
                lo = self.off
                self.off += n_elems
                assert self.off <= AR_ELEMS, f"aarena overflow at {name}: {self.off}"
                return aarena.ap()[:, lo:lo + n_elems], Buf(barrier())

            def take_pair(self, name, n_elems, kc, npc):
                lo_off = self.off
                self.off += n_elems
                assert self.off <= AR_ELEMS, f"aarena overflow at {name}: {self.off}"
                hi = abf[:, 2 * lo_off: 2 * lo_off + n_elems]
                lo = abf[:, 2 * lo_off + n_elems: 2 * lo_off + 2 * n_elems]
                if kc is not None:
                    hi = hi.rearrange("p (k n) -> p k n", k=kc)
                    lo = lo.rearrange("p (k n) -> p k n", k=kc)
                return hi, lo, Buf(barrier())

        def wpair(base, size, kc):
            hi = wbf[:, 2 * base: 2 * base + size].rearrange("p (k n) -> p k n", k=kc)
            lo = wbf[:, 2 * base + size: 2 * base + 2 * size].rearrange(
                "p (k n) -> p k n", k=kc)
            return hi, lo

        O("sync", lambda e: e.dma_start(out=ident[:, :], in_=ident_d[:, :]),
          writes=[b_ident], inc=16, sem="const")
        O("sync", lambda e: e.dma_start(out=mask[:, :], in_=mask_d[:, :]),
          writes=[b_mask], inc=16, sem="const")

        avP = AV()
        g1b, b_g1b = avP.take("g1b", D)
        be1b, b_be1b = avP.take("be1b", D)
        g2b, b_g2b = avP.take("g2b", D)
        be2b, b_be2b = avP.take("be2b", D)
        for (dst, bf, src) in ((g1b, b_g1b, g1_d), (be1b, b_be1b, be1_d),
                               (g2b, b_g2b, g2_d), (be2b, b_be2b, be2_d)):
            O("sync", lambda e, d=dst, s=src: e.dma_start(
                out=d, in_=s[None, :].broadcast_to([128, D])),
              writes=[bf], inc=16, sem="const")

        xb_ap = [None, None]; xb_b = [None, None]
        xb_ap[0], xb_b[0] = avP.take("xb0", D)
        xb_ap[1], xb_b[1] = avP.take("xb1", D)
        scrA, b_scrA = avP.take("scrA", D)
        xn, b_xn = avP.take("xn", D)
        hT_hi, hT_lo, b_hT = avP.take_pair("hT", D, KD, 128)
        qphi, b_qphi = avP.take("qphi", D)
        kphi, b_kphi = avP.take("kphi", D)
        scrE, b_scrE0 = avP.take("scrE", D)
        b_scrE = [b_scrE0, Buf(barrier())]
        qT, b_qT = avP.take("qT", D)
        kT, b_kT = avP.take("kT", D)
        attn, b_attn = avP.take("attn", D)
        aT_hi, aT_lo, b_aT = avP.take_pair("attnT", D, KD, 128)
        _h2T_off = avP.off
        h2T_hi, h2T_lo, b_h2T = avP.take_pair("h2T", D, KD, 128)
        den, b_den = avP.take("den", 8)
        smb, b_smb = avP.take("smb", 256)
        qTv = qT.rearrange("p (k n) -> p k n", k=KD)
        kTv = kT.rearrange("p (k n) -> p k n", k=KD)


        O("vector", lambda e: e.memset(vsall[:, :, DH:DH + 1], 1.0),
          writes=[b_vsall])
        O("vector", lambda e: e.memset(amax1[:, :], 0.0), writes=[b_amax1])
        O("vector", lambda e: e.memset(magicb[:, :], MAGIC), writes=[b_magicb])
        O("vector", lambda e: e.memset(amax2[:, :], 0.0), writes=[b_amax2])
        O("vector", lambda e: e.memset(ccrow[:, :], 0.0), writes=[b_ccrow])

        proj2 = qT[:, 128:192]
        b_proj2 = Buf()
        O("sync", lambda e: e.dma_start(out=proj2[0:DH, :], in_=proj_d[:, :]),
          writes=[b_proj2], inc=16, sem="const")
        O("sync", lambda e: e.dma_start(out=proj2[DH:128, :], in_=proj_d[:, :]),
          writes=[b_proj2], inc=16, sem="const")
        _ctot = [("const", sch.counts["const"])]
        for _b in (b_ident, b_mask, b_g1b, b_be1b, b_g2b, b_be2b, b_proj2):
            _b.w = list(_ctot)

        for _tt in (0, 1):
            O("sync", lambda e, t_=_tt, d=xb_ap[_tt]: e.dma_start(
                out=d, in_=x_d[t_ * T_TILE:(t_ + 1) * T_TILE, :]),
              reads=[b_x_d], writes=[xb_b[_tt]], inc=16, sem=f"xb{_tt}")

        btmp = qT[:, 0:128]
        b_btmp = Buf()
        for (srcd, dst_t, dstb) in ((b1_d, b1f, b_b1f), (bin_d, binf, b_binf)):
            srcv = srcd.rearrange("(m p) -> m p", p=128)
            O("sync", lambda e, s=srcv, d=btmp: e.dma_start(out=d[0:KF, :], in_=s),
              writes=[b_btmp], inc=16, sem="stg0")
            O("tensor", lambda e, d=btmp: e.transpose(
                psum[0][:, 0:KF], d[0:KF, :], ident[0:KF, 0:KF]),
              reads=[b_btmp, b_ident], writes=[pb[0]])
            O("vector", lambda e, d=dst_t: e.tensor_copy(d[:, :], psum[0][:, 0:KF]),
              reads=[pb[0]], writes=[dstb])

        fst_ap = [qT[:, 192:320], qT[:, 320:448]]
        fst_b = [Buf(), Buf()]
        stgw_ap = [scrA, xn]
        stgw_b = [Buf(), Buf()]

        wq_hi, wq_lo = wpair(0, 8192, KD)
        wk_hi, wk_lo = wpair(8192, 8192, KD)
        wv_hi, wv_lo = wpair(16384, 8192, KD)
        wo_hi, wo_lo = wpair(24576, 8192, KD)
        b_wq = Buf(); b_wk = Buf(); b_wv = Buf(); b_wo = Buf()

        wtmp_s = wfp[:, 16384:24576].rearrange("p (k n) -> p k n", k=KD)
        b_wtmp = Buf()

        def fold_split(src_d, hi_flat, lo_flat, b_w):
            O("sync", lambda e, s=src_d: e.dma_start(
                out=wtmp_s, in_=s.rearrange("(k p) n -> p k n", p=128)),
              writes=[b_wtmp], inc=16, sem="warena")
            for k in range(KD):
                for mp in range(KD):
                    i = k * KD + mp
                    bank = 6 + (i % 2)
                    fs, b_fs = fst_ap[i % 2], fst_b[i % 2]
                    ob = (2 * (i % 2), 2 * (i % 2) + 1)
                    O("tensor", lambda e, k_=k, mp_=mp, t=psum[bank]:
                      e.transpose(t[:, 0:128],
                                  wtmp_s[:, k_, mp_ * 128:(mp_ + 1) * 128],
                                  ident[:, :]),
                      reads=[b_wtmp, b_ident], writes=[pb[bank]])
                    O("scalar", lambda e, t=psum[bank], d=fs: e.copy(
                        d, t[:, 0:128]), reads=[pb[bank]], writes=[b_fs])
                    O("tensor", lambda e, d=fs, p=proj2, o=ob[0]: e.matmul(
                        psum[o][:, 0:DH], d[0:DH, :], p[0:DH, :],
                        start=True, stop=True),
                      reads=[b_fs, b_proj2], writes=[pb[ob[0]]])
                    O("tensor", lambda e, d=fs, p=proj2, o=ob[1]: e.matmul(
                        psum[o][:, 0:DH], d[DH:128, :], p[DH:128, :],
                        start=True, stop=True),
                      reads=[b_fs, b_proj2], writes=[pb[ob[1]]])
                    O("vector", lambda e, k_=k, mp_=mp, o=ob[0]: e.tensor_copy(
                        wtmp_s[:, k_, mp_ * 128: mp_ * 128 + DH],
                        psum[o][:, 0:DH]),
                      reads=[pb[ob[0]]], writes=[b_wtmp])
                    O("vector", lambda e, k_=k, mp_=mp, o=ob[1]: e.tensor_copy(
                        wtmp_s[:, k_, mp_ * 128 + DH:(mp_ + 1) * 128],
                        psum[o][:, 0:DH]),
                      reads=[pb[ob[1]]], writes=[b_wtmp])
            src_flat = wfp[:, 16384:24576]
            O("vector", lambda e, h=hi_flat: e.tensor_copy(h, src_flat),
              reads=[b_wtmp], writes=[b_w])
            O("vector", lambda e, h=hi_flat, l=lo_flat: e.tensor_tensor(
                l, src_flat, h, ALU.subtract),
              reads=[b_wtmp, b_w], writes=[b_w])

        def stream_split(src_d, rows, cols, base, b_w, nbands, stg_bufs,
                         extra_writes=()):
            size = rows * cols // 128
            hi_flat = wbf[:, 2 * base: 2 * base + size]
            lo_flat = wbf[:, 2 * base + size: 2 * base + 2 * size]
            per_row = cols
            bands_per_k = per_row // 1024
            ew = list(extra_writes)
            band_bufs = [Buf() for _ in range(nbands)]
            for i in range(nbands):
                k, w = i // bands_per_k, i % bands_per_k
                j = i % 2
                stg, b_stg = stgw_ap[j], stg_bufs[j]
                O("sync", lambda e, k_=k, w_=w, d=stg, s=src_d: e.dma_start(
                    out=d, in_=s[k_ * 128:(k_ + 1) * 128,
                                 w_ * 1024:(w_ + 1) * 1024]),
                  writes=[b_stg], inc=16, sem=f"stg{j}")
                off = k * per_row + w * 1024
                bb = band_bufs[i]
                O("vector", lambda e, d=stg, o=off: e.tensor_copy(
                    hi_flat[:, o:o + 1024], d),
                  reads=[b_stg], writes=[bb] + ew)
                O("vector", lambda e, d=stg, o=off: e.tensor_tensor(
                    lo_flat[:, o:o + 1024], d, hi_flat[:, o:o + 1024],
                    ALU.subtract),
                  reads=[b_stg, bb], writes=[bb] + ew)
            b_w.w = [t for bb in band_bufs for t in bb.w]
            b_w.r = []

        fold_split(wq_d, wbf[:, 0:8192], wbf[:, 8192:16384], b_wq)
        fold_split(wk_d, wbf[:, 16384:24576], wbf[:, 24576:32768], b_wk)
        stream_split(wv_d, D, D, 16384, b_wv, 8, stgw_b, extra_writes=[b_wtmp])
        stream_split(wo_d, D, D, 24576, b_wo, 8, stgw_b)

        _bar = barrier()
        for _b in (b_scrA, b_xn, b_qT):
            _b.w = list(_bar)
        xn2, b_xn2 = scrA, b_scrA

        def ln_stats(src_ap, src_b, so, sb):
            O("vector", lambda e, s=src_ap: e.bn_stats(
                stats[:, so:so + 6], s[:, 0:512]),
              reads=[src_b], writes=[sb])
            O("vector", lambda e, s=src_ap: e.bn_stats(
                stats[:, so + 6:so + 12], s[:, 512:1024]),
              reads=[src_b], writes=[sb])
            O("vector", lambda e: e.bn_aggr(
                stats[:, so + 12:so + 14], stats[:, so:so + 12]),
              reads=[sb], writes=[sb])
            O("vector", lambda e: e.tensor_scalar(
                stats[:, so + 13:so + 14], stats[:, so + 13:so + 14],
                EPS, None, ALU.add),
              reads=[sb], writes=[sb])
            O("scalar", lambda e: e.activation(
                stats[:, so + 14:so + 15], stats[:, so + 13:so + 14], AF.Sqrt),
              reads=[sb], writes=[sb])
            O("vector", lambda e: e.reciprocal(
                stats[:, so + 15:so + 16], stats[:, so + 14:so + 15]),
              reads=[sb], writes=[sb])

        def ln_norm(src_ap, src_b, dst_ap, dst_b, so, sb,
                    gb=None, bb=None, gb_b=None, bb_b=None):
            O("vector", lambda e, s=src_ap, d=dst_ap: e.tensor_scalar(
                d, s, stats[:, so + 12:so + 13], stats[:, so + 15:so + 16],
                ALU.subtract, ALU.mult),
              reads=[src_b, sb], writes=[dst_b])
            if gb is not None:
                O("vector", lambda e, d=dst_ap, g=gb: e.tensor_tensor(
                    d, d, g, ALU.mult), reads=[dst_b, gb_b], writes=[dst_b])
                O("vector", lambda e, d=dst_ap, b=bb: e.tensor_tensor(
                    d, d, b, ALU.add), reads=[dst_b, bb_b], writes=[dst_b])

        def layernorm(src_ap, src_b, dst_ap, dst_b, scr_ap, scr_b,
                      gb=None, bb=None, gb_b=None, bb_b=None):
            ln_stats(src_ap, src_b, 0, b_stats)
            ln_norm(src_ap, src_b, dst_ap, dst_b, 0, b_stats,
                    gb, bb, gb_b, bb_b)

        def transpose8(src_ap, src_b, dstv, dst_b, evac_eng="scalar",
                       pairs=(3, 0)):
            for p in range(4):
                pp = pairs[p % len(pairs)]
                for h in range(2):
                    k = 2 * p + h
                    bank = 2 * pp + h
                    O("tensor", lambda e, k_=k, s=src_ap, t=psum[bank]:
                      e.transpose(t[:, 0:128],
                                  s[:, k_ * 128:(k_ + 1) * 128], ident[:, :]),
                      reads=[src_b, b_ident], writes=[pb[bank]])
                pv = psum_pair[pp][:, :, 0:128]
                if evac_eng == "scalar":
                    O("scalar", lambda e, p_=p, pv=pv, d=dstv: e.copy(
                        d[:, 2 * p_:2 * p_ + 2, :], pv),
                      reads=[pb[2 * pp], pb[2 * pp + 1]], writes=[dst_b])
                else:
                    O("vector", lambda e, p_=p, pv=pv, d=dstv: e.tensor_copy(
                        d[:, 2 * p_:2 * p_ + 2, :], pv),
                      reads=[pb[2 * pp], pb[2 * pp + 1]], writes=[dst_b])

        def transpose8_pair(src_ap, src_b, hiv, lov, dst_b, pairs=(3, 0)):
            for p in range(4):
                pp = pairs[p % len(pairs)]
                for h in range(2):
                    k = 2 * p + h
                    bank = 2 * pp + h
                    O("tensor", lambda e, k_=k, s=src_ap, t=psum[bank]:
                      e.transpose(t[:, 0:128],
                                  s[:, k_ * 128:(k_ + 1) * 128], ident[:, :]),
                      reads=[src_b, b_ident], writes=[pb[bank]])
                pv = psum_pair[pp][:, :, 0:128]
                O("scalar", lambda e, p_=p, pv=pv, d=hiv: e.copy(
                    d[:, 2 * p_:2 * p_ + 2, :], pv),
                  reads=[pb[2 * pp], pb[2 * pp + 1]], writes=[dst_b])
                O("vector", lambda e, p_=p, pv=pv, h=hiv, l=lov:
                  e.tensor_tensor(l[:, 2 * p_:2 * p_ + 2, :], pv,
                                  h[:, 2 * p_:2 * p_ + 2, :], ALU.subtract),
                  reads=[pb[2 * pp], pb[2 * pp + 1], dst_b], writes=[dst_b])

        class GroupEmitter:

            def __init__(self, bank_i, out_view, total):
                self.bank = bank_i
                self.ov = out_view
                self.total = total
                self.i = 0

            def emit(self, mms):
                for (lt, rt, rbufs) in mms:
                    first = self.i == 0
                    last = self.i == self.total - 1
                    if first:
                        sch.op("tensor",
                               lambda e, lt=lt, rt=rt, ov=self.ov, l=last:
                               e.matmul(ov, lt, rt, start=True, stop=l),
                               reads=list(rbufs), writes=[pb[self.bank]])
                    else:
                        sch.op("tensor",
                               lambda e, lt=lt, rt=rt, ov=self.ov, l=last:
                               e.matmul(ov, lt, rt, start=False, stop=l),
                               reads=list(rbufs))
                    self.i += 1
                if self.i == self.total:
                    pb[self.bank].w = [("tensor", sch.counts["tensor"])]
                    pb[self.bank].r = []

        def mm_group(bank_i, out_view, mms):
            n = len(mms)
            for i, (lt, rt, rbufs) in enumerate(mms):
                if i == 0:
                    sch.op("tensor",
                           lambda e, lt=lt, rt=rt, ov=out_view, last=(n == 1):
                           e.matmul(ov, lt, rt, start=True, stop=last),
                           reads=list(rbufs), writes=[pb[bank_i]])
                else:
                    sch.op("tensor",
                           lambda e, lt=lt, rt=rt, ov=out_view, last=(i == n - 1):
                           e.matmul(ov, lt, rt, start=False, stop=last),
                           reads=list(rbufs))
            pb[bank_i].w = [("tensor", sch.counts["tensor"])]
            pb[bank_i].r = []

        def dma_fill(eng, key, buf, emits, extra_reads=()):
            for i, em in enumerate(emits):
                if i == 0:
                    sch.op(eng, em, reads=list(extra_reads), writes=[buf],
                           inc=16, sem=key)
                else:
                    sch.op(eng, em, reads=list(extra_reads), inc=16, sem=key)
            buf.w = [(key, sch.counts[key])]
            buf.r = []

        def mm3(xhi, xlo, b_x, whi, wlo, b_w, nwin):
            mms = []
            for k in range(KD):
                mms.append((xhi[:, k, :], whi[:, k, nwin], [b_x, b_w]))
                mms.append((xhi[:, k, :], wlo[:, k, nwin], [b_x, b_w]))
            for k in range(KD):
                mms.append((xlo[:, k, :], whi[:, k, nwin], [b_x, b_w]))
            return mms

        def elu_pair(b0, b1, dst, dstb):
            halves = [Buf(dstb.w + dstb.r), Buf(dstb.w + dstb.r)]
            for i, bk in ((0, b0), (1, b1)):
                O("vector", lambda e, i_=i, bk_=bk: e.tensor_scalar(
                    scrE[:, i_ * 512:(i_ + 1) * 512],
                    psum[bk_][:, 0:512], 0.0, None, ALU.min),
                  reads=[pb[bk]], writes=[b_scrE[i]])
            toks = []
            for i, bk in ((0, b0), (1, b1)):
                d = dst[:, i * 512:(i + 1) * 512]
                O("scalar", lambda e, i_=i, d=d: e.activation(
                    d, scrE[:, i_ * 512:(i_ + 1) * 512], AF.Exp),
                  reads=[b_scrE[i]], writes=[halves[i]])
                tk = O("vector", lambda e, d=d, bk_=bk: e.scalar_tensor_tensor(
                    d, psum[bk_][:, 0:512], 0.0, d, ALU.max, ALU.add),
                  reads=[pb[bk], halves[i]], writes=[halves[i]])
                toks.append(tk)
            dstb.w = list(toks)
            dstb.r = []

        def k_stage(t):
            for n in range(2):
                bank = 2 + n
                nwin = slice(n * 512, (n + 1) * 512)
                mm_group(bank, psum[bank][:, :],
                         mm3(hT_hi, hT_lo, b_hT, wk_hi, wk_lo, b_wk, nwin))

        def v_stage(t):
            for n in range(2):
                bank = 4 + n
                nwin = slice(n * 512, (n + 1) * 512)
                mm_group(bank, psum[bank][:, :],
                         mm3(hT_hi, hT_lo, b_hT, wv_hi, wv_lo, b_wv, nwin))

        def v_evac(t):
            for n in range(2):
                O("scalar", lambda e, n_=n: e.copy(
                    vsall[:, n_ * 8:(n_ + 1) * 8, 0:DH],
                    psum[4 + n_][:, :].rearrange("p (h n) -> p h n", h=8)),
                  reads=[pb[4 + n]], writes=[b_vsall])

        def h2t_stage(t):
            transpose8_pair(xn2, b_xn2, h2T_hi, h2T_lo, b_h2T,
                            pairs=(3, 0, 1))
            g, jj = t // 4, t % 4
            dma_fill("gpsimd", "sh2t", b_sh2t, [
                (lambda e, g_=g, j_=jj, d=h2T_hi: e.dma_start(
                    out=s_h2t_hi[g_, :, :, j_ * 128:(j_ + 1) * 128],
                    in_=d[:, :, :])),
                (lambda e, g_=g, j_=jj, d=h2T_lo: e.dma_start(
                    out=s_h2t_lo[g_, :, :, j_ * 128:(j_ + 1) * 128],
                    in_=d[:, :, :]))],
                extra_reads=[b_h2T])

        def q_mms(t):
            out = []
            for n in range(2):
                nwin = slice(n * 512, (n + 1) * 512)
                out.append((n, mm3(hT_hi, hT_lo, b_hT, wq_hi, wq_lo, b_wq, nwin)))
            return out

        def attn_stage(t, interleave_q):
            c = t % TILES_PER_SEQ
            if c == 0:
                O("vector", lambda e: e.memset(kvb[:, :, :], 0.0), writes=b_kvh)
            mask3 = mask.ap().rearrange("p (o n) -> p o n", o=1).broadcast_to(
                [128, 2, 128])
            smbv = smb.rearrange("p (b n) -> p b n", b=2)
            den01 = den[:, 0:2].rearrange("p (b o) -> p b o", b=2)
            qgroups = [GroupEmitter(n, psum[n][:, :], 24) for n in range(2)]
            qchunks = None
            if interleave_q:
                qm = q_mms(t + 1)
                flat = [(n, mm) for n, mms in qm for mm in mms]
                qchunks = [flat[i * 6:(i + 1) * 6] for i in range(8)]
            for hpair in range(H // 2):
                h0 = 2 * hpair
                h1 = h0 + 1
                blkh = hpair
                for hp, h in ((0, h0), (1, h1)):
                    p0 = hp * DH
                    O("tensor", lambda e, p0_=p0, blk=blkh, sb_=2 + hp: e.matmul(
                        psum[sb_][:, 0:128],
                        kTv[p0_:p0_ + DH, blk, :], qTv[p0_:p0_ + DH, blk, :],
                        start=True, stop=True),
                      reads=[b_kT, b_qT], writes=[pb[2 + hp]])
                O("vector", lambda e, sm=smbv: e.tensor_tensor(
                    sm, psum_pair[1][:, :, 0:128], mask3, ALU.mult),
                  reads=[pb[2], pb[3], b_mask], writes=[b_smb])
                if qchunks is not None:
                    for n, mm in qchunks[hpair][:3]:
                        qgroups[n].emit([mm])
                for hp, h in ((0, h0), (1, h1)):
                    p0 = hp * DH
                    nbank = 4 + hp
                    mms = []
                    if c > 0:
                        mms.append((qTv[p0:p0 + DH, blkh, :],
                                    kvb[p0:p0 + DH, blkh, :], [b_qT, b_kvh[h]]))
                    mms.append((smb[:, hp * 128:(hp + 1) * 128],
                                vsall[:, h, :], [b_smb, b_vsall]))
                    mm_group(nbank, psum[nbank][:, 0:DH + 1], mms)
                O("vector", lambda e, d=den01: e.tensor_scalar(
                    d, psum_pair[2][:, :, DH:DH + 1], 1e-6, None, ALU.add),
                  reads=[pb[4], pb[5]], writes=[b_den])
                O("vector", lambda e: e.reciprocal(den[:, 2:4], den[:, 0:2]),
                  reads=[b_den], writes=[b_den])
                O("vector", lambda e, h0_=h0: e.tensor_tensor(
                    attn[:, h0_ * DH:(h0_ + 2) * DH].rearrange(
                        "p (b n) -> p b n", b=2),
                    psum_pair[2][:, :, 0:DH],
                    den[:, 2:4].rearrange("p (b o) -> p b o", b=2)
                    .broadcast_to([128, 2, DH]), ALU.mult),
                  reads=[pb[4], pb[5], b_den], writes=[b_attn])
                for hp, h in ((0, h0), (1, h1)):
                    p0 = hp * DH
                    kbank = 2 + hp
                    O("tensor", lambda e, h_=h, kb=kbank: e.matmul(
                        psum[kb][0:DH, 0:DH + 1], kphi[:, h_ * DH:(h_ + 1) * DH],
                        vsall[:, h_, :], start=True, stop=True),
                      reads=[b_kphi, b_vsall], writes=[pb[kbank]])
                    O("vector", lambda e, p0_=p0, blk=blkh, kb=kbank:
                      e.tensor_tensor(
                        kvb[p0_:p0_ + DH, blk, :], kvb[p0_:p0_ + DH, blk, :],
                        psum[kb][0:DH, 0:DH + 1], ALU.add),
                      reads=[b_kvh[h], pb[kbank]], writes=[b_kvh[h]])
                if qchunks is not None:
                    for n, mm in qchunks[hpair][3:]:
                        qgroups[n].emit([mm])
            if interleave_q:
                assert qgroups[0].i == 24 and qgroups[1].i == 24

        def wo_stage(t, xb, b_xb):
            transpose8_pair(attn, b_attn, aT_hi, aT_lo, b_aT, pairs=(3, 2))
            for n in range(2):
                bank = 4 + n
                nwin = slice(n * 512, (n + 1) * 512)
                mm_group(bank, psum[bank][:, :],
                         mm3(aT_hi, aT_lo, b_aT, wo_hi, wo_lo, b_wo, nwin))
                O("vector", lambda e, b_=bank, n_=n, x_=xb: e.tensor_tensor(
                    attn[:, n_ * 512:(n_ + 1) * 512], psum[b_][:, :],
                    x_[:, n_ * 512:(n_ + 1) * 512], ALU.add),
                  reads=[pb[bank], b_xb], writes=[b_attn])
            O("gpsimd", lambda e, t_=t: e.dma_start(
                out=s_r[t_, :, :], in_=attn[:, :]),
              reads=[b_attn], writes=[b_sr], inc=16, sem="sr")

        def xload(tt):
            jn = tt % 2
            O("sync", lambda e, t_=tt, d=xb_ap[jn]: e.dma_start(
                out=d, in_=x_d[t_ * T_TILE:(t_ + 1) * T_TILE, :]),
              reads=[b_x_d], writes=[xb_b[jn]], inc=16, sem=f"xb{jn}")

        layernorm(xb_ap[0], xb_b[0], xn, b_xn, scrA, b_scrA,
                  g1b, be1b, b_g1b, b_be1b)
        transpose8_pair(xn, b_xn, hT_hi, hT_lo, b_hT)
        for n, mms in q_mms(-1):
            mm_group(n, psum[n][:, :], mms)
        k_stage(0)
        v_stage(0)
        for t in range(N_TILES):
            j = t % 2
            elu_pair(0, 1, qphi, b_qphi)
            elu_pair(2, 3, kphi, b_kphi)
            if t + 1 < N_TILES:
                ln_stats(xb_ap[(t + 1) % 2], xb_b[(t + 1) % 2], 0, b_stats)
            if t > 0:
                ln_stats(attn, b_attn, 16, b_stats2)
            if t + 1 < N_TILES:
                ln_norm(xb_ap[(t + 1) % 2], xb_b[(t + 1) % 2], xn, b_xn,
                        0, b_stats, g1b, be1b, b_g1b, b_be1b)
            if t > 0:
                ln_norm(attn, b_attn, xn2, b_xn2, 16, b_stats2,
                        g2b, be2b, b_g2b, b_be2b)
            transpose8(qphi, b_qphi, qTv, b_qT, pairs=(3, 0, 1))
            transpose8(kphi, b_kphi, kTv, b_kT, pairs=(3, 0, 1))
            if t + 1 < N_TILES:
                transpose8_pair(xn, b_xn, hT_hi, hT_lo, b_hT,
                                pairs=(3, 0, 1))
            if t > 0:
                h2t_stage(t - 1)
            v_evac(t)
            attn_stage(t, interleave_q=(t + 1 < N_TILES))
            if t + 1 < N_TILES:
                k_stage(t + 1)
            wo_stage(t, xb_ap[j], xb_b[j])
            if t + 1 < N_TILES:
                v_stage(t + 1)
            if t + 2 < N_TILES:
                xload(t + 2)
        layernorm(attn, b_attn, xn2, b_xn2, scrA, b_scrA,
                  g2b, be2b, b_g2b, b_be2b)
        h2t_stage(N_TILES - 1)


        w1_hi, w1_lo = wpair(0, 32768, KD)
        b_w1 = Buf(barrier())
        stgC_b = [Buf(barrier()), Buf(barrier())]
        stream_split(w1_d, D, FF, 0, b_w1, 32, stgC_b)

        avC = AV()
        h2g_hi = [None, None]; h2g_lo = [None, None]; h2g_b = [None, None]
        h2g_hi[0], h2g_lo[0], h2g_b[0] = avC.take_pair("h2g0", KD * GROUP, KD, GROUP)
        h2g_hi[1], h2g_lo[1], h2g_b[1] = avC.take_pair("h2g1", KD * GROUP, KD, GROUP)
        stage_ap = []; stage_b = []
        for i in range(4):
            a_, b_ = avC.take(f"stC{i}", 512)
            stage_ap.append(a_); stage_b.append(b_)
        stbf_hi = []; stbf_lo = []; stbf_b = []
        for i in range(4):
            h_, l_, b_ = avC.take_pair(f"stCb{i}", 512, None, None)
            stbf_hi.append(h_); stbf_lo.append(l_); stbf_b.append(b_)

        for g in range(N_GROUPS):
            j = g % 2
            dma_fill("sync", f"h2g{j}", h2g_b[j], [
                (lambda e, g_=g, d=h2g_hi[j]: e.dma_start(
                    out=d, in_=s_h2t_hi[g_, :, :, :])),
                (lambda e, g_=g, d=h2g_lo[j]: e.dma_start(
                    out=d, in_=s_h2t_lo[g_, :, :, :]))],
                extra_reads=[b_sh2t])
            for m in range(KF):
                bank = m % 6
                nwin = slice(m * 128, (m + 1) * 128)
                mms = []
                for k in range(KD):
                    mms.append((w1_hi[:, k, nwin], h2g_hi[j][:, k, :],
                                [b_w1, h2g_b[j]]))
                    mms.append((w1_lo[:, k, nwin], h2g_hi[j][:, k, :],
                                [b_w1, h2g_b[j]]))
                for k in range(KD):
                    mms.append((w1_hi[:, k, nwin], h2g_lo[j][:, k, :],
                                [b_w1, h2g_b[j]]))
                mm_group(bank, psum[bank][:, :], mms)
                si = m % 4
                O("scalar", lambda e, b_=bank, m_=m, s_=si: e.activation(
                    stage_ap[s_], psum[b_][:, :], AF.Gelu,
                    bias=b1f[:, m_:m_ + 1], scale=1.0),
                  reads=[pb[bank], b_b1f], writes=[stage_b[si]])
                O("vector", lambda e, s_=si: e.tensor_copy(
                    stbf_hi[s_], stage_ap[s_]),
                  reads=[stage_b[si]], writes=[stbf_b[si]])
                O("vector", lambda e, s_=si: e.tensor_tensor(
                    stbf_lo[s_], stage_ap[s_], stbf_hi[s_], ALU.subtract),
                  reads=[stage_b[si], stbf_b[si]], writes=[stbf_b[si]])
                O("gpsimd", lambda e, g_=g, m_=m, s_=si: e.dma_start(
                    out=s_hidt_hi[m_, :, g_ * GROUP:(g_ + 1) * GROUP],
                    in_=stbf_hi[s_]),
                  reads=[stbf_b[si]], writes=[b_shidt], inc=16, sem=f"st{si}")
                O("gpsimd", lambda e, g_=g, m_=m, s_=si: e.dma_start(
                    out=s_hidt_lo[m_, :, g_ * GROUP:(g_ + 1) * GROUP],
                    in_=stbf_lo[s_]),
                  reads=[stbf_b[si]], writes=[b_shidt], inc=16, sem=f"st{si}")

        w2_hi, w2_lo = wpair(0, 32768, KF)
        b_w2 = Buf(barrier())
        stgD_b = [Buf(barrier()), Buf(barrier())]
        stream_split(w2_d, FF, D, 0, b_w2, 32, stgD_b)

        avD = AV()
        hidt_hi = [None, None]; hidt_lo = [None, None]; hidt_b = [None, None]
        hidt_hi[0], hidt_lo[0], hidt_b[0] = avD.take_pair("hidt0", KF * 128, KF, 128)
        hidt_hi[1], hidt_lo[1], hidt_b[1] = avD.take_pair("hidt1", KF * 128, KF, 128)
        xbD, b_xbD = avD.take("xbD", D)
        rbD, b_rbD = avD.take("rbD", D)
        s2D, b_s2D = avD.take("s2D", D)
        x1D, b_x1D = avD.take("x1D", D)
        scrD, b_scrD = avD.take("scrD", D)
        b2b, b_b2b = avD.take("b2b", D)
        x1TD, b_x1TD = avD.take("x1TD", D)
        x1TDv = x1TD.rearrange("p (k n) -> p k n", k=KD)
        O("sync", lambda e: e.dma_start(
            out=b2b, in_=b2_d[None, :].broadcast_to([128, D])),
          writes=[b_b2b], inc=16, sem="b2b")

        def hidt_load(t):
            j = t % 2
            emits = []
            for (plane_d, dstv) in ((s_hidt_hi, hidt_hi[j]),
                                    (s_hidt_lo, hidt_lo[j])):
                for q4 in range(4):
                    emits.append(lambda e, t_=t, q_=q4, d=dstv, s=plane_d:
                                 e.dma_start(
                        out=d[:, q_ * 8:(q_ + 1) * 8, :],
                        in_=s[q_ * 8:(q_ + 1) * 8, :, t_ * 128:(t_ + 1) * 128]
                        .rearrange("m p n -> p m n")))
            dma_fill("sync", f"hidt{j}", hidt_b[j], emits,
                     extra_reads=[b_shidt])

        def scr_load(t):
            O("sync", lambda e, t_=t: e.dma_start(
                out=xbD, in_=x_d[t_ * T_TILE:(t_ + 1) * T_TILE, :]),
              reads=[b_x_d], writes=[b_xbD], inc=16, sem="xbD")
            O("sync", lambda e, t_=t: e.dma_start(out=rbD, in_=s_r[t_, :, :]),
              reads=[b_sr], writes=[b_rbD], inc=16, sem="rbD")
            O("vector", lambda e: e.tensor_tensor(scrD, rbD, xbD, ALU.add),
              reads=[b_rbD, b_xbD], writes=[b_scrD])
            O("vector", lambda e: e.tensor_tensor(scrD, scrD, b2b, ALU.add),
              reads=[b_scrD, b_b2b], writes=[b_scrD])

        def phd_groups(t):
            j = t % 2
            bank0 = (2 * t) % 6
            for n in range(2):
                bank = (bank0 + n) % 6
                nwin = slice(n * 512, (n + 1) * 512)
                mms = []
                for k in range(KF):
                    mms.append((hidt_hi[j][:, k, :], w2_hi[:, k, nwin],
                                [hidt_b[j], b_w2]))
                    mms.append((hidt_hi[j][:, k, :], w2_lo[:, k, nwin],
                                [hidt_b[j], b_w2]))
                for k in range(KF):
                    mms.append((hidt_lo[j][:, k, :], w2_hi[:, k, nwin],
                                [hidt_b[j], b_w2]))
                mm_group(bank, psum[bank][:, :], mms)

        def phd_evac(t):
            bank0 = (2 * t) % 6
            for n in range(2):
                bank = (bank0 + n) % 6
                sl = slice(n * 512, (n + 1) * 512)
                O("vector", lambda e, b_=bank, s_=sl: e.tensor_tensor(
                    s2D[:, s_], psum[b_][:, :], scrD[:, s_], ALU.add),
                  reads=[pb[bank], b_scrD], writes=[b_s2D])
            if t + 1 < N_TILES:
                scr_load(t + 1)
            layernorm(s2D, b_s2D, x1D, b_x1D, scrD, b_scrD)
            O("vector", lambda e: e.tensor_reduce(
                stats[:, 7:8], x1D, axis=AX, op=ALU.max,
                apply_absolute_value=True),
              reads=[b_x1D], writes=[b_stats])
            O("vector", lambda e: e.tensor_tensor(
                amax1[:, :], amax1[:, :], stats[:, 7:8], ALU.max),
              reads=[b_amax1, b_stats], writes=[b_amax1])
            O("gpsimd", lambda e, t_=t: e.dma_start(out=s_x1[t_, :, :], in_=x1D),
              reads=[b_x1D], writes=[b_sx1], inc=16, sem="sx1")
            transpose8(x1D, b_x1D, x1TDv, b_x1TD,
                       pairs=(3, (t + 2) % 3, t % 3))
            gD, jjD = t // 4, t % 4
            O("gpsimd", lambda e, g_=gD, j_=jjD: e.dma_start(
                out=s_x1t[g_, :, :, j_ * 128:(j_ + 1) * 128], in_=x1TDv[:, :, :]),
              reads=[b_x1TD], writes=[b_sx1], inc=16, sem="sx1t")

        hidt_load(0)
        scr_load(0)
        phd_groups(0)
        for t in range(N_TILES):
            if t + 1 < N_TILES:
                hidt_load(t + 1)
                phd_groups(t + 1)
            phd_evac(t)

        def all_reduce_amax(amax_t, b_amax, cci, cco, b_cci, b_cco,
                            sc, b_sc, isc, b_isc):
            O("gpsimd", lambda e: e.tensor_reduce(
                ccrow[0:1, 0:1], amax_t[:, :], axis=AXALL, op=ALU.max),
              reads=[b_amax], writes=[b_ccrow])
            O("gpsimd", lambda e: e.dma_start(out=cci[:, :], in_=ccrow[:, :]),
              reads=[b_ccrow], writes=[b_cci], inc=16, sem="ccio")
            O("gpsimd", lambda e: e.collective_compute(
                "AllReduce", ALU.max, replica_groups=[list(range(N_CORES))],
                ins=[cci], outs=[cco]),
              reads=[b_cci], writes=[b_cco], sem="cc")
            O("gpsimd", lambda e: e.dma_start(
                out=gscal[:, :], in_=cco[0:1, 0:1].broadcast_to([128, 1])),
              reads=[b_cco], writes=[b_gscal], inc=16, sem="gsc")
            O("vector", lambda e: e.tensor_scalar(
                gscal[:, :], gscal[:, :], 1e-8, None, ALU.max),
              reads=[b_gscal], writes=[b_gscal])
            O("vector", lambda e, s=sc: e.reciprocal(s[:, :], gscal[:, :]),
              reads=[b_gscal], writes=[b_sc])
            O("vector", lambda e, s=sc: e.tensor_scalar(
                s[:, :], s[:, :], 32767.0, None, ALU.mult),
              reads=[b_sc], writes=[b_sc])
            O("vector", lambda e, s=sc, i_=isc: e.reciprocal(i_[:, :], s[:, :]),
              reads=[b_sc], writes=[b_isc])

        all_reduce_amax(amax1, b_amax1, cc1_in, cc1_out, b_cc1i, b_cc1o,
                        sc1, b_sc1, isc1, b_isc1)

        w_in_s = wfp.rearrange("p (k n) -> p k n", k=KD)
        b_winf32 = Buf(barrier())
        dma_fill("sync", "warena", b_winf32, [
            (lambda e: e.dma_start(
                out=w_in_s[:, 0:4, :],
                in_=win_d[0:512, :].rearrange("(k p) n -> p k n", p=128))),
            (lambda e: e.dma_start(
                out=w_in_s[:, 4:8, :],
                in_=win_d[512:1024, :].rearrange("(k p) n -> p k n", p=128)))])

        avE = AV()
        NW = 3
        wtm_ap = []; wtm_b = []
        wib_ap = []; wib_b = []
        for i_ in range(NW):
            a_, b_ = avE.take(f"wtm{i_}", D)
            wtm_ap.append(a_); wtm_b.append(b_)
        for i_ in range(NW):
            a_, b_ = avE.take(f"wib{i_}", 512)
            wib_ap.append(a_); wib_b.append(b_)

        def wq_tps_in(m):
            wtm = wtm_ap[m % NW]
            wtmv = wtm.rearrange("p (k n) -> p k n", k=KD)
            for p in range(4):
                pp = (3, 0)[p % 2]
                for h in range(2):
                    k = 2 * p + h
                    bank = 2 * pp + h
                    O("tensor", lambda e, k_=k, m_=m, t=psum[bank]: e.transpose(
                        t[:, 0:128], w_in_s[:, k_, m_ * 128:(m_ + 1) * 128],
                        ident[:, :]),
                      reads=[b_winf32, b_ident], writes=[pb[bank]])
                O("scalar", lambda e, p_=p, pv=psum_pair[pp][:, :, 0:128],
                  d=wtmv: e.copy(d[:, 2 * p_:2 * p_ + 2, :], pv),
                  reads=[pb[2 * pp], pb[2 * pp + 1]], writes=[wtm_b[m % NW]])

        def wq_chain_store(m):
            wtm, b_wtm = wtm_ap[m % NW], wtm_b[m % NW]
            wib = wib_ap[m % NW].bitcast(BF16)
            O("vector", lambda e, d=wtm: e.tensor_reduce(
                wamax[:, :], d, axis=AX, op=ALU.max, apply_absolute_value=True),
              reads=[b_wtm], writes=[b_wamax])
            O("vector", lambda e: e.tensor_scalar(
                wamax[:, :], wamax[:, :], 1e-8, None, ALU.max),
              reads=[b_wamax], writes=[b_wamax])
            O("vector", lambda e: e.reciprocal(wsc[:, :], wamax[:, :]),
              reads=[b_wamax], writes=[b_wsc])
            O("vector", lambda e: e.tensor_scalar(
                wsc[:, :], wsc[:, :], 127.0, None, ALU.mult),
              reads=[b_wsc], writes=[b_wsc])
            O("vector", lambda e: e.reciprocal(wisc[:, :], wsc[:, :]),
              reads=[b_wsc], writes=[b_wisc])
            O("vector", lambda e, m_=m: e.tensor_copy(
                iscWi[:, m_:m_ + 1], wisc[:, :]),
              reads=[b_wisc], writes=[b_iscWi])
            O("vector", lambda e, d=wtm: e.tensor_scalar(
                d, d, wsc[:, :], MAGIC, ALU.mult, ALU.add),
              reads=[b_wtm, b_wsc], writes=[b_wtm])
            O("vector", lambda e, d=wtm: e.tensor_scalar(
                d, d, MAGIC, 127.0, ALU.subtract, ALU.min),
              reads=[b_wtm], writes=[b_wtm])
            O("vector", lambda e, d=wtm: e.tensor_scalar(
                d, d, -128.0, None, ALU.max),
              reads=[b_wtm], writes=[b_wtm])
            for k in range(KD):
                pass
            wibv = wib.rearrange("p (k n) -> p k n", k=KD)
            for p in range(4):
                pp = (2, 1)[p % 2]
                for h in range(2):
                    k = 2 * p + h
                    bank = 2 * pp + h
                    O("tensor", lambda e, k_=k, t=psum[bank], d=wtm:
                      e.transpose(t[:, 0:128],
                                  d[:, k_ * 128:(k_ + 1) * 128], ident[:, :]),
                      reads=[b_wtm, b_ident], writes=[pb[bank]])
                O("vector", lambda e, p_=p, pv=psum_pair[pp][:, :, 0:128],
                  d=wibv: e.tensor_copy(d[:, 2 * p_:2 * p_ + 2, :], pv),
                  reads=[pb[2 * pp], pb[2 * pp + 1]], writes=[wib_b[m % NW]])
            O("gpsimd", lambda e, m_=m, d=wib: e.dma_start(
                out=s_wint.rearrange("k p n -> p k n")[
                    :, :, m_ * 128:(m_ + 1) * 128],
                in_=d.rearrange("p (k n) -> p k n", k=KD)),
              reads=[wib_b[m % NW]], writes=[b_swint[m % NW]],
              inc=16, sem=f"swint{m % NW}")

        for m in range(KF + 1):
            if m < KF:
                wq_tps_in(m)
            if m > 0:
                wq_chain_store(m - 1)

        winq = wbf[:, 0:32768].rearrange("p (k n) -> p k n", k=KD)
        b_winq = Buf(barrier())
        O("sync", lambda e: e.dma_start(
            out=winq, in_=s_wint.rearrange("k p n -> p k n")),
          reads=[b_swint[0], b_swint[1], b_swint[2]], writes=[b_winq], inc=16, sem="warena")
        O("vector", lambda e: e.tensor_scalar(
            cmbE[:, :], iscWi[:, :], isc1[:, 0:1], None, ALU.mult),
          reads=[b_iscWi, b_isc1], writes=[b_cmbE])

        avE2 = AV()
        _x1t_ap, _x1t_b = avE2.take("x1T", KD * GROUP)
        x1T_ap = [_x1t_ap, _x1t_ap]; x1T_b = [_x1t_b, _x1t_b]
        xsT_hi = [None, None]; xsT_lo = [None, None]; xsT_b = [None, None]
        xsT_hi[0], xsT_lo[0], xsT_b[0] = avE2.take_pair(
            "xsT0", KD * GROUP, KD, GROUP)
        xsT_hi[1], xsT_lo[1], xsT_b[1] = avE2.take_pair(
            "xsT1", KD * GROUP, KD, GROUP)
        stE_ap = []; stE_b = []
        for i in range(4):
            a_, b_ = avE2.take(f"stE{i}", 512)
            stE_ap.append(a_); stE_b.append(b_)
        stgE_ap = [None, None]; stgE_b = [None, None]
        stgE_ap[0], stgE_b[0] = avE2.take("stgE0", D)
        stgE_ap[1], stgE_b[1] = avE2.take("stgE1", D)

        def prep_group(g):
            jg = g % 2
            x1t, b_x1t = x1T_ap[jg], x1T_b[jg]
            O("sync", lambda e, g_=g, d=x1t.rearrange("p (k n) -> p k n", k=KD):
              e.dma_start(out=d, in_=s_x1t[g_, :, :, :]),
              reads=[b_sx1], writes=[b_x1t], inc=16, sem=f"x1T{jg}")
            O("vector", lambda e, s=x1t: e.tensor_scalar(
                s, s, sc1[:, :], MAGIC, ALU.mult, ALU.add),
              reads=[b_x1t, b_sc1], writes=[b_x1t])
            O("vector", lambda e, s=x1t: e.tensor_scalar(
                s, s, MAGIC, 32767.0, ALU.subtract, ALU.min),
              reads=[b_x1t], writes=[b_x1t])
            O("vector", lambda e, s=x1t: e.tensor_scalar(
                s, s, -32768.0, None, ALU.max),
              reads=[b_x1t], writes=[b_x1t])
            O("vector", lambda e, s=x1t, h=xsT_hi[jg]: e.tensor_copy(
                h.rearrange("p k n -> p (k n)"), s),
              reads=[b_x1t], writes=[xsT_b[jg]])
            O("vector", lambda e, s=x1t, h=xsT_hi[jg], l=xsT_lo[jg]:
              e.tensor_tensor(
                l.rearrange("p k n -> p (k n)"), s,
                h.rearrange("p k n -> p (k n)"), ALU.subtract),
              reads=[b_x1t, xsT_b[jg]], writes=[xsT_b[jg]])

        woq = wbf[:, 32768:65536].rearrange("p (k n) -> p k n", k=KF)
        b_woq = Buf(barrier())
        stgE2_b = [Buf(barrier()), Buf(barrier())]
        wo_band_bufs = [Buf() for _ in range(KF)]
        b_wamE = [Buf(barrier()), Buf(barrier())]
        b_wscE = [Buf(barrier()), Buf(barrier())]
        b_wiscE = [Buf(barrier()), Buf(barrier())]
        wamE = [wamax, wamax2]
        wscE = [wsc[:, :], wsc2[:, :]]
        wiscE = [wisc[:, :], wisc2[:, :]]

        def wout_band(k):
            j = k % 2
            stg, b_stg = stgE_ap[j], stgE_b[j]
            wam, bam = wamE[j], b_wamE[j]
            ws_, bws = wscE[j], b_wscE[j]
            wi_, bwi = wiscE[j], b_wiscE[j]
            bb = wo_band_bufs[k]
            O("sync", lambda e, k_=k, d=stg: e.dma_start(
                out=d, in_=wout_d[k_ * 128:(k_ + 1) * 128, :]),
              writes=[b_stg], inc=16, sem=f"stgE{j}")
            O("vector", lambda e, d=stg, w=wam: e.tensor_reduce(
                w[:, :], d, axis=AX, op=ALU.max, apply_absolute_value=True),
              reads=[b_stg], writes=[bam])
            O("vector", lambda e, w=wam: e.tensor_scalar(
                w[:, :], w[:, :], 1e-8, None, ALU.max),
              reads=[bam], writes=[bam])
            O("vector", lambda e, w=wam, s=ws_: e.reciprocal(s, w[:, :]),
              reads=[bam], writes=[bws])
            O("vector", lambda e, s=ws_: e.tensor_scalar(
                s, s, 127.0, None, ALU.mult),
              reads=[bws], writes=[bws])
            O("vector", lambda e, s=ws_, i_=wi_: e.reciprocal(i_, s),
              reads=[bws], writes=[bwi])
            O("vector", lambda e, k_=k, i_=wi_: e.tensor_copy(
                iscWo[:, k_:k_ + 1], i_),
              reads=[bwi], writes=[b_iscWo])
            O("vector", lambda e, d=stg, s=ws_: e.tensor_scalar(
                d, d, s, MAGIC, ALU.mult, ALU.add),
              reads=[b_stg, bws], writes=[b_stg])
            O("vector", lambda e, d=stg: e.tensor_scalar(
                d, d, MAGIC, 127.0, ALU.subtract, ALU.min),
              reads=[b_stg], writes=[b_stg])
            O("vector", lambda e, d=stg: e.tensor_scalar(
                d, d, -128.0, None, ALU.max),
              reads=[b_stg], writes=[b_stg])
            O("vector", lambda e, k_=k, d=stg: e.tensor_copy(
                woq[:, k_, :], d),
              reads=[b_stg], writes=[bb])

        prep_group(0)
        for g in range(N_GROUPS):
            jg = g % 2
            for kb in range(4):
                wout_band(g * 4 + kb)
            for m in range(KF):
                bank = m % 4
                nwin = slice(m * 128, (m + 1) * 128)
                mms = []
                for k in range(KD):
                    mms.append((winq[:, k, nwin], xsT_hi[jg][:, k, :],
                                [b_winq, xsT_b[jg]]))
                for k in range(KD):
                    mms.append((winq[:, k, nwin], xsT_lo[jg][:, k, :],
                                [b_winq, xsT_b[jg]]))
                mm_group(bank, psum[bank][:, :], mms)
                si = m % 4
                O("scalar", lambda e, b_=bank, m_=m, s_=si: e.activation(
                    stE_ap[s_], psum[b_][:, :], AF.Gelu,
                    bias=binf[:, m_:m_ + 1], scale=cmbE[:, m_:m_ + 1]),
                  reads=[pb[bank], b_binf, b_cmbE], writes=[stE_b[si]])
                O("vector", lambda e, s_=si: e.tensor_reduce(
                    stats[:, 7:8], stE_ap[s_], axis=AX, op=ALU.max,
                    apply_absolute_value=True),
                  reads=[stE_b[si]], writes=[b_stats])
                O("vector", lambda e: e.tensor_tensor(
                    amax2[:, :], amax2[:, :], stats[:, 7:8], ALU.max),
                  reads=[b_amax2, b_stats], writes=[b_amax2])
                O("gpsimd", lambda e, g_=g, m_=m, s_=si: e.dma_start(
                    out=s_h2at[m_, :, g_ * GROUP:(g_ + 1) * GROUP],
                    in_=stE_ap[s_]),
                  reads=[stE_b[si]], writes=[b_sh2at], inc=16, sem=f"st{si}")
                if m == 0 and g + 1 < N_GROUPS:
                    prep_group(g + 1)

        b_woq.w = b_woq.w + [t for bb in wo_band_bufs for t in bb.w]
        b_woq.r = []
        avF = AV()
        RF = []
        for i in range(3):
            offF = avF.off
            a_, b_ = avF.take(f"hF{i}", KF * 128)
            hiF = abf[:, 2 * offF: 2 * offF + KF * 128].rearrange(
                "p (k n) -> p k n", k=KF)
            loF = abf[:, 2 * offF + KF * 128: 2 * offF + 2 * KF * 128].rearrange(
                "p (k n) -> p k n", k=KF)
            RF.append([a_, hiF, loF, b_])

        def loadF(t):
            j = (t + 1) % 3
            src_ap, src_b = RF[j][0], RF[j][3]
            h2a = src_ap.rearrange("p (k n) -> p k n", k=KF)
            emitsF = []
            for q4 in range(4):
                emitsF.append(lambda e, t_=t, q_=q4, d=h2a: e.dma_start(
                    out=d[:, q_ * 8:(q_ + 1) * 8, :],
                    in_=s_h2at[q_ * 8:(q_ + 1) * 8, :, t_ * 128:(t_ + 1) * 128]
                    .rearrange("m p n -> p m n")))
            dma_fill("sync", f"h2a{j}", src_b, emitsF,
                     extra_reads=[b_sh2at])

        loadF(0)
        all_reduce_amax(amax2, b_amax2, cc2_in, cc2_out, b_cc2i, b_cc2o,
                        sc2, b_sc2, isc2, b_isc2)
        O("vector", lambda e: e.tensor_scalar(
            cmbF[:, :], iscWo[:, :], isc2[:, 0:1], None, ALU.mult),
          reads=[b_iscWo, b_isc2], writes=[b_cmbF])

        x1F, b_x1F = avF.take("x1F", D)
        s2F, b_s2F = avF.take("s2F", D)
        outF, b_outF = avF.take("outF", D)
        scrF, b_scrF = avF.take("scrF", D)
        boutb, b_boutb = avF.take("boutb", D)
        O("sync", lambda e: e.dma_start(
            out=boutb, in_=bout_d[None, :].broadcast_to([128, D])),
          writes=[b_boutb], inc=16, sem="boutb")

        cmbFb = cmbF.ap() if hasattr(cmbF, "ap") else cmbF
        cmbF3 = cmbF[:, :, None].broadcast_to([128, KF, 128])

        def quantF(t):
            j = (t + 1) % 3
            src_ap, src_b = RF[j][0], RF[j][3]
            h2a = src_ap.rearrange("p (k n) -> p k n", k=KF)
            O("vector", lambda e, d=src_ap: e.tensor_scalar(
                d, d, sc2[:, :], MAGIC, ALU.mult, ALU.add),
              reads=[src_b, b_sc2], writes=[src_b])
            O("vector", lambda e, d=src_ap: e.tensor_scalar(
                d, d, MAGIC, 32767.0, ALU.subtract, ALU.min),
              reads=[src_b], writes=[src_b])
            O("vector", lambda e, d=src_ap: e.tensor_scalar(
                d, d, -32768.0, None, ALU.max),
              reads=[src_b], writes=[src_b])
            O("vector", lambda e, d=h2a: e.tensor_tensor(
                d, d, cmbF3, ALU.mult),
              reads=[src_b, b_cmbF], writes=[src_b])

        def splitF(t):
            src_ap, src_b = RF[(t + 1) % 3][0], RF[(t + 1) % 3][3]
            _, hiF, loF, dst_b = RF[t % 3]
            O("vector", lambda e, d=src_ap, h=hiF: e.tensor_copy(
                h.rearrange("p k n -> p (k n)"), d),
              reads=[src_b], writes=[dst_b])
            O("vector", lambda e, d=src_ap, h=hiF, l=loF: e.tensor_tensor(
                l.rearrange("p k n -> p (k n)"), d,
                h.rearrange("p k n -> p (k n)"), ALU.subtract),
              reads=[src_b, dst_b], writes=[dst_b])

        quantF(0)
        splitF(0)
        for t in range(N_TILES):
            j = t % 2
            bank0 = (2 * t) % 6
            O("sync", lambda e, t_=t: e.dma_start(out=x1F, in_=s_x1[t_, :, :]),
              reads=[b_sx1], writes=[b_x1F], inc=16, sem="x1F")
            _, mhi, mlo, b_m = RF[t % 3]
            for n in range(2):
                bank = (bank0 + n) % 6
                nwin = slice(n * 512, (n + 1) * 512)
                mms = []
                for k in range(KF):
                    mms.append((mhi[:, k, :], woq[:, k, nwin],
                                [b_m, b_woq]))
                for k in range(KF):
                    mms.append((mlo[:, k, :], woq[:, k, nwin],
                                [b_m, b_woq]))
                mm_group(bank, psum[bank][:, :], mms)
            O("vector", lambda e: e.tensor_tensor(scrF, boutb, x1F, ALU.add),
              reads=[b_boutb, b_x1F], writes=[b_scrF])
            if t + 1 < N_TILES:
                loadF(t + 1)
                quantF(t + 1)
                splitF(t + 1)
            for n in range(2):
                bank = (bank0 + n) % 6
                sl = slice(n * 512, (n + 1) * 512)
                O("vector", lambda e, b_=bank, s_=sl: e.tensor_tensor(
                    s2F[:, s_], psum[b_][:, :], scrF[:, s_], ALU.add),
                  reads=[pb[bank], b_scrF], writes=[b_s2F])
            layernorm(s2F, b_s2F, outF, b_outF, scrF, b_scrF)
            O("gpsimd", lambda e, t_=t: e.dma_start(
                out=out_d[t_ * T_TILE:(t_ + 1) * T_TILE, :], in_=outF),
              reads=[b_outF], writes=[b_out_d], inc=16, sem="out")

        store_keys = [k for k in sch.counts
                      if k not in ENG and k.startswith(("sr", "sh2t", "st", "sx1",
                                                        "out", "swint"))]
        final_totals = [(k, sch.counts[k]) for k in store_keys]

        sems = {k: st.enter_context(nc.semaphore(f"sem_{k}"))
                for k in sch.counts}
        blk = st.enter_context(nc.Block())

        def make_body(ename):
            def body(eng):
                for waits, emit, inc, key in sch.ops[ename]:
                    for k2, v in waits.items():
                        eng.wait_ge(sems[k2], v)
                    inst = emit(eng)
                    inst.then_inc(sems[key], inc)
                if ename == "gpsimd":
                    for k, v in final_totals:
                        eng.wait_ge(sems[k], v)
            return body

        blk.sync(make_body("sync"))
        blk.tensor(make_body("tensor"))
        blk.vector(make_body("vector"))
        blk.scalar(make_body("scalar"))
        blk.gpsimd(make_body("gpsimd"))
    return nc


_MASK = np.triu(np.ones((128, 128), dtype=np.float32))
_IDENT = np.eye(128, dtype=np.float32)


def kernel(**inputs):
    f32 = np.float32

    def c(name):
        return np.ascontiguousarray(np.asarray(inputs[name], dtype=f32))

    x = c("L0_attn_x")
    xs = x.reshape(N_CORES, SEQ_PER_CORE * S, D)
    common = {
        "wq": c("L0_attn_wq"), "wk": c("L0_attn_wk"), "wv": c("L0_attn_wv"),
        "wo": c("L0_attn_wo"), "proj": c("L0_attn_proj"),
        "w1": c("L0_attn_w1"), "b1": c("L0_attn_b1"),
        "w2": c("L0_attn_w2"), "b2": c("L0_attn_b2"),
        "g1": c("L0_attn_g1"), "beta1": c("L0_attn_beta1"),
        "g2": c("L0_attn_g2"), "beta2": c("L0_attn_beta2"),
        "W_in": c("L0_ffn_W_in"), "b_in": c("L0_ffn_b_in"),
        "W_out": c("L0_ffn_W_out"), "b_out": c("L0_ffn_b_out"),
        "mask": _MASK, "ident": _IDENT,
    }
    in_maps = [dict(common, x=np.ascontiguousarray(xs[i])) for i in range(N_CORES)]
    nc = build_program()
    res = run_bass_kernel_spmd(nc, in_maps, core_ids=list(range(N_CORES)))
    out = np.stack([np.asarray(r["out"]) for r in res.results], axis=0)
    return out.reshape(16, S, D).astype(np.float32)
